# version 11
# speedup vs baseline: 1.5293x; 1.0218x over previous
import numpy as np
import ml_dtypes
import concourse.bass as bass
import concourse.mybir as mybir
from concourse.bass_utils import run_bass_kernel_spmd

F32 = mybir.dt.float32
BF16 = mybir.dt.bfloat16
AF = mybir.ActivationFunctionType
ALU = mybir.AluOpType
AX = mybir.AxisListType

T = 2048
D = 1024
NB = 8
HA = 16
DH = 64
C = 128
NC = T // C
HB = 8
WIN = 512
NW = T // WIN
HID = 2730
HIDP = 2816


def host_constants():
    c = {}
    c["I128f"] = np.eye(128, dtype=np.float32)
    c["I128b"] = np.eye(128, dtype=ml_dtypes.bfloat16)
    c["ones128"] = np.ones((128, 128), dtype=ml_dtypes.bfloat16)
    ed = (np.arange(128)[:, None] // 64 == np.arange(128)[None, :] // 64)
    c["EDIAG"] = ed.astype(ml_dtypes.bfloat16)
    HSLOT = [2 * (s % 8) + (s // 8) for s in range(16)]
    sel = np.zeros((16, 1024), dtype=np.float32)
    for s in range(16):
        h = HSLOT[s]
        sel[s, 64 * h:64 * h + 64] = 1.0
    c["SEL16"] = sel.astype(ml_dtypes.bfloat16)
    selq = np.zeros((16, 1024), dtype=np.float32)
    for s in range(16):
        h = HSLOT[s]
        pair, par = h // 2, h % 2
        selq[s, 128 * pair + 64 * par:128 * pair + 64 * par + 64] = 1.0
    c["SELQ"] = selq.astype(ml_dtypes.bfloat16)
    return c


def host_constants_b():
    c = {}
    half = np.arange(0, DH, 2, dtype=np.float64)
    inv_freq = 1.0 / 10000.0 ** (half / DH)
    t = np.arange(T, dtype=np.float64)
    freqs = t[:, None] * inv_freq[None, :]
    freqs = np.concatenate([freqs, freqs], axis=-1)
    base = (half + 0.4 * DH) / (1.4 * DH)
    power = (t - T // 2) / float(WIN // 2)
    scale = base[None, :] ** power[:, None]
    scale = np.concatenate([scale, scale], axis=-1)
    cos, sin = np.cos(freqs), np.sin(freqs)
    sgn = np.ones(DH); sgn[:DH // 2] = -1.0
    QK = 8.0
    def tm(a):
        return np.ascontiguousarray(
            a.reshape(NC, 128, DH).transpose(1, 0, 2)).astype(ml_dtypes.bfloat16)
    c["CSQ8"] = tm(cos * scale * QK)
    c["SNQ8"] = tm(sin * scale * QK * sgn[None, :])
    c["CSK1"] = tm(cos / scale)
    c["SNK1"] = tm(sin / scale * sgn[None, :])
    i = np.arange(128)[:, None]; j = np.arange(128)[None, :]
    m = np.zeros((128, 2, 128), dtype=np.float32)
    m[:, 0, :] = np.where(j < i, -1e30, 0.0)
    m[:, 1, :] = np.where(j > i, -1e30, 0.0)
    c["P1MASK"] = m
    m2 = np.zeros((128, 2, 128), dtype=np.float32)
    m2[:, 0, :] = np.where(i < j, -1e30, 0.0)
    m2[:, 1, :] = np.where(i > j, -1e30, 0.0)
    c["P2ADD"] = m2
    return c


class Ctx:
    def __init__(self, nc):
        self.nc = nc
        self._sems = {}
        self.cnt = {}
        self._open = {}

    def sem(self, name):
        g = self.nc.semaphore(name)
        h = g.__enter__()
        self._open["sem_" + name] = g
        self._sems[name] = h
        self.cnt[name] = 0
        return h

    def s(self, name):
        return self._sems[name]

    def inc(self, ins, name, n=1):
        self.cnt[name] += n
        ins.then_inc(self._sems[name], n)
        return self.cnt[name]

    def alloc(self, name, shape, dtype, psum=False, side="left"):
        if psum:
            g = self.nc.psum_tensor(name, list(shape), dtype)
        else:
            g = self.nc.sbuf_tensor(name, list(shape), dtype, side=side)
        h = g.__enter__()
        self._open[name] = g
        return h

    def free(self, *names):
        for n in names:
            self._open.pop(n).__exit__(None, None, None)


def build_program(dbg_names=(), race_check=False, stop_after=None, n_chunks=NC):
    nc = bass.Bass("TRN2", target_bir_lowering=False,
                   detect_race_conditions=race_check)
    cstA = host_constants()
    cstB = host_constants_b()

    def din(name, shape, dtype=F32):
        return nc.dram_tensor(name, list(shape), dtype, kind="ExternalInput").ap()

    xT = din("xT", [128, NB, T])
    A_Wq = din("A_Wq", [128, NB, D]); A_Wk = din("A_Wk", [128, NB, D])
    A_Wv = din("A_Wv", [128, NB, D]); A_Wg = din("A_Wg", [128, NB, D])
    A_Wa = din("A_Wa", [128, NB, HA]); A_ba = din("A_ba", [HA, 1])
    A_Wo = din("A_Wo", [128, NB, D])
    A_gnw = din("A_gnw", [128, NB]); A_gnb = din("A_gnb", [128, NB])
    B_wqkv = [din(f"B_wqkv{l}", [128, NB, 1536]) for l in range(2)]
    B_qs = [din(f"B_qs{l}", [128, DH]) for l in range(2)]
    B_ks = [din(f"B_ks{l}", [128, DH]) for l in range(2)]
    B_wo = [din(f"B_wo{l}", [128, 4, D]) for l in range(2)]
    B_lng = [din(f"B_lng{l}", [128, NB]) for l in range(2)]
    B_lnb = [din(f"B_lnb{l}", [128, NB]) for l in range(2)]
    B_w1 = [din(f"B_w1{l}", [128, NB, 2 * HIDP]) for l in range(2)]
    B_b1a = [din(f"B_b1a{l}", [128, 22]) for l in range(2)]
    B_b1g = [din(f"B_b1g{l}", [128, 22]) for l in range(2)]
    B_w2 = [din(f"B_w2{l}", [128, 22, D]) for l in range(2)]
    B_b2 = [din(f"B_b2{l}", [128, NB]) for l in range(2)]
    CONST = {}
    for k, v in {**cstA, **cstB}.items():
        CONST[k] = din("c_" + k, v.shape, F32 if v.dtype == np.float32 else BF16)

    yT = nc.dram_tensor("yT", [128, NB, T], F32, kind="ExternalOutput").ap()
    dbg_out = {}

    x_spill = nc.dram_tensor("x_spill", [128, NB, T], F32).ap()
    s_spill = nc.dram_tensor("s_spill", [HA, T + 1], F32).ap()

    ctx = Ctx(nc)
    for s in ("T", "V", "S", "DS", "DG", "G"):
        ctx.sem(s)

    CI = {}
    for k in cstA:
        v = cstA[k]
        CI[k] = ctx.alloc("k_" + k, v.shape, F32 if v.dtype == np.float32 else BF16)

    cpool = ctx.alloc("cpool", [128, 4], F32)
    CPOOL_VALS = [1e-5, 1.0, -1.0, 1e-24]
    for i, val in enumerate(CPOOL_VALS):
        nc.const_aps.register(F32, val, cpool[:, i:i + 1])

    x_sb = ctx.alloc("x_sb", [128, NB, T + 1], F32, side="right")
    gnw = ctx.alloc("gnw", [128, NB], F32); gnb = ctx.alloc("gnb", [128, NB], F32)
    mu_b = ctx.alloc("mu_b", [128, 512], F32)
    mu2_b = ctx.alloc("mu2_b", [128, 512], F32)
    var_b = ctx.alloc("var_b", [128, 512], F32)
    rb_b = ctx.alloc("rb_b", [128, 512], F32)
    sd_b = mu2_b
    tsub = ctx.alloc("tsub", [128, 512], F32)

    gamhat = ctx.alloc("gamhat", [128, NB], F32)
    bethat = ctx.alloc("bethat", [128, NB], F32)
    ln_scr = ctx.alloc("ln_scr", [128, NB, 512], BF16)
    xn = ctx.alloc("xn", [128, NB, T], BF16)
    Wa = ctx.alloc("Wa", [128, NB, HA], BF16)
    ba = ctx.alloc("ba", [HA, 1], F32)
    Wq = ctx.alloc("Wq", [128, NB, D], BF16)
    Wk = ctx.alloc("Wk", [128, NB, D], BF16)
    o_gla = ctx.alloc("o_gla", [128, NB, T], BF16)
    opart = ctx.alloc("opart", [128, NB, NC], F32)
    opart2 = ctx.alloc("opart2", [128, NB, NC], F32)
    Wv = ctx.alloc("Wv", [128, NB, D], BF16)

    psA = ctx.alloc("psA", [128, 1024], F32, psum=True)
    psB = ctx.alloc("psB", [128, 1024], F32, psum=True)
    psC = ctx.alloc("psC", [128, 512], F32, psum=True)
    psD = ctx.alloc("psD", [128, 512], F32, psum=True)
    psE = ctx.alloc("psE", [128, 512], F32, psum=True)
    psF = ctx.alloc("psF", [128, 512], F32, psum=True)

    blkg = nc.Block()
    block = blkg.__enter__()
    ctx._open["block"] = blkg

    LASTMM = [None]
    def MM(t, *a, **kw):
        ins = t.matmul(*a, **kw)
        LASTMM[0] = ins
        return ins

    def SYNC(eng, sem, ins):
        m = ctx.inc(ins, sem)
        eng.wait_ge(ctx.s(sem), m)

    T_, V_, S_, G_, Y_ = block.tensor, block.vector, block.scalar, block.gpsimd, block.sync

    def dbg(name, shape, dtype, src_ap, waits):
        if name not in dbg_names:
            return
        dbg_out[name] = nc.dram_tensor("dbg_" + name, list(shape), dtype,
                                       kind="ExternalOutput").ap()
        def _(sy):
            for se, vv in waits:
                sy.wait_ge(ctx.s(se), vv)
            ins = sy.dma_start(out=dbg_out[name], in_=src_ap)
            ctx.inc(ins, "DS", 16)
        Y_(_)


    def finish_early():
        def _(v):
            v.wait_ge(ctx.s("T"), ctx.cnt["T"])
            v.wait_ge(ctx.s("S"), ctx.cnt["S"])
            ins = v.memset(mu_b[:], 0.0)
            ctx.inc(ins, "V")
        V_(_)
        mz = ctx.cnt["V"]
        for tch in range(4):
            def _(sy, tch=tch):
                sy.wait_ge(ctx.s("V"), mz)
                for kb in range(NB):
                    ins = sy.dma_start(out=yT[:, kb, 512 * tch:512 * tch + 512],
                                       in_=mu_b[:])
                    ctx.inc(ins, "DS", 16)
            Y_(_)
        def _(sy):
            sy.wait_ge(ctx.s("DS"), ctx.cnt["DS"])
            sy.wait_ge(ctx.s("DG"), ctx.cnt["DG"])
            if ctx.cnt["G"]:
                sy.wait_ge(ctx.s("G"), ctx.cnt["G"])
            sy.nop()
        Y_(_)
        ctx._open.pop("block").__exit__(None, None, None)
        for k in reversed(list(ctx._open)):
            ctx._open.pop(k).__exit__(None, None, None)
        return nc, dbg_out

    def _(g):
        for k in cstA:
            ctx.inc(g.dma_start(out=CI[k][:], in_=CONST[k]), "DG", 16)
        ctx.inc(g.dma_start(out=Wq[:], in_=A_Wq), "DG", 16)
        ctx.inc(g.dma_start(out=Wk[:], in_=A_Wk), "DG", 16)
        ctx.inc(g.dma_start(out=Wv[:], in_=A_Wv), "DG", 16)
        ctx.inc(g.dma_start(out=Wa[:], in_=A_Wa), "DG", 16)
        ctx.inc(g.dma_start(out=ba[:], in_=A_ba), "DG", 16)
        ctx.inc(g.dma_start(out=gnw[:], in_=A_gnw), "DG", 16)
        ctx.inc(g.dma_start(out=gnb[:], in_=A_gnb), "DG", 16)
    G_(_)
    m_loads = ctx.cnt["DG"]

    def _(sy):
        ins = sy.dma_start(out=x_sb[:, :, 1:T + 1], in_=xT)
        ctx.inc(ins, "DS", 16)
    Y_(_)
    m_xin = ctx.cnt["DS"]

    def _(v):
        v.memset(x_sb[:, :, 0:1], 0.0)
        for i, val in enumerate(CPOOL_VALS):
            ins = v.memset(cpool[:, i:i + 1], val)
        ctx.inc(ins, "V")
    V_(_)
    m_cpool = ctx.cnt["V"]
    def _(s):
        s.wait_ge(ctx.s("V"), m_cpool)
        s.nop()
    S_(_)

    ln_state = {"pslast": 0}

    def emit_ln(src3, out3, waits, shift=False, out_f32=False, pstats=None,
                waits_tch=None, post_tch=None, scratch_side="right",
                scratch=None):
        ln_state["n"] = ln_state.get("n", 0) + 1
        nm = str(ln_state["n"])
        psS1, psS2 = pstats if pstats is not None else (psC[:], psD[:])
        xbfs = scratch if scratch is not None else \
            ctx.alloc("ln_xbfs" + nm, [128, NB, 512], BF16, side=scratch_side)
        m_apply = 0
        marks = []
        for tch in range(4):
            t0 = 512 * tch
            wt = list(waits) + (list(waits_tch[tch]) if waits_tch else [])
            def _(s, tch=tch, t0=t0, wt=wt):
                for se, vv in wt:
                    s.wait_ge(ctx.s(se), vv)
                if ln_state["pslast"]:
                    s.wait_ge(ctx.s("V"), ln_state["pslast"])
                for kb in range(NB):
                    off = 0 if (shift and kb >= 4) else 1
                    src = src3[:, kb, off + t0: off + t0 + 512]
                    ins = s.activation(xbfs[:, kb, :], src, AF.Copy)
                ctx.inc(ins, "S")
            S_(_)
            mcast = ctx.cnt["S"]
            def _(t, tch=tch, wt=wt):
                t.wait_ge(ctx.s("S"), mcast)
                for se, vv in wt:
                    t.wait_ge(ctx.s(se), vv)
                if ln_state["pslast"]:
                    t.wait_ge(ctx.s("V"), ln_state["pslast"])
                for kb in range(NB):
                    MM(t, psS1, CI["ones128"][:], xbfs[:, kb, :],
                             start=(kb == 0), stop=(kb == NB - 1))
                ctx.inc(LASTMM[0], "T")
            T_(_)
            msum = ctx.cnt["T"]
            def _(s, tch=tch, t0=t0):
                s.wait_ge(ctx.s("T"), msum)
                for kb in range(NB):
                    off = 0 if (shift and kb >= 4) else 1
                    src = src3[:, kb, off + t0: off + t0 + 512]
                    ins = s.activation(xbfs[:, kb, :], src, AF.Square)
                ctx.inc(ins, "S")
            S_(_)
            msq = ctx.cnt["S"]
            def _(t, tch=tch):
                t.wait_ge(ctx.s("S"), msq)
                for kb in range(NB):
                    MM(t, psS2, CI["ones128"][:], xbfs[:, kb, :],
                             start=(kb == 0), stop=(kb == NB - 1))
                ctx.inc(LASTMM[0], "T")
            T_(_)
            mstat = ctx.cnt["T"]
            def _(v, tch=tch):
                v.wait_ge(ctx.s("T"), mstat)
                SYNC(v, "V", v.tensor_scalar(mu_b[:], psS1, 1.0 / D, None, ALU.mult))
                SYNC(v, "V", v.tensor_mul(mu2_b[:], mu_b[:], mu_b[:]))
                ins = v.scalar_tensor_tensor(var_b[:], psS2, 1.0 / D, mu2_b[:],
                                             ALU.mult, ALU.subtract)
                ctx.inc(ins, "V")
            V_(_)
            mvar = ctx.cnt["V"]
            def _(s, tch=tch):
                s.wait_ge(ctx.s("V"), mvar)
                s.activation(sd_b[:], var_b[:], AF.Ln, bias=1e-5)
                ins = s.activation(rb_b[:], sd_b[:], AF.Exp, scale=-0.5)
                ctx.inc(ins, "S")
            S_(_)
            msd = ctx.cnt["S"]
            def _(v, tch=tch, t0=t0):
                v.wait_ge(ctx.s("S"), msd)
                tb = [tsub, var_b]
                for kb2 in range(0, NB, 2):
                    for j in range(2):
                        kb = kb2 + j
                        off = 0 if (shift and kb >= 4) else 1
                        v.tensor_sub(tb[j][:], src3[:, kb, off + t0: off + t0 + 512],
                                     mu_b[:])
                    for j in range(2):
                        kb = kb2 + j
                        ins = v.tensor_mul(out3[:, kb, t0:t0 + 512], tb[j][:], rb_b[:])
                ctx.inc(ins, "V")
            V_(_)
            ln_state["pslast"] = ctx.cnt["V"]
            m_apply = ctx.cnt["V"]
            marks.append(m_apply)
            if post_tch is not None:
                post_tch(tch, m_apply)
        if scratch is None:
            ctx.free("ln_xbfs" + nm)
        return m_apply, marks

    m_ln1, ln1_marks = emit_ln(x_sb[:], xn[:],
                               waits=[("DS", m_xin), ("DG", m_loads)],
                               scratch=ln_scr)

    def _(sy):
        sy.wait_ge(ctx.s("V"), m_ln1)
        ins = sy.dma_start(out=x_spill, in_=x_sb[:, :, 1:T + 1])
        ctx.inc(ins, "DS", 16)
    Y_(_)
    m_xspill = ctx.cnt["DS"]
    ctx.free("x_sb")
    dbg("xn", [128, NB, T], BF16, xn[:], [("V", m_ln1)])
    if stop_after == "ln1":
        return finish_early()

    S_sb = ctx.alloc("S_sb", [HA, T + 1], F32, side="right")
    qdF = ctx.alloc("qdF", [HA, T], BF16, side="right")
    kdF = ctx.alloc("kdF", [HA, T], BF16, side="right")
    sp_sb = ctx.alloc("sp_sb", [HA, T], F32, side="right")
    m_sp = 0
    def _(v):
        v.wait_ge(ctx.s("DS"), m_xspill)
        ins = v.memset(S_sb[:, 0:1], 0.0)
        ctx.inc(ins, "V")
    V_(_)
    for tch in range(4):
        def _(t, tch=tch):
            t.wait_ge(ctx.s("V"), ln1_marks[tch])
            if tch == 0:
                t.wait_ge(ctx.s("DG"), m_loads)
            if m_sp:
                t.wait_ge(ctx.s("S"), m_sp)
            for kb in range(NB):
                MM(t, psE[0:HA, :], Wa[:, kb, :],
                         xn[:, kb, 512 * tch:512 * tch + 512],
                         start=(kb == 0), stop=(kb == NB - 1))
            ctx.inc(LASTMM[0], "T")
        T_(_)
        mla = ctx.cnt["T"]
        def _(s, tch=tch, mla=mla):
            s.wait_ge(ctx.s("T"), mla)
            s.wait_ge(ctx.s("DS"), m_xspill)
            SYNC(s, "S", s.activation(sd_b[0:HA, :], psE[0:HA, :], AF.Exp, bias=ba[:]))
            ins = s.activation(sp_sb[:, 512 * tch:512 * tch + 512], sd_b[0:HA, :],
                               AF.Ln, bias=1.0)
            ctx.inc(ins, "S")
        S_(_)
        m_sp = ctx.cnt["S"]

    qkE = ctx.alloc("qkE", [HA, T], F32, side="right")
    def _(v):
        v.wait_ge(ctx.s("V"), ctx.cnt["V"])
        v.wait_ge(ctx.s("S"), m_sp)
        ins = v.tensor_tensor_scan(S_sb[:, 1:T + 1], sp_sb[:], sp_sb[:], 0.0,
                                   ALU.add, ALU.bypass)
        mscan = ctx.inc(ins, "V")
        v.wait_ge(ctx.s("V"), mscan)
        for c in range(NC):
            ins = v.tensor_scalar(qkE[:, C * c:C * c + C],
                                  S_sb[:, 1 + C * c: 1 + C * c + C],
                                  S_sb[:, C * c:C * c + 1], None, ALU.subtract)
        ctx.inc(ins, "V")
    V_(_)
    m_qde = ctx.cnt["V"]
    def _(s):
        s.wait_ge(ctx.s("V"), m_qde)
        ins = s.activation(qdF[:], qkE[:], AF.Exp, scale=-1.0)
        ctx.inc(ins, "S")
    S_(_)
    m_qdf = ctx.cnt["S"]
    def _(v):
        v.wait_ge(ctx.s("S"), m_qdf)
        for c in range(NC):
            ins = v.tensor_scalar(qkE[:, C * c:C * c + C],
                                  S_sb[:, 1 + C * c: 1 + C * c + C],
                                  S_sb[:, C * (c + 1):C * (c + 1) + 1], None,
                                  ALU.subtract)
        ctx.inc(ins, "V")
    V_(_)
    m_kde = ctx.cnt["V"]
    def _(s):
        s.wait_ge(ctx.s("V"), m_kde)
        ins = s.activation(kdF[:], qkE[:], AF.Exp)
        ctx.inc(ins, "S")
    S_(_)
    m_kdf = ctx.cnt["S"]
    def _(sy):
        sy.wait_ge(ctx.s("V"), m_qde)
        ins = sy.dma_start(out=s_spill, in_=S_sb[:])
        ctx.inc(ins, "DS", 16)
    Y_(_)
    m_sdram = ctx.cnt["DS"]
    dbg("S", [HA, T + 1], F32, S_sb[:], [("V", m_qde)])

    ctx.free("qkE", "sp_sb")
    if stop_after == "a2":
        return finish_early()

    stok = ctx.alloc("stok", [128, HA], F32, side="right")
    SRb = ctx.alloc("SRb", [128, HA, C], F32, side="right")
    Dst = ctx.alloc("Dst", [128, HA, C], BF16, side="right")
    vcb = [ctx.alloc(f"vcb{i}", [128, D], BF16, side="right") for i in range(2)]
    kcb = [ctx.alloc(f"kcb{i}", [128, D], BF16, side="right") for i in range(2)]
    k2b = [ctx.alloc(f"k2b{i}", [128, D], BF16, side="right") for i in range(2)]
    qcb = [ctx.alloc(f"qcb{i}", [128, NB, C], BF16, side="right") for i in range(2)]
    q2b = [ctx.alloc(f"q2b{i}", [128, NB, C], BF16, side="right") for i in range(2)]
    ktb = [ctx.alloc(f"ktb{i}", [128, NB, C], BF16, side="right") for i in range(2)]
    a2b = ctx.alloc("a2b", [128, HA, C], BF16, side="right")
    MEV = ctx.alloc("MEV", [128, NB, DH], F32, side="right")
    MOD = ctx.alloc("MOD", [128, NB, DH], F32, side="right")
    MEVb = ctx.alloc("MEVb", [128, NB, DH], BF16, side="right")
    MODb = ctx.alloc("MODb", [128, NB, DH], BF16, side="right")
    gmb = ctx.alloc("gmb", [128, NB], F32, side="right")
    osq = ctx.alloc("osq", [128, D], BF16, side="right")
    if stop_after == "calloc":
        return finish_early()

    def _(v):
        v.memset(MEV[:], 0.0)
        ins = v.memset(MOD[:], 0.0)
        ctx.inc(ins, "V")
    V_(_)
    m_msinit = ctx.cnt["V"]
    def _(s):
        s.wait_ge(ctx.s("V"), m_msinit)
        s.activation(MEVb[:], MEV[:], AF.Copy)
        ins = s.activation(MODb[:], MOD[:], AF.Copy)
        ctx.inc(ins, "S")
    S_(_)
    if stop_after == "cm":
        return finish_early()

    m_G = 0
    m_a2rdy = 0
    m_psB_free = 0
    m_psE_sfree = 0
    m_Mup = 0
    m_Mb = ctx.cnt["S"]
    m_ost = 0
    m_opred = 0
    m_v = 0

    for c in range(n_chunks):
        ib = c % 2
        cc = slice(C * c, C * c + C)

        def _(sy, c=c):
            sy.wait_ge(ctx.s("DS"), m_sdram)
            if c == 0:
                sy.wait_ge(ctx.s("V"), m_kde)
                sy.wait_ge(ctx.s("S"), m_kdf)
            if m_G:
                sy.wait_ge(ctx.s("V"), m_G)
                sy.wait_ge(ctx.s("S"), m_Dexp)
            src = s_spill[:, 1 + C * c: 1 + C * c + C].unsqueeze(0).partition_broadcast(128)
            ins = sy.dma_start(out=SRb[:], in_=src)
            ctx.inc(ins, "DS", 16)
        Y_(_)
        m_sr = ctx.cnt["DS"]
        if stop_after == "csr" and c == n_chunks - 1:
            return finish_early()

        def _(t, c=c, cc=cc):
            if c < 4:
                t.wait_ge(ctx.s("V"), ln1_marks[c // 4])
            if m_a2rdy:
                t.wait_ge(ctx.s("V"), m_a2rdy)
            for n in range(2):
                for kb in range(NB):
                    MM(t, psA[:, 512 * n:512 * n + 512], xn[:, kb, cc],
                             Wv[:, kb, 512 * n:512 * n + 512],
                             start=(kb == 0), stop=(kb == NB - 1))
            ctx.inc(LASTMM[0], "T")
            if stop_after == "cw1a":
                return
            if c == 0:
                t.wait_ge(ctx.s("S"), m_kdf)
            if m_psE_sfree:
                t.wait_ge(ctx.s("S"), m_psE_sfree)
            MM(t, psE[:, 0:HA], S_sb[:, 1 + C * c:1 + C * c + C],
                     CI["I128f"][0:HA, 0:HA], start=True, stop=True)
            ctx.inc(LASTMM[0], "T")
        T_(_)
        m_w1 = ctx.cnt["T"]
        if stop_after == "cw1a" and c == n_chunks - 1:
            return finish_early()

        def _(s, c=c):
            s.wait_ge(ctx.s("T"), m_w1)
            s.activation(vcb[ib][:], psA[:], AF.Copy)
            ins = s.activation(stok[:], psE[:, 0:HA], AF.Copy)
            ctx.inc(ins, "S")
        S_(_)
        m_v = ctx.cnt["S"]
        if stop_after == "cw1" and c == n_chunks - 1:
            return finish_early()

        def _(t, c=c, cc=cc):
            t.wait_ge(ctx.s("S"), m_v)
            for n in range(2):
                for kb in range(NB):
                    MM(t, psA[:, 512 * n:512 * n + 512], xn[:, kb, cc],
                             Wk[:, kb, 512 * n:512 * n + 512],
                             start=(kb == 0), stop=(kb == NB - 1))
            ctx.inc(LASTMM[0], "T")
            if m_ost:
                t.wait_ge(ctx.s("S"), m_ost)
                t.wait_ge(ctx.s("V"), m_opred)
            for n in range(2):
                MM(t, psB[:, 512 * n:512 * n + 512], kdF[:, cc],
                         CI["SEL16"][:, 512 * n:512 * n + 512], start=True, stop=True)
            ctx.inc(LASTMM[0], "T")
        T_(_)
        m_w2 = ctx.cnt["T"]

        def _(s, c=c):
            s.wait_ge(ctx.s("T"), m_w2)
            ins = s.activation(kcb[ib][:], psA[:], AF.Silu)
            ctx.inc(ins, "S")
        S_(_)
        m_ksilu = ctx.cnt["S"]

        def _(v, c=c):
            v.wait_ge(ctx.s("S"), m_ksilu)
            ins = v.tensor_mul(k2b[ib][:], kcb[ib][:], psB[:])
            ctx.inc(ins, "V")
        V_(_)
        m_k2 = ctx.cnt["V"]
        if stop_after == "cw2" and c == n_chunks - 1:
            return finish_early()

        def _(t, c=c, cc=cc):
            t.wait_ge(ctx.s("S"), m_ksilu)
            for i in range(NB):
                for kb in range(NB):
                    MM(t, psA[:, 128 * i:128 * i + 128],
                             Wq[:, kb, 128 * i:128 * i + 128], xn[:, kb, cc],
                             start=(kb == 0), stop=(kb == NB - 1))
            ctx.inc(LASTMM[0], "T")
            t.wait_ge(ctx.s("V"), m_k2)
            for pair in range(8):
                MM(t, psB[:, 128 * pair:128 * pair + 128],
                         CI["SELQ"][:, 128 * pair:128 * pair + 128],
                         qdF[:, cc], start=True, stop=True)
            ctx.inc(LASTMM[0], "T")
        T_(_)
        m_w3 = ctx.cnt["T"]

        def _(s, c=c):
            s.wait_ge(ctx.s("T"), m_w3)
            ins = s.activation(qcb[ib][:], psA[:], AF.Silu)
            ctx.inc(ins, "S")
        S_(_)
        m_qsilu = ctx.cnt["S"]

        def _(v, c=c):
            v.wait_ge(ctx.s("S"), m_qsilu)
            ins = v.tensor_mul(q2b[ib][:], qcb[ib][:], psB[:])
            ctx.inc(ins, "V")
        V_(_)
        m_q2 = ctx.cnt["V"]
        if stop_after == "cw3" and c == n_chunks - 1:
            return finish_early()

        def _(t, c=c):
            t.wait_ge(ctx.s("S"), m_qsilu)
            for pair in range(8):
                MM(t, psA[:, 128 * pair:128 * pair + 128],
                         kcb[ib][:, 128 * pair:128 * pair + 128], CI["I128b"][:],
                         start=True, stop=True)
            ctx.inc(LASTMM[0], "T")
        T_(_)
        m_w4 = ctx.cnt["T"]
        def _(s, c=c):
            s.wait_ge(ctx.s("T"), m_w4)
            ins = s.activation(ktb[ib][:], psA[:], AF.Copy)
            ctx.inc(ins, "S")
        S_(_)
        m_kt = ctx.cnt["S"]
        if stop_after == "cw4" and c == n_chunks - 1:
            return finish_early()

        def _(v, c=c):
            v.wait_ge(ctx.s("DS"), m_sr)
            v.wait_ge(ctx.s("S"), m_v)
            if m_a2rdy:
                v.wait_ge(ctx.s("S"), m_Dexp)
            for s in range(HA):
                sl = SRb[:, s, :]
                ins = v.tensor_scalar(sl, sl, stok[:, s:s + 1],
                                      -1.0, ALU.subtract, ALU.mult)
            SYNC(v, "V", ins)
            ins = v.scalar_tensor_tensor(SRb[:], SRb[:], -1e30, SRb[:],
                                         ALU.mult, ALU.min)
            ctx.inc(ins, "V")
        V_(_)
        m_G = ctx.cnt["V"]
        if c == 0:
            dbg("G0", [128, HA, C], F32, SRb[:], [("V", m_G)])
            dbg("stok0", [128, HA], F32, stok[:], [("V", m_G)])

        def _(s, c=c):
            s.wait_ge(ctx.s("V"), m_G)
            if c == 0 and "E0" in dbg_names:
                s.wait_ge(ctx.s("DS"), ctx.cnt["DS"])
            if m_a2rdy:
                s.wait_ge(ctx.s("V"), m_a2rdy)
            ins = s.activation(Dst[:], SRb[:], AF.Exp)
            ctx.inc(ins, "S")
        S_(_)
        m_Dexp = ctx.cnt["S"]
        if stop_after == "cdecay" and c == n_chunks - 1:
            return finish_early()

        def _(t, c=c):
            t.wait_ge(ctx.s("S"), m_kt)
            t.wait_ge(ctx.s("V"), m_q2)
            npar = 1 if stop_after in ("cw5a", "cw5k") else 2
            if stop_after == "cw5m":
                npar = 2
            for par in range(npar):
                for h2 in range(8):
                    h = 2 * h2 + par
                    pair = h // 2
                    dst = psA if par == 0 else psB
                    if stop_after == "cw5k":
                        MM(t, dst[:, 128 * h2:128 * h2 + 128],
                                 ktb[ib][:, pair, :], qcb[ib][:, pair, :],
                                 start=True, stop=True)
                    else:
                        MM(t, dst[:, 128 * h2:128 * h2 + 128],
                                 ktb[ib][64 * par:64 * par + 64, pair, :],
                                 qcb[ib][64 * par:64 * par + 64, pair, :],
                                 start=True, stop=True)
                ctx.inc(LASTMM[0], "T")
            if stop_after in ("cw5a", "cw5k"):
                ctx.inc(LASTMM[0], "T")
        T_(_)
        m_A2m = ctx.cnt["T"]
        m_A1m = m_A2m - 1
        if stop_after in ("cw5a", "cw5k", "cw5m") and c == n_chunks - 1:
            return finish_early()

        def _(v, c=c):
            v.wait_ge(ctx.s("S"), m_Dexp)
            v.wait_ge(ctx.s("T"), m_A1m)
            v.tensor_mul(a2b[:, 0:8, :],
                         psA[:].rearrange("p (h q) -> p h q", h=8), Dst[:, 0:8, :])
            v.wait_ge(ctx.s("T"), m_A2m)
            ins = v.tensor_mul(a2b[:, 8:16, :],
                               psB[:].rearrange("p (h q) -> p h q", h=8),
                               Dst[:, 8:16, :])
            ctx.inc(ins, "V")
        V_(_)
        m_a2rdy = ctx.cnt["V"]
        if stop_after == "cw5" and c == n_chunks - 1:
            return finish_early()
        if c == 0:
            dbg("Dst0", [128, HA, C], BF16, Dst[:], [("S", m_Dexp)])
            dbg("a2b0", [128, HA, C], BF16, a2b[:], [("V", m_a2rdy)])
            dbg("q0", [128, NB, C], BF16, qcb[ib][:], [("S", m_qsilu)])
            dbg("kt0", [128, NB, C], BF16, ktb[ib][:], [("S", m_kt)])

        def _(t, c=c):
            t.wait_ge(ctx.s("V"), m_a2rdy)
            t.wait_ge(ctx.s("S"), m_Mb)
            skip_inter = bool(dbg_names) and "nointer0" in dbg_names and c == 0
            for h in range(HA):
                pair, par = h // 2, h % 2
                orgn = psB[64 * par:64 * par + 64, 128 * pair:128 * pair + 128]
                MM(t, orgn, vcb[ib][:, 64 * h:64 * h + 64],
                         a2b[:, 8 * (h % 2) + h // 2, :],
                         start=True, stop=skip_inter)
                if not skip_inter:
                    Mb_ = MEVb if par == 0 else MODb
                    MM(t, orgn, Mb_[:, pair, :], q2b[ib][:, pair, :],
                             start=False, stop=True)
            ctx.inc(LASTMM[0], "T")
            if m_Mup:
                t.wait_ge(ctx.s("V"), m_Mup)
            for h in range(HA):
                pair, par = h // 2, h % 2
                MM(t, psC[64 * par:64 * par + 64, 64 * pair:64 * pair + 64],
                         k2b[ib][:, 64 * h:64 * h + 64],
                         vcb[ib][:, 64 * h:64 * h + 64], start=True, stop=True)
            ctx.inc(LASTMM[0], "T")
            if m_Mup:
                t.wait_ge(ctx.s("S"), m_gmbev)
            for pair in range(8):
                MM(t, psF[:, pair:pair + 1],
                         CI["SELQ"][:, 128 * pair:128 * pair + 128],
                         qdF[:, C * c + C - 1:C * c + C], start=True, stop=True)
            ctx.inc(LASTMM[0], "T")
        T_(_)
        m_gam = ctx.cnt["T"]
        m_dM = m_gam - 1
        m_o = m_gam - 2

        def _(s, c=c):
            s.wait_ge(ctx.s("T"), m_gam)
            ins = s.activation(gmb[:], psF[:, 0:NB], AF.Copy)
            ctx.inc(ins, "S")
        S_(_)
        m_gmbev = ctx.cnt["S"]
        if stop_after == "cw6" and c == n_chunks - 1:
            return finish_early()
        m_psE_sfree = m_v

        def _(s, c=c, cc=cc):
            s.wait_ge(ctx.s("T"), m_o)
            s.activation(o_gla[:, :, cc], psB[:].rearrange("p (h q) -> p h q", h=8),
                         AF.Copy)
            ins = s.activation(osq[:], psB[:], AF.Square)
            ctx.inc(ins, "S")
        S_(_)
        m_ost = ctx.cnt["S"]

        def _(v, c=c):
            v.wait_ge(ctx.s("S"), m_ost)
            v.reduce_sum(opart[:, :, c:c + 1],
                         psB[:].rearrange("p (h q) -> p h q", h=8), axis=AX.X)
            ins = v.reduce_sum(opart2[:, :, c:c + 1],
                               osq[:].rearrange("p (h q) -> p h q", h=8), axis=AX.X)
            ctx.inc(ins, "V")
        V_(_)
        m_opred = ctx.cnt["V"]

        def _(v, c=c):
            v.wait_ge(ctx.s("S"), m_gmbev)
            v.wait_ge(ctx.s("T"), m_dM)
            for h in range(HA):
                pair, par = h // 2, h % 2
                M_ = MEV if par == 0 else MOD
                rows = slice(64 * par, 64 * par + 64)
                ins = v.scalar_tensor_tensor(
                    M_[rows, pair, :], M_[rows, pair, :], gmb[rows, pair:pair + 1],
                    psC[rows, 64 * pair:64 * pair + 64], ALU.mult, ALU.add)
            ctx.inc(ins, "V")
        V_(_)
        m_Mup = ctx.cnt["V"]

        def _(s, c=c):
            s.wait_ge(ctx.s("V"), m_Mup)
            s.activation(MEVb[:], MEV[:], AF.Copy)
            ins = s.activation(MODb[:], MOD[:], AF.Copy)
            ctx.inc(ins, "S")
        S_(_)
        m_Mb = ctx.cnt["S"]
        if stop_after == "cw7" and c == n_chunks - 1:
            return finish_early()

    m_chunkV = ctx.cnt["V"]
    m_chunkS = ctx.cnt["S"]
    m_chunkT = ctx.cnt["T"]
    if stop_after == "chunks":
        return finish_early()
    dbg("o_gla", [128, NB, T], BF16, o_gla[:], [("S", m_chunkS)])

    ostat = ctx.alloc("ostat", [128, NB], F32, side="right")
    ostat2 = ctx.alloc("ostat2", [128, NB], F32, side="right")
    ost_b = ctx.alloc("ost_b", [128, NB], BF16, side="right")
    ost2_b = ctx.alloc("ost2_b", [128, NB], BF16, side="right")

    def _(v):
        v.wait_ge(ctx.s("V"), m_chunkV)
        v.reduce_sum(ostat[:], opart[:], axis=AX.X)
        ins = v.reduce_sum(ostat2[:], opart2[:], axis=AX.X)
        ctx.inc(ins, "V")
    V_(_)
    m_op = ctx.cnt["V"]
    def _(s):
        s.wait_ge(ctx.s("V"), m_op)
        s.activation(ost_b[:], ostat[:], AF.Copy)
        ins = s.activation(ost2_b[:], ostat2[:], AF.Copy)
        ctx.inc(ins, "S")
    S_(_)
    m_opb = ctx.cnt["S"]
    def _(t):
        t.wait_ge(ctx.s("S"), m_opb)
        t.wait_ge(ctx.s("S"), m_gmbev)
        MM(t, psF[:, 0:NB], CI["EDIAG"][:], ost_b[:], start=True, stop=True)
        MM(t, psF[:, NB:2 * NB], CI["EDIAG"][:], ost2_b[:], start=True, stop=True)
        ctx.inc(LASTMM[0], "T")
    T_(_)
    m_gsum = ctx.cnt["T"]
    NEL = float(T * DH)
    def _(v):
        v.wait_ge(ctx.s("T"), m_gsum)
        SYNC(v, "V", v.tensor_scalar(ostat[:], psF[:, 0:NB], 1.0 / NEL, None, ALU.mult))
        SYNC(v, "V", v.tensor_mul(mu2_b[:, 0:NB], ostat[:], ostat[:]))
        ins = v.scalar_tensor_tensor(ostat2[:], psF[:, NB:2 * NB], 1.0 / NEL,
                                     mu2_b[:, 0:NB], ALU.mult, ALU.subtract)
        ctx.inc(ins, "V")
    V_(_)
    m_gvar = ctx.cnt["V"]
    def _(s):
        s.wait_ge(ctx.s("V"), m_gvar)
        s.activation(ostat2[:], ostat2[:], AF.Ln, bias=1e-5)
        ins = s.activation(mu2_b[:, 0:NB], ostat2[:], AF.Exp, scale=-0.5)
        ctx.inc(ins, "S")
    S_(_)
    m_gsd = ctx.cnt["S"]
    def _(v):
        v.wait_ge(ctx.s("S"), m_gsd)
        SYNC(v, "V", v.tensor_mul(gamhat[:], mu2_b[:, 0:NB], gnw[:]))
        SYNC(v, "V", v.tensor_mul(var_b[:, 0:NB], ostat[:], gamhat[:]))
        ins = v.tensor_sub(bethat[:], gnb[:], var_b[:, 0:NB])
        ctx.inc(ins, "V")
    V_(_)
    m_gn = ctx.cnt["V"]

    ctx.free("ost2_b", "ost_b", "ostat2", "ostat", "osq", "gmb", "MODb", "MEVb",
             "MOD", "MEV", "a2b", "ktb1", "ktb0", "q2b1", "q2b0", "qcb1", "qcb0",
             "k2b1", "k2b0", "kcb1", "kcb0", "vcb1", "vcb0", "Dst", "SRb", "stok",
             "kdF", "qdF", "S_sb")
    ctx.free("Wv", "opart2", "opart")
    if stop_after == "a4":
        return finish_early()

    Wg_sb = Wq
    Wo_sb = Wk
    def _(g):
        g.wait_ge(ctx.s("T"), m_chunkT)
        ctx.inc(g.dma_start(out=Wg_sb[:], in_=A_Wg), "DG", 16)
        ctx.inc(g.dma_start(out=Wo_sb[:], in_=A_Wo), "DG", 16)
    G_(_)
    m_wgo = ctx.cnt["DG"]

    x2_sb = ctx.alloc("x2_sb", [128, NB, T + 1], F32, side="right")
    def _(sy):
        sy.wait_ge(ctx.s("V"), m_chunkV)
        sy.wait_ge(ctx.s("S"), m_chunkS)
        ins = sy.dma_start(out=x2_sb[:, :, 1:T + 1], in_=x_spill)
        ctx.inc(ins, "DS", 16)
    Y_(_)
    m_xrel = ctx.cnt["DS"]
    def _(v):
        v.wait_ge(ctx.s("DS"), m_xrel)
        ins = v.memset(x2_sb[:, :, 0:1], 0.0)
        ctx.inc(ins, "V")
    V_(_)
    m_xrl2 = ctx.cnt["V"]

    gateb = ctx.alloc("gateb", [128, NB, 512], BF16, side="right")
    otld = ctx.alloc("otld", [128, NB, 512], BF16, side="right")
    m_gate_ev = 0
    m_otld = 0
    m_wo_half = [0, 0]
    m_xadd = 0
    for tch in range(4):
        t0 = 512 * tch
        for i2 in range(4):
            def _(t, tch=tch, t0=t0, i2=i2):
                if tch == 0 and i2 == 0:
                    t.wait_ge(ctx.s("DG"), m_wgo)
                    t.wait_ge(ctx.s("V"), m_gn)
                if m_gate_ev:
                    t.wait_ge(ctx.s("S"), m_gate_ev)
                if m_otld:
                    t.wait_ge(ctx.s("V"), m_otld)
                for j in range(2):
                    i = 2 * i2 + j
                    for kb in range(NB):
                        MM(t, psA[:, 512 * j:512 * j + 512],
                                 Wg_sb[:, kb, 128 * i:128 * i + 128],
                                 xn[:, kb, t0:t0 + 512],
                                 start=(kb == 0), stop=(kb == NB - 1))
                ctx.inc(LASTMM[0], "T")
            T_(_)
            m_gmm = ctx.cnt["T"]
            def _(s, i2=i2, m_gmm=m_gmm):
                s.wait_ge(ctx.s("T"), m_gmm)
                s.activation(gateb[:, 2 * i2, :], psA[:, 0:512], AF.Sigmoid)
                ins = s.activation(gateb[:, 2 * i2 + 1, :], psA[:, 512:1024],
                                   AF.Sigmoid)
                ctx.inc(ins, "S")
            S_(_)
            m_gate_ev = ctx.cnt["S"]

        def _(s, tch=tch, t0=t0):
            if tch == 0:
                s.wait_ge(ctx.s("V"), m_gn)
            if m_wo_half[1]:
                s.wait_ge(ctx.s("T"), m_wo_half[1])
            for pair in range(NB):
                ins = s.activation(otld[:, pair, :], o_gla[:, pair, t0:t0 + 512],
                                   AF.Identity, bias=bethat[:, pair:pair + 1],
                                   scale=gamhat[:, pair:pair + 1])
            ctx.inc(ins, "S")
        S_(_)
        m_oaff = ctx.cnt["S"]
        def _(v, tch=tch):
            v.wait_ge(ctx.s("S"), max(m_oaff, m_gate_ev))
            ins = v.tensor_mul(otld[:], otld[:], gateb[:])
            ctx.inc(ins, "V")
        V_(_)
        m_otld = ctx.cnt["V"]

        for dmb in range(NB):
            half = dmb % 2
            def _(t, dmb=dmb, half=half):
                t.wait_ge(ctx.s("V"), m_otld)
                if m_wo_half[half]:
                    t.wait_ge(ctx.s("V"), m_xadd_half[half])
                ps = psB[:, 512 * half:512 * half + 512]
                for kb in range(NB):
                    MM(t, ps, Wo_sb[:, kb, 128 * dmb:128 * dmb + 128],
                             otld[:, kb, :], start=(kb == 0), stop=(kb == NB - 1))
                ctx.inc(LASTMM[0], "T")
            T_(_)
            m_wo_half[half] = ctx.cnt["T"]
            def _(v, dmb=dmb, half=half, tch=tch, t0=t0, mwo=ctx.cnt["T"]):
                v.wait_ge(ctx.s("T"), mwo)
                ps = psB[:, 512 * half:512 * half + 512]
                if dmb == 0 and tch == 0:
                    v.wait_ge(ctx.s("V"), m_xrl2)
                ins = v.tensor_add(x2_sb[:, dmb, 1 + t0:1 + t0 + 512], ps,
                                   x2_sb[:, dmb, 1 + t0:1 + t0 + 512])
                ctx.inc(ins, "V")
            V_(_)
            if dmb == 0 and tch == 0:
                m_xadd_half = [0, 0]
                m_x2_tch = []
            m_xadd_half[half] = ctx.cnt["V"]
        m_x2_tch.append(ctx.cnt["V"])
    m_xnew = ctx.cnt["V"]

    ctx.free("otld", "gateb")
    ctx.free("o_gla", "Wk", "Wq", "ba", "Wa", "xn")
    dbg("xA", [128, NB, T], F32, x2_sb[:, :, 1:T + 1], [("V", m_xnew)])

    if stop_after == "a5":
        return finish_early()
    def _(sy):
        sy.wait_ge(ctx.s("V"), m_xnew)
        ins = sy.dma_start(out=x_spill, in_=x2_sb[:, :, 1:T + 1])
        ctx.inc(ins, "DS", 16)
    Y_(_)
    m_xnspill = ctx.cnt["DS"]

    xi_sb = ctx.alloc("xi_sb", [128, NB, T + 1], F32)
    def _(v):
        ins = v.memset(xi_sb[:, :, 0:1], 0.0)
        ctx.inc(ins, "V")
    V_(_)
    m_ln2, ln2_marks = emit_ln(x2_sb[:], xi_sb[:, :, 1:T + 1], waits=[],
                               waits_tch=[[("V", m_x2_tch[t])] for t in range(4)],
                               scratch=ln_scr)
    dbg("xi0", [128, NB, T], F32, xi_sb[:, :, 1:T + 1], [("V", m_ln2)])
    ctx.free("x2_sb")

    CIB = {}
    for k in cstB:
        v = cstB[k]
        CIB[k] = ctx.alloc("k_" + k, v.shape, F32 if v.dtype == np.float32 else BF16)

    def _(g):
        g.wait_ge(ctx.s("V"), m_ln2)
        for k in CIB:
            ctx.inc(g.dma_start(out=CIB[k][:], in_=CONST[k]), "DG", 16)
    G_(_)
    m_cib = ctx.cnt["DG"]

    def barrier():
        j = {k: ctx.cnt[k] for k in ("T", "V", "S", "DS", "DG", "G")}
        for emit, eng in ((T_, "T"), (V_, "V"), (S_, "S"), (Y_, "DS"), (G_, "DG")):
            def _(e, j=j):
                for k2, v2 in j.items():
                    if v2:
                        e.wait_ge(ctx.s(k2), v2)
                e.nop()
            emit(_)
        return j

    m_xi = m_ln2
    barrier()
    ctx.free("psF", "psE", "psD", "psC")
    sc2 = ctx.alloc("sc2", [128, 1024], F32, psum=True)
    sc3 = ctx.alloc("sc3", [128, 1024], F32, psum=True)
    SCB = [psA, psB, sc2, sc3]
    for l in range(2):
        barrier()
        xbfa = ctx.alloc(f"xbfa{l}", [128, NB, T + 1], BF16, side="right")
        wqkv_sb = ctx.alloc(f"wqkv{l}", [128, NB, 1536], BF16, side="right")
        wo_sb2 = ctx.alloc(f"wob{l}", [128, 4, D], BF16, side="right")
        qs_sb = ctx.alloc(f"qs{l}", [128, DH], BF16, side="right")
        ks_sb = ctx.alloc(f"ks{l}", [128, DH], BF16, side="right")
        v_sb = ln_scr
        ktr = ctx.alloc(f"ktr{l}", [65, HB, 1024], BF16, side="right")
        qtr = ctx.alloc(f"qtr{l}", [65, HB, 512], BF16, side="right")
        oT = ctx.alloc(f"oT{l}", [128, 4, 512], BF16, side="right")
        qraw = ctx.alloc(f"qraw{l}", [128, 512], BF16, side="right")
        kraw = ctx.alloc(f"kraw{l}", [128, 512], BF16, side="right")
        sqb = ctx.alloc(f"sqb{l}", [128, 1024], BF16, side="right")
        rnx = ctx.alloc(f"rnx{l}", [128, 1024], BF16, side="right")
        qp = ctx.alloc(f"qp{l}", [128, 512], BF16, side="right")
        kp = ctx.alloc(f"kp{l}", [128, 512], BF16, side="right")
        ss8 = ctx.alloc(f"ss8{l}", [128, 16], F32, side="right")
        sd8 = ctx.alloc(f"sd8{l}", [128, 16], F32, side="right")
        rn8 = ctx.alloc(f"rn8{l}", [128, 16], F32, side="right")
        nmall = ctx.alloc(f"nmall{l}", [128, HB, 4], F32, side="right")
        ssall = ctx.alloc(f"ssall{l}", [128, HB, 4], F32, side="right")
        lnsb = ctx.alloc(f"lnsb{l}", [128, HB, 4], F32, side="right")
        zf = ctx.alloc(f"zf{l}", [128, 32], F32, side="right")
        ztb = ctx.alloc(f"ztb{l}", [32, 128], BF16, side="right")
        ptil = ctx.alloc(f"ptil{l}", [128, 4, 640], BF16, side="right")

        def _(g, l=l):
            ctx.inc(g.dma_start(out=wqkv_sb[:], in_=B_wqkv[l]), "DG", 16)
            ctx.inc(g.dma_start(out=wo_sb2[:], in_=B_wo[l]), "DG", 16)
            ctx.inc(g.dma_start(out=qs_sb[:], in_=B_qs[l]), "DG", 16)
            ctx.inc(g.dma_start(out=ks_sb[:], in_=B_ks[l]), "DG", 16)
        G_(_)
        m_wl = ctx.cnt["DG"]

        def _(v, l=l):
            v.wait_ge(ctx.s("V"), m_xi)
            v.memset(xbfa[:, :, 0:1], 0.0)
            ins = v.memset(ktr[64:65, :, :], 1.0)
            ctx.inc(ins, "V")
        V_(_)
        m_xbset = ctx.cnt["V"]
        def _(s, l=l):
            s.wait_ge(ctx.s("V"), m_xbset)
            for tch in range(4):
                ins = s.activation(xbfa[:, :, 1 + 512 * tch:1 + 512 * tch + 512],
                                   xi_sb[:, :, 1 + 512 * tch:1 + 512 * tch + 512],
                                   AF.Copy)
            ctx.inc(ins, "S")
        S_(_)
        m_xbf = ctx.cnt["S"]

        m_oT_done = 0
        m_resid = 0
        m_xbfw = m_xbf
        m_tr_ev = 0
        m_qkev = 0
        m_tf = 0
        m_tp = 0
        m_p1sfree = [0, 0, 0, 0]
        m_p2sfree = [0, 0, 0, 0]
        m_ptilfree = [0, 0, 0, 0]
        m_oslotfree = [0] * 8
        m_resid_half = [0, 0]
        for w in range(NW):
            m_tf_l = {}
            for ci in range(5):
                if ci < 4:
                    c = 4 * w + ci
                    def _(t, l=l, w=w, c=c, ci=ci, mqk=m_qkev):
                        if c == 0:
                            t.wait_ge(ctx.s("DG"), m_wl)
                            t.wait_ge(ctx.s("S"), m_xbf)
                        if w > 0 and ci == 0:
                            t.wait_ge(ctx.s("S"), m_xbfw)
                        if mqk:
                            t.wait_ge(ctx.s("S"), mqk)
                        for n in range(3):
                            ps = psA[:, 0:512] if n == 0 else (
                                psA[:, 512:1024] if n == 1 else psB[:, 0:512])
                            for kb in range(NB):
                                off = 0 if kb >= 4 else 1
                                MM(t, ps,
                                   xbfa[:, kb, off + 128 * c: off + 128 * c + 128],
                                   wqkv_sb[:, kb, 512 * n:512 * n + 512],
                                   start=(kb == 0), stop=(kb == NB - 1))
                            ctx.inc(LASTMM[0], "T")
                    T_(_)
                    m_qkv = ctx.cnt["T"]
                if ci >= 1:
                    cm = ci - 1
                    def _(t, cm=cm, mtf=m_tf_l[ci - 1], mtrev=m_tr_ev):
                        t.wait_ge(ctx.s("V"), mtf)
                        if mtrev:
                            t.wait_ge(ctx.s("V"), mtrev)
                        for h in range(HB):
                            MM(t, sc2[0:64, 128 * h:128 * h + 128],
                               qp[:, 64 * h:64 * h + 64], CI["I128b"][:],
                               start=True, stop=True)
                        ctx.inc(LASTMM[0], "T")
                        for h in range(HB):
                            MM(t, sc3[0:64, 128 * h:128 * h + 128],
                               kp[:, 64 * h:64 * h + 64], CI["I128b"][:],
                               start=True, stop=True)
                        ctx.inc(LASTMM[0], "T")
                    T_(_)
                    m_tp = ctx.cnt["T"]
                    def _(v, cm=cm, w=w, m_tp=m_tp):
                        v.wait_ge(ctx.s("T"), m_tp)
                        v.tensor_copy(qtr[0:64, :, 128 * cm:128 * cm + 128],
                                      sc2[0:64, :].rearrange("p (h q) -> p h q", h=HB))
                        ins = v.tensor_copy(
                            ktr[0:64, :, 512 * (w % 2) + 128 * cm:
                                512 * (w % 2) + 128 * cm + 128],
                            sc3[0:64, :].rearrange("p (h q) -> p h q", h=HB))
                        ctx.inc(ins, "V")
                    V_(_)
                    m_tr_ev = ctx.cnt["V"]
                if ci < 4:
                    def _(s, c=c, m_qkv=m_qkv, mtf=m_tf):
                        s.wait_ge(ctx.s("T"), m_qkv)
                        if mtf:
                            s.wait_ge(ctx.s("V"), mtf)
                        s.activation(qraw[:], psA[:, 0:512], AF.Copy)
                        s.activation(kraw[:], psA[:, 512:1024], AF.Copy)
                        s.activation(v_sb[:, c % 8, :], psB[:, 0:512], AF.Copy)
                        s.activation(sqb[:, 0:512], psA[:, 0:512], AF.Square)
                        ins = s.activation(sqb[:, 512:1024], psA[:, 512:1024],
                                           AF.Square)
                        ctx.inc(ins, "S")
                    S_(_)
                    m_qkev = ctx.cnt["S"]
                    def _(v, m_qkev=m_qkev):
                        v.wait_ge(ctx.s("S"), m_qkev)
                        v.reduce_sum(ss8[:, 0:8].unsqueeze(2),
                                     sqb[:, 0:512].rearrange("p (h e) -> p h e", h=HB),
                                     axis=AX.X)
                        ins = v.reduce_sum(ss8[:, 8:16].unsqueeze(2),
                                     sqb[:, 512:1024].rearrange("p (h e) -> p h e", h=HB),
                                     axis=AX.X)
                        ctx.inc(ins, "V")
                    V_(_)
                    m_ss = ctx.cnt["V"]
                    def _(s, m_ss=m_ss):
                        s.wait_ge(ctx.s("V"), m_ss)
                        s.activation(sd8[:], ss8[:], AF.Ln, bias=1e-24)
                        ins = s.activation(rn8[:], sd8[:], AF.Exp, scale=-0.5)
                        ctx.inc(ins, "S")
                    S_(_)
                    m_rn = ctx.cnt["S"]
                    def _(v, c=c, m_rn=m_rn, mtp=m_tp):
                        v.wait_ge(ctx.s("S"), m_rn)
                        if mtp:
                            v.wait_ge(ctx.s("T"), mtp)
                        for which in range(2):
                            raw = qraw if which == 0 else kraw
                            out_p = qp if which == 0 else kp
                            scl = qs_sb if which == 0 else ks_sb
                            CS = CIB["CSQ8"] if which == 0 else CIB["CSK1"]
                            SN = CIB["SNQ8"] if which == 0 else CIB["SNK1"]
                            off = 512 * which
                            rnxs = rnx[:, off:off + 512]
                            sqbs = sqb[:, off:off + 512]
                            rn_ap = rn8[:, 8 * which:8 * which + 8].unsqueeze(2) \
                                .broadcast_to([128, HB, DH])
                            qs_ap = scl[:].unsqueeze(1).broadcast_to([128, HB, DH])
                            v.tensor_mul(
                                rnxs.rearrange("p (h e) -> p h e", h=HB), rn_ap, qs_ap)
                            v.tensor_mul(sqbs, raw[:], rnxs)
                            qn3 = sqbs.rearrange("p (h e) -> p h e", h=HB)
                            op3 = out_p[:].rearrange("p (h e) -> p h e", h=HB)
                            t23 = rnxs.rearrange("p (h e) -> p h e", h=HB)
                            cs_ap = CS[:, c, :].unsqueeze(1).broadcast_to([128, HB, DH])
                            sn0 = SN[:, c, 0:32].unsqueeze(1).broadcast_to([128, HB, 32])
                            sn1 = SN[:, c, 32:64].unsqueeze(1).broadcast_to([128, HB, 32])
                            v.tensor_mul(op3, qn3, cs_ap)
                            v.tensor_mul(t23[:, :, 0:32], qn3[:, :, 32:64], sn0)
                            v.tensor_mul(t23[:, :, 32:64], qn3[:, :, 0:32], sn1)
                            ins = v.tensor_add(out_p[:], out_p[:], rnxs)
                            ctx.inc(ins, "V")
                    V_(_)
                    m_tf = ctx.cnt["V"]
                    m_tf_l[ci] = m_tf
                    if l == 0 and c == 0:
                        dbg("qp00", [128, 512], BF16, qp[:], [("V", m_tf)])
                        dbg("kp00", [128, 512], BF16, kp[:], [("V", m_tf)])

            if l == 0 and w == 0:
                dbg("v00", [128, 4, 512], BF16, v_sb[:, 0:4, :], [("V", m_tr_ev)])
                dbg("qtr00", [64, HB, 512], BF16, qtr[0:64], [("V", m_tr_ev)])
                dbg("ktr00", [64, HB, 512], BF16, ktr[0:64, :, 0:512], [("V", m_tr_ev)])
            tiles = [(h, ci) for h in range(HB) for ci in range(4)]
            m_p1v = 0
            for ti in range(0, 32, 2):
                for k2 in range(2):
                    h, ci = tiles[ti + k2]
                    b = (ti + k2) % 4
                    psX = SCB[b]
                    c = 4 * w + ci
                    jmin = max(0, 4 - c)
                    def _(t, h=h, c=c, ci=ci, jmin=jmin, psX=psX,
                          mtr=m_tr_ev, msf=m_p1sfree[b]):
                        t.wait_ge(ctx.s("V"), mtr)
                        if msf:
                            t.wait_ge(ctx.s("S"), msf)
                        for j in range(jmin, 5):
                            kt = c - 4 + j
                            slot = (kt // 4) % 2
                            MM(t, psX[:, 128 * j:128 * j + 128],
                               qtr[0:64, h, 128 * ci:128 * ci + 128],
                               ktr[0:64, h, 512 * slot + 128 * (kt % 4):
                                   512 * slot + 128 * (kt % 4) + 128],
                               start=True, stop=True)
                        ctx.inc(LASTMM[0], "T")
                    T_(_)
                    m_s1 = ctx.cnt["T"]
                    def _(v, h=h, ci=ci, jmin=jmin, psX=psX, m_s1=m_s1):
                        v.wait_ge(ctx.s("T"), m_s1)
                        if jmin == 0:
                            mview = psX[:, 0:640].rearrange(
                                "p (a b) -> p a b", b=128)[:, 0:5:4, :]
                            ins = v.tensor_add(mview, mview, CIB["P1MASK"][:])
                        else:
                            ins = v.tensor_add(psX[:, 512:640], psX[:, 512:640],
                                               CIB["P1MASK"][:, 1, :])
                        mend = ctx.inc(ins, "V")
                        v.wait_ge(ctx.s("V"), mend)
                        ins = v.reduce_max(nmall[:, h, ci:ci + 1],
                                           psX[:, 128 * jmin:640], axis=AX.X,
                                           negate=True)
                        ctx.inc(ins, "V")
                    V_(_)
                    m_p1v = ctx.cnt["V"]
                def _(s, ti=ti, w=w, m_p1v=m_p1v):
                    s.wait_ge(ctx.s("V"), m_p1v)
                    for k2 in range(2):
                        h, ci = tiles[ti + k2]
                        psX = SCB[(ti + k2) % 4]
                        jmin = max(0, 4 - (4 * w + ci))
                        ins = s.activation(ptil[:, 0, 128 * jmin:640],
                                           psX[:, 128 * jmin:640], AF.Exp,
                                           bias=nmall[:, h, ci:ci + 1],
                                           accum_out=ssall[:, h, ci:ci + 1])
                    ctx.inc(ins, "S")
                S_(_)
                m_p1sfree[ti % 4] = m_p1sfree[(ti + 1) % 4] = ctx.cnt["S"]
            m_p1 = ctx.cnt["S"]

            if l == 0 and w == 0:
                dbg("nm00", [128, HB, 4], F32, nmall[:], [("S", m_p1)])
                dbg("ss00", [128, HB, 4], F32, ssall[:], [("S", m_p1)])
            def _(s, m_p1=m_p1):
                s.wait_ge(ctx.s("S"), m_p1)
                ins = s.activation(lnsb[:], ssall[:], AF.Ln)
                ctx.inc(ins, "S")
            S_(_)
            m_lns = ctx.cnt["S"]
            def _(v, m_lns=m_lns):
                v.wait_ge(ctx.s("S"), m_lns)
                ins = v.tensor_sub(zf[:].rearrange("p (h q) -> p h q", h=HB),
                                   nmall[:], lnsb[:])
                ctx.inc(ins, "V")
            V_(_)
            m_zf = ctx.cnt["V"]
            def _(t, m_zf=m_zf):
                t.wait_ge(ctx.s("V"), m_zf)
                MM(t, psB[0:32, 0:128], zf[:], CI["I128f"][:], start=True, stop=True)
                ctx.inc(LASTMM[0], "T")
            T_(_)
            m_zt = ctx.cnt["T"]
            def _(s, m_zt=m_zt):
                s.wait_ge(ctx.s("T"), m_zt)
                ins = s.activation(ztb[:], psB[0:32, 0:128], AF.Copy)
                ctx.inc(ins, "S")
            S_(_)
            m_ztb = ctx.cnt["S"]
            def _(sy, m_ztb=m_ztb):
                sy.wait_ge(ctx.s("S"), m_ztb)
                ins = sy.dma_start(out=qtr[64:65, :, :], in_=ztb[:])
                ctx.inc(ins, "DS", 16)
            Y_(_)
            m_zdma = ctx.cnt["DS"]

            m_p2v = 0
            pend = None
            for ti in range(0, 32, 2):
                for k2 in range(2):
                    h, ci = tiles[ti + k2]
                    b = (ti + k2) % 4
                    psX = SCB[b]
                    c = 4 * w + ci
                    jmin = max(0, 4 - c)
                    def _(t, h=h, c=c, ci=ci, jmin=jmin, psX=psX,
                          m_zdma=m_zdma, msf=m_p2sfree[b]):
                        t.wait_ge(ctx.s("DS"), m_zdma)
                        if msf:
                            t.wait_ge(ctx.s("S"), msf)
                        for j in range(jmin, 5):
                            kt = c - 4 + j
                            slot = (kt // 4) % 2
                            MM(t, psX[:, 128 * j:128 * j + 128],
                               ktr[0:65, h, 512 * slot + 128 * (kt % 4):
                                   512 * slot + 128 * (kt % 4) + 128],
                               qtr[0:65, h, 128 * ci:128 * ci + 128],
                               start=True, stop=True)
                        ctx.inc(LASTMM[0], "T")
                    T_(_)
                    m_s2 = ctx.cnt["T"]
                    def _(v, jmin=jmin, psX=psX, m_s2=m_s2):
                        v.wait_ge(ctx.s("T"), m_s2)
                        if jmin == 0:
                            mview = psX[:, 0:640].rearrange(
                                "p (a b) -> p a b", b=128)[:, 0:5:4, :]
                            ins = v.tensor_add(mview, mview, CIB["P2ADD"][:])
                        else:
                            ins = v.tensor_add(psX[:, 512:640], psX[:, 512:640],
                                               CIB["P2ADD"][:, 1, :])
                        ctx.inc(ins, "V")
                    V_(_)
                    m_p2v = ctx.cnt["V"]
                def _(s, ti=ti, w=w, m_p2v=m_p2v,
                      mpt=(m_ptilfree[ti % 4], m_ptilfree[(ti + 1) % 4])):
                    s.wait_ge(ctx.s("V"), m_p2v)
                    for k2 in range(2):
                        if mpt[k2]:
                            s.wait_ge(ctx.s("T"), mpt[k2])
                        h, ci = tiles[ti + k2]
                        b = (ti + k2) % 4
                        jmin = max(0, 4 - (4 * w + ci))
                        ins = s.activation(ptil[:, b, 128 * jmin:640],
                                           SCB[b][:, 128 * jmin:640], AF.Exp)
                    ctx.inc(ins, "S")
                S_(_)
                msec = ctx.cnt["S"]
                m_p2sfree[ti % 4] = m_p2sfree[(ti + 1) % 4] = msec
                mgmask = msec

                def emit_oacc(oti, omg, w=w):
                    def _(t, ti=oti, w=w, mg=omg,
                          mev=(m_oslotfree[oti % 8], m_oslotfree[(oti + 1) % 8])):
                        t.wait_ge(ctx.s("S"), mg)
                        for k2 in range(2):
                            k = ti + k2
                            h, ci = tiles[k]
                            b = k % 4
                            sl = k % 8
                            oten = SCB[sl % 4]
                            ocol = 640 + 128 * (sl // 4)
                            if mev[k2]:
                                t.wait_ge(ctx.s("V"), mev[k2])
                            par = h % 2
                            orgn = oten[64 * par:64 * par + 64, ocol:ocol + 128]
                            c = 4 * w + ci
                            jmin = max(0, 4 - c)
                            for j in range(jmin, 5):
                                kt = c - 4 + j
                                MM(t, orgn, v_sb[:, kt % 8, 64 * h:64 * h + 64],
                                   ptil[:, b, 128 * j:128 * j + 128],
                                   start=(j == jmin), stop=(j == 4))
                            ctx.inc(LASTMM[0], "T")
                    T_(_)
                    mT2 = ctx.cnt["T"]
                    m_ptilfree[oti % 4] = mT2 - 1
                    m_ptilfree[(oti + 1) % 4] = mT2
                    def _(v, ti=oti, mT2=mT2):
                        v.wait_ge(ctx.s("T"), mT2)
                        for k2 in range(2):
                            k = ti + k2
                            h, ci = tiles[k]
                            sl = k % 8
                            oten = SCB[sl % 4]
                            ocol = 640 + 128 * (sl // 4)
                            par = h % 2
                            p2i = h // 2
                            ins = v.tensor_copy(
                                oT[64 * par:64 * par + 64, p2i,
                                   128 * ci:128 * ci + 128],
                                oten[64 * par:64 * par + 64, ocol:ocol + 128])
                        ctx.inc(ins, "V")
                    V_(_)
                    m_oslotfree[oti % 8] = m_oslotfree[(oti + 1) % 8] = ctx.cnt["V"]

                emit_oacc(ti, mgmask)
            m_oev = ctx.cnt["V"]

            if l == 0 and w == 0:
                dbg("oT00", [128, 4, 512], BF16, oT[:], [("V", m_oev)])
            m_wo = 0
            for dmb in range(NB):
                half = dmb % 2
                def _(t, dmb=dmb, half=half, m_oev=m_oev, w=w,
                      mrh=m_resid_half[half]):
                    t.wait_ge(ctx.s("V"), m_oev)
                    if mrh:
                        t.wait_ge(ctx.s("V"), mrh)
                    ps = psA[:, 512 * half:512 * half + 512]
                    for kb in range(4):
                        MM(t, ps, wo_sb2[:, kb, 128 * dmb:128 * dmb + 128],
                           oT[:, kb, :], start=(kb == 0), stop=(kb == 3))
                    ctx.inc(LASTMM[0], "T")
                T_(_)
                m_wo = ctx.cnt["T"]
                def _(v, dmb=dmb, half=half, w=w, m_wo=m_wo):
                    v.wait_ge(ctx.s("T"), m_wo)
                    ps = psA[:, 512 * half:512 * half + 512]
                    dst = xi_sb[:, dmb, 1 + 512 * w:1 + 512 * w + 512]
                    ins = v.tensor_add(dst, ps, dst)
                    ctx.inc(ins, "V")
                V_(_)
                m_resid_half[half] = ctx.cnt["V"]
            m_resid = ctx.cnt["V"]
            def _(s, w=w, m_resid=m_resid):
                s.wait_ge(ctx.s("V"), m_resid)
                ins = s.activation(xbfa[:, :, 1 + 512 * w:1 + 512 * w + 512],
                                   xi_sb[:, :, 1 + 512 * w:1 + 512 * w + 512],
                                   AF.Copy)
                ctx.inc(ins, "S")
            S_(_)
            m_xbfw = ctx.cnt["S"]

        m_attn_done_V = ctx.cnt["V"]
        m_attn_done_S = ctx.cnt["S"]
        dbg(f"xiA{l}", [128, NB, T], F32, xi_sb[:, :, 1:T + 1], [("V", m_attn_done_V)])

        barrier()
        ctx.free(f"ptil{l}", f"ztb{l}", f"zf{l}", f"lnsb{l}",
                 f"ssall{l}", f"nmall{l}", f"rn8{l}", f"sd8{l}", f"ss8{l}",
                 f"kp{l}", f"qp{l}", f"rnx{l}", f"sqb{l}", f"kraw{l}", f"qraw{l}",
                 f"oT{l}", f"qtr{l}", f"ktr{l}", f"ks{l}", f"qs{l}",
                 f"wob{l}", f"wqkv{l}", f"xbfa{l}")

        h2 = ctx.alloc(f"h2_{l}", [128, NB, T], BF16, side="right")
        lngc = ctx.alloc(f"lng{l}", [128, NB], F32, side="right")
        lnbc = ctx.alloc(f"lnb{l}", [128, NB], F32, side="right")
        b1ac = ctx.alloc(f"b1a{l}", [128, 22], F32, side="right")
        b1gc = ctx.alloc(f"b1g{l}", [128, 22], F32, side="right")
        b2c = ctx.alloc(f"b2{l}", [128, NB], F32, side="right")
        w1q = ctx.alloc(f"w1q{l}", [128, NB, 1536], BF16, side="right")
        w2q = ctx.alloc(f"w2q{l}", [128, 6, D], BF16, side="right")
        glu = ctx.alloc(f"glu{l}", [128, 6, T], BF16, side="right")
        glscr = ctx.alloc(f"glscr{l}", [128, 1024], BF16, side="right")

        def _(g, l=l):
            ctx.inc(g.dma_start(out=lngc[:], in_=B_lng[l]), "DG", 16)
            ctx.inc(g.dma_start(out=lnbc[:], in_=B_lnb[l]), "DG", 16)
            ctx.inc(g.dma_start(out=b1ac[:], in_=B_b1a[l]), "DG", 16)
            ctx.inc(g.dma_start(out=b1gc[:], in_=B_b1g[l]), "DG", 16)
            ctx.inc(g.dma_start(out=b2c[:], in_=B_b2[l]), "DG", 16)
        G_(_)
        m_fl = ctx.cnt["DG"]

        m_h2a_t = [0, 0, 0, 0]
        def _affine(tch, mk):
            def _(v, tch=tch):
                if tch == 0:
                    v.wait_ge(ctx.s("DG"), m_fl)
                for kb in range(NB):
                    ins = v.tensor_scalar(h2[:, kb, 512 * tch:512 * tch + 512],
                                          h2[:, kb, 512 * tch:512 * tch + 512],
                                          lngc[:, kb:kb + 1], lnbc[:, kb:kb + 1],
                                          ALU.mult, ALU.add)
                ctx.inc(ins, "V")
            V_(_)
            m_h2a_t[tch] = ctx.cnt["V"]
        m_h2, _lnm = emit_ln(xi_sb[:], h2[:], waits=[("V", m_attn_done_V)],
                             shift=True, pstats=(sc2[:, 0:512], sc3[:, 0:512]),
                             post_tch=_affine)
        m_h2a = ctx.cnt["V"]
        dbg(f"h2_{l}", [128, NB, T], BF16, h2[:], [("V", m_h2a)])

        QBLKS = [list(range(6 * qd, min(6 * qd + 6, 22))) for qd in range(4)]
        m_w2add = 0
        m_gluev = 0
        m_w1read = {}
        m_w2mm_last = 0
        m_glu_ev = [0, 0]
        m_gelu_ev = [0, 0]
        m_w2slot = [0, 0, 0, 0]
        PSB2 = [psA, psB]
        for qd in range(4):
            blks = QBLKS[qd]
            nb_q = len(blks)
            m_w1blk = {}
            for ji, gb in enumerate(blks):
                def _(g, qd=qd, ji=ji, gb=gb, l=l,
                      mrd=m_w1read.get(ji, 0)):
                    if mrd:
                        g.wait_ge(ctx.s("T"), mrd)
                    a0 = 128 * gb
                    ctx.inc(g.dma_start(out=w1q[:, :, 128 * ji:128 * ji + 128],
                                        in_=B_w1[l][:, :, a0:a0 + 128]), "DG", 16)
                    ctx.inc(g.dma_start(
                        out=w1q[:, :, 768 + 128 * ji:768 + 128 * ji + 128],
                        in_=B_w1[l][:, :, 2816 + a0:2816 + a0 + 128]), "DG", 16)
                G_(_)
                m_w1blk[ji] = ctx.cnt["DG"]
            def _(g, blks=blks, nb_q=nb_q, l=l, mw=m_w2mm_last):
                if mw:
                    g.wait_ge(ctx.s("T"), mw)
                ctx.inc(g.dma_start(out=w2q[:, 0:nb_q, :],
                                    in_=B_w2[l][:, blks[0]:blks[0] + nb_q, :]),
                        "DG", 16)
            G_(_)
            m_w2dma = ctx.cnt["DG"]

            it = 0
            for ji in range(nb_q):
                gb = blks[ji]
                for tch in range(4):
                    buf = PSB2[it % 2]
                    def _(t, ji=ji, tch=tch, qd=qd, it=it, buf=buf,
                          mblk=m_w1blk[ji],
                          mge=m_glu_ev[it % 2], mse=m_gelu_ev[it % 2]):
                        t.wait_ge(ctx.s("DG"), mblk)
                        if qd == 0:
                            t.wait_ge(ctx.s("V"), m_h2a_t[tch])
                        if it < 2 and m_w2add:
                            t.wait_ge(ctx.s("V"), m_w2add)
                        if mge:
                            t.wait_ge(ctx.s("V"), mge)
                            t.wait_ge(ctx.s("S"), mse)
                        for kb in range(NB):
                            MM(t, buf[:, 0:512], w1q[:, kb, 128 * ji:128 * ji + 128],
                               h2[:, kb, 512 * tch:512 * tch + 512],
                               start=(kb == 0), stop=(kb == NB - 1))
                        ctx.inc(LASTMM[0], "T")
                        for kb in range(NB):
                            MM(t, buf[:, 512:1024],
                               w1q[:, kb, 768 + 128 * ji:768 + 128 * ji + 128],
                               h2[:, kb, 512 * tch:512 * tch + 512],
                               start=(kb == 0), stop=(kb == NB - 1))
                        ctx.inc(LASTMM[0], "T")
                    T_(_)
                    m_ag = ctx.cnt["T"]
                    if tch == 3:
                        m_w1read[ji] = m_ag
                    def _(s, gb=gb, buf=buf, it=it, m_ag=m_ag):
                        s.wait_ge(ctx.s("T"), m_ag)
                        ins = s.activation(
                            glscr[:, 512 * (it % 2):512 * (it % 2) + 512],
                            buf[:, 512:1024], AF.Gelu, bias=b1gc[:, gb:gb + 1])
                        ctx.inc(ins, "S")
                    S_(_)
                    m_gelu_ev[it % 2] = ctx.cnt["S"]
                    def _(v, ji=ji, tch=tch, gb=gb, buf=buf, it=it,
                          mg=ctx.cnt["S"]):
                        v.wait_ge(ctx.s("S"), mg)
                        ins = v.scalar_tensor_tensor(
                            glu[:, ji, 512 * tch:512 * tch + 512], buf[:, 0:512],
                            b1ac[:, gb:gb + 1],
                            glscr[:, 512 * (it % 2):512 * (it % 2) + 512],
                            ALU.add, ALU.mult)
                        ctx.inc(ins, "V")
                    V_(_)
                    m_glu_ev[it % 2] = ctx.cnt["V"]
                    it += 1
            m_gluev = ctx.cnt["V"]

            SLOT = [(psA, 0), (psA, 1), (psB, 0), (psB, 1)]
            it2 = 0
            for tch in range(4):
                for dmb in range(NB):
                    ps, half = SLOT[it2 % 4]
                    def _(t, tch=tch, dmb=dmb, ps=ps, half=half, it2=it2,
                          nb_q=nb_q, mglu=m_gluev, mw2=m_w2dma,
                          msl=m_w2slot[it2 % 4]):
                        t.wait_ge(ctx.s("V"), mglu)
                        t.wait_ge(ctx.s("DG"), mw2)
                        if msl:
                            t.wait_ge(ctx.s("V"), msl)
                        pss = ps[:, 512 * half:512 * half + 512]
                        for jji in range(nb_q):
                            MM(t, pss, w2q[:, jji, 128 * dmb:128 * dmb + 128],
                               glu[:, jji, 512 * tch:512 * tch + 512],
                               start=(jji == 0), stop=(jji == nb_q - 1))
                        ctx.inc(LASTMM[0], "T")
                    T_(_)
                    m_w2m = ctx.cnt["T"]
                    def _(v, tch=tch, dmb=dmb, ps=ps, half=half, qd=qd,
                          m_w2m=m_w2m):
                        v.wait_ge(ctx.s("T"), m_w2m)
                        pss = ps[:, 512 * half:512 * half + 512]
                        dst = xi_sb[:, dmb, 1 + 512 * tch:1 + 512 * tch + 512]
                        if qd == 0:
                            ins = v.scalar_tensor_tensor(dst, pss, b2c[:, dmb:dmb + 1],
                                                         dst, ALU.add, ALU.add)
                        else:
                            ins = v.tensor_add(dst, pss, dst)
                        ctx.inc(ins, "V")
                    V_(_)
                    m_w2slot[it2 % 4] = ctx.cnt["V"]
                    it2 += 1
            m_w2mm_last = ctx.cnt["T"]
            m_w2add = ctx.cnt["V"]

        m_xi = ctx.cnt["V"]
        dbg(f"xiF{l}", [128, NB, T], F32, xi_sb[:, :, 1:T + 1], [("V", m_xi)])
        barrier()
        ctx.free(f"glscr{l}", f"glu{l}", f"w2q{l}", f"w1q{l}", f"b2{l}",
                 f"b1g{l}", f"b1a{l}", f"lnb{l}", f"lng{l}", f"h2_{l}")

    m_inner = m_xi

    x3_sb = ctx.alloc("x3_sb", [128, NB, T], F32, side="right")
    yst = ctx.alloc("yst", [128, NB, 512], F32, side="right")
    def _(sy):
        sy.wait_ge(ctx.s("DS"), m_xnspill)
        sy.wait_ge(ctx.s("V"), ctx.cnt["V"])
        sy.wait_ge(ctx.s("S"), ctx.cnt["S"])
        sy.wait_ge(ctx.s("T"), ctx.cnt["T"])
        ins = sy.dma_start(out=x3_sb[:], in_=x_spill)
        ctx.inc(ins, "DS", 16)
    Y_(_)
    m_x3 = ctx.cnt["DS"]
    m_ydma = 0
    for tch in range(4):
        t0 = 512 * tch
        def _(v, tch=tch, t0=t0):
            v.wait_ge(ctx.s("V"), m_inner)
            v.wait_ge(ctx.s("DS"), m_x3)
            if m_ydma:
                v.wait_ge(ctx.s("DS"), m_ydma)
            ins = v.tensor_add(yst[:], x3_sb[:, :, t0:t0 + 512],
                               xi_sb[:, :, 1 + t0:1 + t0 + 512])
            ctx.inc(ins, "V")
        V_(_)
        m_y = ctx.cnt["V"]
        def _(sy, tch=tch, t0=t0, m_y=m_y):
            sy.wait_ge(ctx.s("V"), m_y)
            ins = sy.dma_start(out=yT[:, :, t0:t0 + 512], in_=yst[:])
            ctx.inc(ins, "DS", 16)
        Y_(_)
        m_ydma = ctx.cnt["DS"]

    def _(sy):
        sy.wait_ge(ctx.s("DS"), ctx.cnt["DS"])
        sy.wait_ge(ctx.s("DG"), ctx.cnt["DG"])
        if ctx.cnt["G"]:
            sy.wait_ge(ctx.s("G"), ctx.cnt["G"])
        sy.nop()
    Y_(_)

    ctx._open.pop("block").__exit__(None, None, None)
    for k in reversed(list(ctx._open)):
        ctx._open.pop(k).__exit__(None, None, None)
    return nc, dbg_out


def make_in_maps(inputs):
    f = np.float32
    cstA = host_constants()
    cstB = host_constants_b()
    x = np.asarray(inputs["L0_attn_x"], dtype=f)
    base = {}
    base["A_Wq"] = np.ascontiguousarray(np.asarray(inputs["L0_attn_Wq"], f).reshape(NB, 128, D).transpose(1, 0, 2))
    base["A_Wk"] = np.ascontiguousarray(np.asarray(inputs["L0_attn_Wk"], f).reshape(NB, 128, D).transpose(1, 0, 2))
    base["A_Wv"] = np.ascontiguousarray(np.asarray(inputs["L0_attn_Wv"], f).reshape(NB, 128, D).transpose(1, 0, 2))
    base["A_Wg"] = np.ascontiguousarray(np.asarray(inputs["L0_attn_Wg"], f).reshape(NB, 128, D).transpose(1, 0, 2))
    HSLOT = [2 * (s % 8) + (s // 8) for s in range(16)]
    wa = np.asarray(inputs["L0_attn_Wa"], f)[:, HSLOT]
    base["A_Wa"] = np.ascontiguousarray(wa.reshape(NB, 128, HA).transpose(1, 0, 2))
    base["A_ba"] = np.ascontiguousarray(np.asarray(inputs["L0_attn_ba"], f)[HSLOT].reshape(HA, 1))
    base["A_Wo"] = np.ascontiguousarray(np.asarray(inputs["L0_attn_Wo"], f).reshape(NB, 128, D).transpose(1, 0, 2))
    base["A_gnw"] = np.ascontiguousarray(np.asarray(inputs["L0_attn_gn_w"], f).reshape(NB, 128).T)
    base["A_gnb"] = np.ascontiguousarray(np.asarray(inputs["L0_attn_gn_b"], f).reshape(NB, 128).T)
    wqkv = np.asarray(inputs["L0_ffn_wqkv"], f)
    qs = np.asarray(inputs["L0_ffn_q_scale"], f)
    ks = np.asarray(inputs["L0_ffn_k_scale"], f)
    wo = np.asarray(inputs["L0_ffn_wo"], f)
    lng = np.asarray(inputs["L0_ffn_ln_g"], f)
    lnb = np.asarray(inputs["L0_ffn_ln_b"], f)
    w1 = np.asarray(inputs["L0_ffn_w1"], f)
    b1 = np.asarray(inputs["L0_ffn_b1"], f)
    w2 = np.asarray(inputs["L0_ffn_w2"], f)
    b2 = np.asarray(inputs["L0_ffn_b2"], f)
    for l in range(2):
        base[f"B_wqkv{l}"] = np.ascontiguousarray(wqkv[l].reshape(NB, 128, 1536).transpose(1, 0, 2))
        base[f"B_qs{l}"] = np.ascontiguousarray(np.broadcast_to(qs[l], (128, DH)))
        base[f"B_ks{l}"] = np.ascontiguousarray(np.broadcast_to(ks[l], (128, DH)))
        base[f"B_wo{l}"] = np.ascontiguousarray(wo[l].reshape(4, 128, D).transpose(1, 0, 2))
        base[f"B_lng{l}"] = np.ascontiguousarray(lng[l].reshape(NB, 128).T)
        base[f"B_lnb{l}"] = np.ascontiguousarray(lnb[l].reshape(NB, 128).T)
        w1p = np.zeros((D, 2 * HIDP), f)
        w1p[:, 0:HID] = w1[l][:, 0:HID]
        w1p[:, HIDP:HIDP + HID] = w1[l][:, HID:2 * HID]
        base[f"B_w1{l}"] = np.ascontiguousarray(w1p.reshape(NB, 128, 2 * HIDP).transpose(1, 0, 2))
        b1a = np.zeros((HIDP,), f); b1a[0:HID] = b1[l][0:HID]
        b1g = np.zeros((HIDP,), f); b1g[0:HID] = b1[l][HID:2 * HID]
        base[f"B_b1a{l}"] = np.ascontiguousarray(b1a.reshape(22, 128).T)
        base[f"B_b1g{l}"] = np.ascontiguousarray(b1g.reshape(22, 128).T)
        w2p = np.zeros((HIDP, D), f)
        w2p[0:HID] = w2[l]
        base[f"B_w2{l}"] = np.ascontiguousarray(w2p.reshape(22, 128, D).transpose(1, 0, 2))
        base[f"B_b2{l}"] = np.ascontiguousarray(b2[l].reshape(NB, 128).T)
    for k, v in {**cstA, **cstB}.items():
        base["c_" + k] = v
    maps = []
    for b in range(8):
        m = dict(base)
        m["xT"] = np.ascontiguousarray(x[b].T.reshape(NB, 128, T).transpose(1, 0, 2))
        maps.append(m)
    return maps


def kernel(**inputs):
    nc, _ = build_program()
    in_maps = make_in_maps(inputs)
    res = run_bass_kernel_spmd(nc, in_maps, core_ids=list(range(8)))
    out = np.stack([r["yT"].transpose(1, 0, 2).reshape(D, T).T for r in res.results], axis=0)
    return np.ascontiguousarray(out.astype(np.float32))


# version 13
# speedup vs baseline: 1.5432x; 1.0090x over previous
import numpy as np
import ml_dtypes
import concourse.bass as bass
import concourse.mybir as mybir
from concourse.bass_utils import run_bass_kernel_spmd

F32 = mybir.dt.float32
BF16 = mybir.dt.bfloat16
AF = mybir.ActivationFunctionType
ALU = mybir.AluOpType
AX = mybir.AxisListType

T = 2048
D = 1024
NB = 8
HA = 16
DH = 64
C = 128
NC = T // C
HB = 8
WIN = 512
NW = T // WIN
HID = 2730
HIDP = 2816


def host_constants():
    c = {}
    c["I128f"] = np.eye(128, dtype=np.float32)
    c["I128b"] = np.eye(128, dtype=ml_dtypes.bfloat16)
    c["ones128"] = np.ones((128, 128), dtype=ml_dtypes.bfloat16)
    ed = (np.arange(128)[:, None] // 64 == np.arange(128)[None, :] // 64)
    c["EDIAG"] = ed.astype(ml_dtypes.bfloat16)
    HSLOT = [2 * (s % 8) + (s // 8) for s in range(16)]
    sel = np.zeros((16, 1024), dtype=np.float32)
    for s in range(16):
        h = HSLOT[s]
        sel[s, 64 * h:64 * h + 64] = 1.0
    c["SEL16"] = sel.astype(ml_dtypes.bfloat16)
    selq = np.zeros((16, 1024), dtype=np.float32)
    for s in range(16):
        h = HSLOT[s]
        pair, par = h // 2, h % 2
        selq[s, 128 * pair + 64 * par:128 * pair + 64 * par + 64] = 1.0
    c["SELQ"] = selq.astype(ml_dtypes.bfloat16)
    return c


def host_constants_b():
    c = {}
    half = np.arange(0, DH, 2, dtype=np.float64)
    inv_freq = 1.0 / 10000.0 ** (half / DH)
    t = np.arange(T, dtype=np.float64)
    freqs = t[:, None] * inv_freq[None, :]
    freqs = np.concatenate([freqs, freqs], axis=-1)
    base = (half + 0.4 * DH) / (1.4 * DH)
    power = (t - T // 2) / float(WIN // 2)
    scale = base[None, :] ** power[:, None]
    scale = np.concatenate([scale, scale], axis=-1)
    cos, sin = np.cos(freqs), np.sin(freqs)
    sgn = np.ones(DH); sgn[:DH // 2] = -1.0
    QK = 8.0
    def tm(a):
        return np.ascontiguousarray(
            a.reshape(NC, 128, DH).transpose(1, 0, 2)).astype(ml_dtypes.bfloat16)
    c["CSQ8"] = tm(cos * scale * QK)
    c["SNQ8"] = tm(sin * scale * QK * sgn[None, :])
    c["CSK1"] = tm(cos / scale)
    c["SNK1"] = tm(sin / scale * sgn[None, :])
    i = np.arange(128)[:, None]; j = np.arange(128)[None, :]
    m = np.zeros((128, 2, 128), dtype=np.float32)
    m[:, 0, :] = np.where(j < i, -1e30, 0.0)
    m[:, 1, :] = np.where(j > i, -1e30, 0.0)
    c["P1MASK"] = m
    m2 = np.zeros((128, 2, 128), dtype=np.float32)
    m2[:, 0, :] = np.where(i < j, -1e30, 0.0)
    m2[:, 1, :] = np.where(i > j, -1e30, 0.0)
    c["P2ADD"] = m2
    return c


class Ctx:
    def __init__(self, nc):
        self.nc = nc
        self._sems = {}
        self.cnt = {}
        self._open = {}

    def sem(self, name):
        g = self.nc.semaphore(name)
        h = g.__enter__()
        self._open["sem_" + name] = g
        self._sems[name] = h
        self.cnt[name] = 0
        return h

    def s(self, name):
        return self._sems[name]

    def inc(self, ins, name, n=1):
        self.cnt[name] += n
        ins.then_inc(self._sems[name], n)
        return self.cnt[name]

    def alloc(self, name, shape, dtype, psum=False, side="left"):
        if psum:
            g = self.nc.psum_tensor(name, list(shape), dtype)
        else:
            g = self.nc.sbuf_tensor(name, list(shape), dtype, side=side)
        h = g.__enter__()
        self._open[name] = g
        return h

    def free(self, *names):
        for n in names:
            self._open.pop(n).__exit__(None, None, None)


def build_program(dbg_names=(), race_check=False, stop_after=None, n_chunks=NC):
    nc = bass.Bass("TRN2", target_bir_lowering=False,
                   detect_race_conditions=race_check)
    cstA = host_constants()
    cstB = host_constants_b()

    def din(name, shape, dtype=F32):
        return nc.dram_tensor(name, list(shape), dtype, kind="ExternalInput").ap()

    xT = din("xT", [128, NB, T])
    A_Wq = din("A_Wq", [128, NB, D]); A_Wk = din("A_Wk", [128, NB, D])
    A_Wv = din("A_Wv", [128, NB, D]); A_Wg = din("A_Wg", [128, NB, D])
    A_Wa = din("A_Wa", [128, NB, HA]); A_ba = din("A_ba", [HA, 1])
    A_Wo = din("A_Wo", [128, NB, D])
    A_gnw = din("A_gnw", [128, NB]); A_gnb = din("A_gnb", [128, NB])
    B_wqkv = [din(f"B_wqkv{l}", [128, NB, 1536]) for l in range(2)]
    B_qs = [din(f"B_qs{l}", [128, DH]) for l in range(2)]
    B_ks = [din(f"B_ks{l}", [128, DH]) for l in range(2)]
    B_wo = [din(f"B_wo{l}", [128, 4, D]) for l in range(2)]
    B_lng = [din(f"B_lng{l}", [128, NB]) for l in range(2)]
    B_lnb = [din(f"B_lnb{l}", [128, NB]) for l in range(2)]
    B_w1 = [din(f"B_w1{l}", [128, NB, 2 * HIDP]) for l in range(2)]
    B_b1a = [din(f"B_b1a{l}", [128, 22]) for l in range(2)]
    B_b1g = [din(f"B_b1g{l}", [128, 22]) for l in range(2)]
    B_w2 = [din(f"B_w2{l}", [128, 22, D]) for l in range(2)]
    B_b2 = [din(f"B_b2{l}", [128, NB]) for l in range(2)]
    CONST = {}
    for k, v in {**cstA, **cstB}.items():
        CONST[k] = din("c_" + k, v.shape, F32 if v.dtype == np.float32 else BF16)

    yT = nc.dram_tensor("yT", [128, NB, T], F32, kind="ExternalOutput").ap()
    dbg_out = {}

    x_spill = nc.dram_tensor("x_spill", [128, NB, T], F32).ap()
    s_spill = nc.dram_tensor("s_spill", [HA, T + 1], F32).ap()

    ctx = Ctx(nc)
    for s in ("T", "V", "S", "DS", "DG", "G"):
        ctx.sem(s)

    CI = {}
    for k in cstA:
        v = cstA[k]
        CI[k] = ctx.alloc("k_" + k, v.shape, F32 if v.dtype == np.float32 else BF16)

    cpool = ctx.alloc("cpool", [128, 4], F32)
    CPOOL_VALS = [1e-5, 1.0, -1.0, 1e-24]
    for i, val in enumerate(CPOOL_VALS):
        nc.const_aps.register(F32, val, cpool[:, i:i + 1])

    x_sb = ctx.alloc("x_sb", [128, NB, T + 1], F32, side="right")
    gnw = ctx.alloc("gnw", [128, NB], F32); gnb = ctx.alloc("gnb", [128, NB], F32)
    mu_b = ctx.alloc("mu_b", [128, 512], F32)
    mu2_b = ctx.alloc("mu2_b", [128, 512], F32)
    var_b = ctx.alloc("var_b", [128, 512], F32)
    rb_b = ctx.alloc("rb_b", [128, 512], F32)
    sd_b = mu2_b
    tsub = ctx.alloc("tsub", [128, 512], F32)

    gamhat = ctx.alloc("gamhat", [128, NB], F32)
    bethat = ctx.alloc("bethat", [128, NB], F32)
    ln_scr = ctx.alloc("ln_scr", [128, NB, 512], BF16)
    xn = ctx.alloc("xn", [128, NB, T], BF16)
    Wa = ctx.alloc("Wa", [128, NB, HA], BF16)
    ba = ctx.alloc("ba", [HA, 1], F32)
    Wq = ctx.alloc("Wq", [128, NB, D], BF16)
    Wk = ctx.alloc("Wk", [128, NB, D], BF16)
    o_gla = ctx.alloc("o_gla", [128, NB, T], BF16)
    opart = ctx.alloc("opart", [128, NB, NC], F32)
    opart2 = ctx.alloc("opart2", [128, NB, NC], F32)
    Wv = ctx.alloc("Wv", [128, NB, D], BF16)

    psA = ctx.alloc("psA", [128, 1024], F32, psum=True)
    psB = ctx.alloc("psB", [128, 1024], F32, psum=True)
    psC = ctx.alloc("psC", [128, 512], F32, psum=True)
    psD = ctx.alloc("psD", [128, 512], F32, psum=True)
    psE = ctx.alloc("psE", [128, 512], F32, psum=True)
    psF = ctx.alloc("psF", [128, 512], F32, psum=True)

    blkg = nc.Block()
    block = blkg.__enter__()
    ctx._open["block"] = blkg

    LASTMM = [None]
    def MM(t, *a, **kw):
        ins = t.matmul(*a, **kw)
        LASTMM[0] = ins
        return ins

    def SYNC(eng, sem, ins):
        m = ctx.inc(ins, sem)
        eng.wait_ge(ctx.s(sem), m)

    T_, V_, S_, G_, Y_ = block.tensor, block.vector, block.scalar, block.gpsimd, block.sync

    def dbg(name, shape, dtype, src_ap, waits):
        if name not in dbg_names:
            return
        dbg_out[name] = nc.dram_tensor("dbg_" + name, list(shape), dtype,
                                       kind="ExternalOutput").ap()
        def _(sy):
            for se, vv in waits:
                sy.wait_ge(ctx.s(se), vv)
            ins = sy.dma_start(out=dbg_out[name], in_=src_ap)
            ctx.inc(ins, "DS", 16)
        Y_(_)


    def finish_early():
        def _(v):
            v.wait_ge(ctx.s("T"), ctx.cnt["T"])
            v.wait_ge(ctx.s("S"), ctx.cnt["S"])
            ins = v.memset(mu_b[:], 0.0)
            ctx.inc(ins, "V")
        V_(_)
        mz = ctx.cnt["V"]
        for tch in range(4):
            def _(sy, tch=tch):
                sy.wait_ge(ctx.s("V"), mz)
                for kb in range(NB):
                    ins = sy.dma_start(out=yT[:, kb, 512 * tch:512 * tch + 512],
                                       in_=mu_b[:])
                    ctx.inc(ins, "DS", 16)
            Y_(_)
        def _(sy):
            sy.wait_ge(ctx.s("DS"), ctx.cnt["DS"])
            sy.wait_ge(ctx.s("DG"), ctx.cnt["DG"])
            if ctx.cnt["G"]:
                sy.wait_ge(ctx.s("G"), ctx.cnt["G"])
            sy.nop()
        Y_(_)
        ctx._open.pop("block").__exit__(None, None, None)
        for k in reversed(list(ctx._open)):
            ctx._open.pop(k).__exit__(None, None, None)
        return nc, dbg_out

    def _(g):
        for k in cstA:
            ctx.inc(g.dma_start(out=CI[k][:], in_=CONST[k]), "DG", 16)
        ctx.inc(g.dma_start(out=Wq[:], in_=A_Wq), "DG", 16)
        ctx.inc(g.dma_start(out=Wk[:], in_=A_Wk), "DG", 16)
        ctx.inc(g.dma_start(out=Wv[:], in_=A_Wv), "DG", 16)
        ctx.inc(g.dma_start(out=Wa[:], in_=A_Wa), "DG", 16)
        ctx.inc(g.dma_start(out=ba[:], in_=A_ba), "DG", 16)
        ctx.inc(g.dma_start(out=gnw[:], in_=A_gnw), "DG", 16)
        ctx.inc(g.dma_start(out=gnb[:], in_=A_gnb), "DG", 16)
    G_(_)
    m_loads = ctx.cnt["DG"]

    def _(sy):
        ins = sy.dma_start(out=x_sb[:, :, 1:T + 1], in_=xT)
        ctx.inc(ins, "DS", 16)
    Y_(_)
    m_xin = ctx.cnt["DS"]

    def _(v):
        v.memset(x_sb[:, :, 0:1], 0.0)
        for i, val in enumerate(CPOOL_VALS):
            ins = v.memset(cpool[:, i:i + 1], val)
        ctx.inc(ins, "V")
    V_(_)
    m_cpool = ctx.cnt["V"]
    def _(s):
        s.wait_ge(ctx.s("V"), m_cpool)
        s.nop()
    S_(_)

    ln_state = {"pslast": 0}

    def emit_ln(src3, out3, waits, shift=False, out_f32=False, pstats=None,
                waits_tch=None, post_tch=None, scratch_side="right",
                scratch=None):
        ln_state["n"] = ln_state.get("n", 0) + 1
        nm = str(ln_state["n"])
        psS1, psS2 = pstats if pstats is not None else (psC[:], psD[:])
        xbfs = scratch if scratch is not None else \
            ctx.alloc("ln_xbfs" + nm, [128, NB, 512], BF16, side=scratch_side)
        m_apply = 0
        marks = []
        for tch in range(4):
            t0 = 512 * tch
            wt = list(waits) + (list(waits_tch[tch]) if waits_tch else [])
            def _(s, tch=tch, t0=t0, wt=wt):
                for se, vv in wt:
                    s.wait_ge(ctx.s(se), vv)
                if ln_state["pslast"]:
                    s.wait_ge(ctx.s("V"), ln_state["pslast"])
                for kb in range(NB):
                    off = 0 if (shift and kb >= 4) else 1
                    src = src3[:, kb, off + t0: off + t0 + 512]
                    ins = s.activation(xbfs[:, kb, :], src, AF.Copy)
                ctx.inc(ins, "S")
            S_(_)
            mcast = ctx.cnt["S"]
            def _(t, tch=tch, wt=wt):
                t.wait_ge(ctx.s("S"), mcast)
                for se, vv in wt:
                    t.wait_ge(ctx.s(se), vv)
                if ln_state["pslast"]:
                    t.wait_ge(ctx.s("V"), ln_state["pslast"])
                for kb in range(NB):
                    MM(t, psS1, CI["ones128"][:], xbfs[:, kb, :],
                             start=(kb == 0), stop=(kb == NB - 1))
                ctx.inc(LASTMM[0], "T")
            T_(_)
            msum = ctx.cnt["T"]
            def _(s, tch=tch, t0=t0):
                s.wait_ge(ctx.s("T"), msum)
                for kb in range(NB):
                    off = 0 if (shift and kb >= 4) else 1
                    src = src3[:, kb, off + t0: off + t0 + 512]
                    ins = s.activation(xbfs[:, kb, :], src, AF.Square)
                ctx.inc(ins, "S")
            S_(_)
            msq = ctx.cnt["S"]
            def _(t, tch=tch):
                t.wait_ge(ctx.s("S"), msq)
                for kb in range(NB):
                    MM(t, psS2, CI["ones128"][:], xbfs[:, kb, :],
                             start=(kb == 0), stop=(kb == NB - 1))
                ctx.inc(LASTMM[0], "T")
            T_(_)
            mstat = ctx.cnt["T"]
            def _(v, tch=tch):
                v.wait_ge(ctx.s("T"), mstat)
                SYNC(v, "V", v.tensor_scalar(mu_b[:], psS1, 1.0 / D, None, ALU.mult))
                SYNC(v, "V", v.tensor_mul(mu2_b[:], mu_b[:], mu_b[:]))
                ins = v.scalar_tensor_tensor(var_b[:], psS2, 1.0 / D, mu2_b[:],
                                             ALU.mult, ALU.subtract)
                ctx.inc(ins, "V")
            V_(_)
            mvar = ctx.cnt["V"]
            def _(s, tch=tch):
                s.wait_ge(ctx.s("V"), mvar)
                s.activation(sd_b[:], var_b[:], AF.Ln, bias=1e-5)
                ins = s.activation(rb_b[:], sd_b[:], AF.Exp, scale=-0.5)
                ctx.inc(ins, "S")
            S_(_)
            msd = ctx.cnt["S"]
            def _(v, tch=tch, t0=t0):
                v.wait_ge(ctx.s("S"), msd)
                tb = [tsub, var_b]
                for kb2 in range(0, NB, 2):
                    for j in range(2):
                        kb = kb2 + j
                        off = 0 if (shift and kb >= 4) else 1
                        v.tensor_sub(tb[j][:], src3[:, kb, off + t0: off + t0 + 512],
                                     mu_b[:])
                    for j in range(2):
                        kb = kb2 + j
                        ins = v.tensor_mul(out3[:, kb, t0:t0 + 512], tb[j][:], rb_b[:])
                ctx.inc(ins, "V")
            V_(_)
            ln_state["pslast"] = ctx.cnt["V"]
            m_apply = ctx.cnt["V"]
            marks.append(m_apply)
            if post_tch is not None:
                post_tch(tch, m_apply)
        if scratch is None:
            ctx.free("ln_xbfs" + nm)
        return m_apply, marks

    m_ln1, ln1_marks = emit_ln(x_sb[:], xn[:],
                               waits=[("DS", m_xin), ("DG", m_loads)],
                               scratch=ln_scr)

    def _(sy):
        sy.wait_ge(ctx.s("V"), m_ln1)
        ins = sy.dma_start(out=x_spill, in_=x_sb[:, :, 1:T + 1])
        ctx.inc(ins, "DS", 16)
    Y_(_)
    m_xspill = ctx.cnt["DS"]
    ctx.free("x_sb")
    dbg("xn", [128, NB, T], BF16, xn[:], [("V", m_ln1)])
    if stop_after == "ln1":
        return finish_early()

    S_sb = ctx.alloc("S_sb", [HA, T + 1], F32, side="right")
    qdF = ctx.alloc("qdF", [HA, T], BF16, side="right")
    kdF = ctx.alloc("kdF", [HA, T], BF16, side="right")
    sp_sb = ctx.alloc("sp_sb", [HA, T], F32, side="right")
    m_sp = 0
    def _(v):
        v.wait_ge(ctx.s("DS"), m_xspill)
        ins = v.memset(S_sb[:, 0:1], 0.0)
        ctx.inc(ins, "V")
    V_(_)
    for tch in range(4):
        def _(t, tch=tch):
            t.wait_ge(ctx.s("V"), ln1_marks[tch])
            if tch == 0:
                t.wait_ge(ctx.s("DG"), m_loads)
            if m_sp:
                t.wait_ge(ctx.s("S"), m_sp)
            for kb in range(NB):
                MM(t, psE[0:HA, :], Wa[:, kb, :],
                         xn[:, kb, 512 * tch:512 * tch + 512],
                         start=(kb == 0), stop=(kb == NB - 1))
            ctx.inc(LASTMM[0], "T")
        T_(_)
        mla = ctx.cnt["T"]
        def _(s, tch=tch, mla=mla):
            s.wait_ge(ctx.s("T"), mla)
            s.wait_ge(ctx.s("DS"), m_xspill)
            SYNC(s, "S", s.activation(sd_b[0:HA, :], psE[0:HA, :], AF.Exp, bias=ba[:]))
            ins = s.activation(sp_sb[:, 512 * tch:512 * tch + 512], sd_b[0:HA, :],
                               AF.Ln, bias=1.0)
            ctx.inc(ins, "S")
        S_(_)
        m_sp = ctx.cnt["S"]

    qkE = ctx.alloc("qkE", [HA, T], F32, side="right")
    def _(v):
        v.wait_ge(ctx.s("V"), ctx.cnt["V"])
        v.wait_ge(ctx.s("S"), m_sp)
        ins = v.tensor_tensor_scan(S_sb[:, 1:T + 1], sp_sb[:], sp_sb[:], 0.0,
                                   ALU.add, ALU.bypass)
        mscan = ctx.inc(ins, "V")
        v.wait_ge(ctx.s("V"), mscan)
        for c in range(NC):
            ins = v.tensor_scalar(qkE[:, C * c:C * c + C],
                                  S_sb[:, 1 + C * c: 1 + C * c + C],
                                  S_sb[:, C * c:C * c + 1], None, ALU.subtract)
        ctx.inc(ins, "V")
    V_(_)
    m_qde = ctx.cnt["V"]
    def _(s):
        s.wait_ge(ctx.s("V"), m_qde)
        ins = s.activation(qdF[:], qkE[:], AF.Exp, scale=-1.0)
        ctx.inc(ins, "S")
    S_(_)
    m_qdf = ctx.cnt["S"]
    def _(v):
        v.wait_ge(ctx.s("S"), m_qdf)
        for c in range(NC):
            ins = v.tensor_scalar(qkE[:, C * c:C * c + C],
                                  S_sb[:, 1 + C * c: 1 + C * c + C],
                                  S_sb[:, C * (c + 1):C * (c + 1) + 1], None,
                                  ALU.subtract)
        ctx.inc(ins, "V")
    V_(_)
    m_kde = ctx.cnt["V"]
    def _(s):
        s.wait_ge(ctx.s("V"), m_kde)
        ins = s.activation(kdF[:], qkE[:], AF.Exp)
        ctx.inc(ins, "S")
    S_(_)
    m_kdf = ctx.cnt["S"]
    def _(sy):
        sy.wait_ge(ctx.s("V"), m_qde)
        ins = sy.dma_start(out=s_spill, in_=S_sb[:])
        ctx.inc(ins, "DS", 16)
    Y_(_)
    m_sdram = ctx.cnt["DS"]
    dbg("S", [HA, T + 1], F32, S_sb[:], [("V", m_qde)])

    ctx.free("qkE", "sp_sb")
    if stop_after == "a2":
        return finish_early()

    stok = ctx.alloc("stok", [128, HA], F32, side="right")
    SRb = ctx.alloc("SRb", [128, HA, C], F32, side="right")
    Dst = ctx.alloc("Dst", [128, HA, C], BF16, side="right")
    vcb = [ctx.alloc(f"vcb{i}", [128, D], BF16, side="right") for i in range(2)]
    kcb = [ctx.alloc(f"kcb{i}", [128, D], BF16, side="right") for i in range(2)]
    k2b = [ctx.alloc(f"k2b{i}", [128, D], BF16, side="right") for i in range(2)]
    qcb = [ctx.alloc(f"qcb{i}", [128, NB, C], BF16, side="right") for i in range(2)]
    q2b = [ctx.alloc(f"q2b{i}", [128, NB, C], BF16, side="right") for i in range(2)]
    ktb = [ctx.alloc(f"ktb{i}", [128, NB, C], BF16, side="right") for i in range(2)]
    a2b = ctx.alloc("a2b", [128, HA, C], BF16, side="right")
    MEV = ctx.alloc("MEV", [128, NB, DH], F32, side="right")
    MOD = ctx.alloc("MOD", [128, NB, DH], F32, side="right")
    MEVb = ctx.alloc("MEVb", [128, NB, DH], BF16, side="right")
    MODb = ctx.alloc("MODb", [128, NB, DH], BF16, side="right")
    gmb = ctx.alloc("gmb", [128, NB], F32, side="right")
    osq = ctx.alloc("osq", [128, D], BF16, side="right")
    if stop_after == "calloc":
        return finish_early()

    def _(v):
        v.memset(MEV[:], 0.0)
        ins = v.memset(MOD[:], 0.0)
        ctx.inc(ins, "V")
    V_(_)
    m_msinit = ctx.cnt["V"]
    def _(v):
        v.tensor_copy(MEVb[:], MEV[:])
        ins = v.tensor_copy(MODb[:], MOD[:])
        ctx.inc(ins, "V")
    V_(_)
    if stop_after == "cm":
        return finish_early()

    m_G = 0
    m_a2rdy = 0
    m_psB_free = 0
    m_psE_sfree = 0
    m_Mup = 0
    m_Mb = ctx.cnt["V"]
    m_ost = 0
    m_opred = 0
    m_v = 0

    for c in range(n_chunks):
        ib = c % 2
        cc = slice(C * c, C * c + C)

        def _(sy, c=c):
            sy.wait_ge(ctx.s("DS"), m_sdram)
            if c == 0:
                sy.wait_ge(ctx.s("V"), m_kde)
                sy.wait_ge(ctx.s("S"), m_kdf)
            if m_G:
                sy.wait_ge(ctx.s("V"), m_G)
                sy.wait_ge(ctx.s("S"), m_Dexp)
            src = s_spill[:, 1 + C * c: 1 + C * c + C].unsqueeze(0).partition_broadcast(128)
            ins = sy.dma_start(out=SRb[:], in_=src)
            ctx.inc(ins, "DS", 16)
        Y_(_)
        m_sr = ctx.cnt["DS"]
        if stop_after == "csr" and c == n_chunks - 1:
            return finish_early()

        def _(t, c=c, cc=cc):
            if c < 4:
                t.wait_ge(ctx.s("V"), ln1_marks[c // 4])
            if m_a2rdy:
                t.wait_ge(ctx.s("V"), m_a2rdy)
            for n in range(2):
                for kb in range(NB):
                    MM(t, psA[:, 512 * n:512 * n + 512], xn[:, kb, cc],
                             Wv[:, kb, 512 * n:512 * n + 512],
                             start=(kb == 0), stop=(kb == NB - 1))
            ctx.inc(LASTMM[0], "T")
            if stop_after == "cw1a":
                return
            if c == 0:
                t.wait_ge(ctx.s("S"), m_kdf)
            if m_psE_sfree:
                t.wait_ge(ctx.s("V"), m_psE_sfree)
            MM(t, psE[:, 0:HA], S_sb[:, 1 + C * c:1 + C * c + C],
                     CI["I128f"][0:HA, 0:HA], start=True, stop=True)
            ctx.inc(LASTMM[0], "T")
        T_(_)
        m_w1 = ctx.cnt["T"]
        if stop_after == "cw1a" and c == n_chunks - 1:
            return finish_early()

        def _(v, c=c):
            v.wait_ge(ctx.s("T"), m_w1)
            v.tensor_copy(vcb[ib][:], psA[:])
            ins = v.tensor_copy(stok[:], psE[:, 0:HA])
            ctx.inc(ins, "V")
        V_(_)
        m_v = ctx.cnt["V"]
        if stop_after == "cw1" and c == n_chunks - 1:
            return finish_early()

        def _(t, c=c, cc=cc):
            t.wait_ge(ctx.s("V"), m_v)
            for n in range(2):
                for kb in range(NB):
                    MM(t, psA[:, 512 * n:512 * n + 512], xn[:, kb, cc],
                             Wk[:, kb, 512 * n:512 * n + 512],
                             start=(kb == 0), stop=(kb == NB - 1))
            ctx.inc(LASTMM[0], "T")
            if m_ost:
                t.wait_ge(ctx.s("V"), m_opred)
            for n in range(2):
                MM(t, psB[:, 512 * n:512 * n + 512], kdF[:, cc],
                         CI["SEL16"][:, 512 * n:512 * n + 512], start=True, stop=True)
            ctx.inc(LASTMM[0], "T")
        T_(_)
        m_w2 = ctx.cnt["T"]

        def _(s, c=c):
            s.wait_ge(ctx.s("T"), m_w2)
            ins = s.activation(kcb[ib][:], psA[:], AF.Silu)
            ctx.inc(ins, "S")
        S_(_)
        m_ksilu = ctx.cnt["S"]

        def _(v, c=c):
            v.wait_ge(ctx.s("S"), m_ksilu)
            ins = v.tensor_mul(k2b[ib][:], kcb[ib][:], psB[:])
            ctx.inc(ins, "V")
        V_(_)
        m_k2 = ctx.cnt["V"]
        if stop_after == "cw2" and c == n_chunks - 1:
            return finish_early()

        def _(t, c=c, cc=cc):
            t.wait_ge(ctx.s("S"), m_ksilu)
            for i in range(NB):
                for kb in range(NB):
                    MM(t, psA[:, 128 * i:128 * i + 128],
                             Wq[:, kb, 128 * i:128 * i + 128], xn[:, kb, cc],
                             start=(kb == 0), stop=(kb == NB - 1))
            ctx.inc(LASTMM[0], "T")
            t.wait_ge(ctx.s("V"), m_k2)
            for pair in range(8):
                MM(t, psB[:, 128 * pair:128 * pair + 128],
                         CI["SELQ"][:, 128 * pair:128 * pair + 128],
                         qdF[:, cc], start=True, stop=True)
            ctx.inc(LASTMM[0], "T")
        T_(_)
        m_w3 = ctx.cnt["T"]

        def _(s, c=c):
            s.wait_ge(ctx.s("T"), m_w3)
            ins = s.activation(qcb[ib][:], psA[:], AF.Silu)
            ctx.inc(ins, "S")
        S_(_)
        m_qsilu = ctx.cnt["S"]

        def _(v, c=c):
            v.wait_ge(ctx.s("S"), m_qsilu)
            ins = v.tensor_mul(q2b[ib][:], qcb[ib][:], psB[:])
            ctx.inc(ins, "V")
        V_(_)
        m_q2 = ctx.cnt["V"]
        if stop_after == "cw3" and c == n_chunks - 1:
            return finish_early()

        def _(t, c=c):
            t.wait_ge(ctx.s("S"), m_qsilu)
            for pair in range(8):
                MM(t, psA[:, 128 * pair:128 * pair + 128],
                         kcb[ib][:, 128 * pair:128 * pair + 128], CI["I128b"][:],
                         start=True, stop=True)
            ctx.inc(LASTMM[0], "T")
        T_(_)
        m_w4 = ctx.cnt["T"]
        def _(v, c=c):
            v.wait_ge(ctx.s("T"), m_w4)
            ins = v.tensor_copy(ktb[ib][:], psA[:])
            ctx.inc(ins, "V")
        V_(_)
        m_kt = ctx.cnt["V"]
        if stop_after == "cw4" and c == n_chunks - 1:
            return finish_early()

        def _(v, c=c):
            v.wait_ge(ctx.s("DS"), m_sr)
            if m_a2rdy:
                v.wait_ge(ctx.s("S"), m_Dexp)
            stok_bc = stok[:].unsqueeze(2).broadcast_to([128, HA, C])
            ins = v.tensor_sub(SRb[:], stok_bc, SRb[:])
            mend = ctx.inc(ins, "V")
            v.wait_ge(ctx.s("V"), mend)
            ins = v.scalar_tensor_tensor(SRb[:], SRb[:], -1e30, SRb[:],
                                         ALU.mult, ALU.min)
            ctx.inc(ins, "V")
        V_(_)
        m_G = ctx.cnt["V"]
        if c == 0:
            dbg("G0", [128, HA, C], F32, SRb[:], [("V", m_G)])
            dbg("stok0", [128, HA], F32, stok[:], [("V", m_G)])

        def _(s, c=c):
            s.wait_ge(ctx.s("V"), m_G)
            if c == 0 and "E0" in dbg_names:
                s.wait_ge(ctx.s("DS"), ctx.cnt["DS"])
            if m_a2rdy:
                s.wait_ge(ctx.s("V"), m_a2rdy)
            ins = s.activation(Dst[:], SRb[:], AF.Exp)
            ctx.inc(ins, "S")
        S_(_)
        m_Dexp = ctx.cnt["S"]
        if stop_after == "cdecay" and c == n_chunks - 1:
            return finish_early()

        def _(t, c=c):
            t.wait_ge(ctx.s("V"), m_kt)
            t.wait_ge(ctx.s("V"), m_q2)
            npar = 1 if stop_after in ("cw5a", "cw5k") else 2
            if stop_after == "cw5m":
                npar = 2
            for par in range(npar):
                for h2 in range(8):
                    h = 2 * h2 + par
                    pair = h // 2
                    dst = psA if par == 0 else psB
                    if stop_after == "cw5k":
                        MM(t, dst[:, 128 * h2:128 * h2 + 128],
                                 ktb[ib][:, pair, :], qcb[ib][:, pair, :],
                                 start=True, stop=True)
                    else:
                        MM(t, dst[:, 128 * h2:128 * h2 + 128],
                                 ktb[ib][64 * par:64 * par + 64, pair, :],
                                 qcb[ib][64 * par:64 * par + 64, pair, :],
                                 start=True, stop=True)
                ctx.inc(LASTMM[0], "T")
            if stop_after in ("cw5a", "cw5k"):
                ctx.inc(LASTMM[0], "T")
        T_(_)
        m_A2m = ctx.cnt["T"]
        m_A1m = m_A2m - 1
        if stop_after in ("cw5a", "cw5k", "cw5m") and c == n_chunks - 1:
            return finish_early()

        def _(v, c=c):
            v.wait_ge(ctx.s("S"), m_Dexp)
            v.wait_ge(ctx.s("T"), m_A1m)
            v.tensor_mul(a2b[:, 0:8, :],
                         psA[:].rearrange("p (h q) -> p h q", h=8), Dst[:, 0:8, :])
            v.wait_ge(ctx.s("T"), m_A2m)
            ins = v.tensor_mul(a2b[:, 8:16, :],
                               psB[:].rearrange("p (h q) -> p h q", h=8),
                               Dst[:, 8:16, :])
            ctx.inc(ins, "V")
        V_(_)
        m_a2rdy = ctx.cnt["V"]
        if stop_after == "cw5" and c == n_chunks - 1:
            return finish_early()
        if c == 0:
            dbg("Dst0", [128, HA, C], BF16, Dst[:], [("S", m_Dexp)])
            dbg("a2b0", [128, HA, C], BF16, a2b[:], [("V", m_a2rdy)])
            dbg("q0", [128, NB, C], BF16, qcb[ib][:], [("S", m_qsilu)])
            dbg("kt0", [128, NB, C], BF16, ktb[ib][:], [("S", m_kt)])

        def _(t, c=c):
            t.wait_ge(ctx.s("V"), m_a2rdy)
            t.wait_ge(ctx.s("V"), m_Mb)
            skip_inter = bool(dbg_names) and "nointer0" in dbg_names and c == 0
            for h in range(HA):
                pair, par = h // 2, h % 2
                orgn = psB[64 * par:64 * par + 64, 128 * pair:128 * pair + 128]
                MM(t, orgn, vcb[ib][:, 64 * h:64 * h + 64],
                         a2b[:, 8 * (h % 2) + h // 2, :],
                         start=True, stop=skip_inter)
                if not skip_inter:
                    Mb_ = MEVb if par == 0 else MODb
                    MM(t, orgn, Mb_[:, pair, :], q2b[ib][:, pair, :],
                             start=False, stop=True)
            ctx.inc(LASTMM[0], "T")
            if m_Mup:
                t.wait_ge(ctx.s("V"), m_Mup)
            for h in range(HA):
                pair, par = h // 2, h % 2
                MM(t, psC[64 * par:64 * par + 64, 64 * pair:64 * pair + 64],
                         k2b[ib][:, 64 * h:64 * h + 64],
                         vcb[ib][:, 64 * h:64 * h + 64], start=True, stop=True)
            ctx.inc(LASTMM[0], "T")
            if m_Mup:
                t.wait_ge(ctx.s("V"), m_gmbev)
            for pair in range(8):
                MM(t, psF[:, pair:pair + 1],
                         CI["SELQ"][:, 128 * pair:128 * pair + 128],
                         qdF[:, C * c + C - 1:C * c + C], start=True, stop=True)
            ctx.inc(LASTMM[0], "T")
        T_(_)
        m_gam = ctx.cnt["T"]
        m_dM = m_gam - 1
        m_o = m_gam - 2

        def _(v, c=c):
            v.wait_ge(ctx.s("T"), m_gam)
            ins = v.tensor_copy(gmb[:], psF[:, 0:NB])
            ctx.inc(ins, "V")
        V_(_)
        m_gmbev = ctx.cnt["V"]
        if stop_after == "cw6" and c == n_chunks - 1:
            return finish_early()
        m_psE_sfree = m_v

        def _(v, c=c, cc=cc):
            v.wait_ge(ctx.s("T"), m_o)
            og3 = o_gla[:, :, cc]
            ins = v.tensor_copy(og3, psB[:].rearrange("p (h q) -> p h q", h=8))
            mend = ctx.inc(ins, "V")
            v.wait_ge(ctx.s("V"), mend)
            ins = v.tensor_mul(osq[:].rearrange("p (h q) -> p h q", h=8), og3, og3)
            mend = ctx.inc(ins, "V")
            v.wait_ge(ctx.s("V"), mend)
            v.reduce_sum(opart[:, :, c:c + 1],
                         psB[:].rearrange("p (h q) -> p h q", h=8), axis=AX.X)
            ins = v.reduce_sum(opart2[:, :, c:c + 1],
                               osq[:].rearrange("p (h q) -> p h q", h=8), axis=AX.X)
            ctx.inc(ins, "V")
        V_(_)
        m_ost = ctx.cnt["V"]
        m_opred = ctx.cnt["V"]

        def _(v, c=c):
            v.wait_ge(ctx.s("T"), m_dM)
            for h in range(HA):
                pair, par = h // 2, h % 2
                M_ = MEV if par == 0 else MOD
                rows = slice(64 * par, 64 * par + 64)
                ins = v.scalar_tensor_tensor(
                    M_[rows, pair, :], M_[rows, pair, :], gmb[rows, pair:pair + 1],
                    psC[rows, 64 * pair:64 * pair + 64], ALU.mult, ALU.add)
            ctx.inc(ins, "V")
        V_(_)
        m_Mup = ctx.cnt["V"]

        def _(v, c=c):
            v.tensor_copy(MEVb[:], MEV[:])
            ins = v.tensor_copy(MODb[:], MOD[:])
            ctx.inc(ins, "V")
        V_(_)
        m_Mb = ctx.cnt["V"]
        if stop_after == "cw7" and c == n_chunks - 1:
            return finish_early()

    m_chunkV = ctx.cnt["V"]
    m_chunkS = ctx.cnt["S"]
    m_chunkT = ctx.cnt["T"]
    if stop_after == "chunks":
        return finish_early()
    dbg("o_gla", [128, NB, T], BF16, o_gla[:], [("S", m_chunkS)])

    ostat = ctx.alloc("ostat", [128, NB], F32, side="right")
    ostat2 = ctx.alloc("ostat2", [128, NB], F32, side="right")
    ost_b = ctx.alloc("ost_b", [128, NB], BF16, side="right")
    ost2_b = ctx.alloc("ost2_b", [128, NB], BF16, side="right")

    def _(v):
        v.wait_ge(ctx.s("V"), m_chunkV)
        v.reduce_sum(ostat[:], opart[:], axis=AX.X)
        ins = v.reduce_sum(ostat2[:], opart2[:], axis=AX.X)
        ctx.inc(ins, "V")
    V_(_)
    m_op = ctx.cnt["V"]
    def _(s):
        s.wait_ge(ctx.s("V"), m_op)
        s.activation(ost_b[:], ostat[:], AF.Copy)
        ins = s.activation(ost2_b[:], ostat2[:], AF.Copy)
        ctx.inc(ins, "S")
    S_(_)
    m_opb = ctx.cnt["S"]
    def _(t):
        t.wait_ge(ctx.s("S"), m_opb)
        t.wait_ge(ctx.s("V"), m_gmbev)
        MM(t, psF[:, 0:NB], CI["EDIAG"][:], ost_b[:], start=True, stop=True)
        MM(t, psF[:, NB:2 * NB], CI["EDIAG"][:], ost2_b[:], start=True, stop=True)
        ctx.inc(LASTMM[0], "T")
    T_(_)
    m_gsum = ctx.cnt["T"]
    NEL = float(T * DH)
    def _(v):
        v.wait_ge(ctx.s("T"), m_gsum)
        SYNC(v, "V", v.tensor_scalar(ostat[:], psF[:, 0:NB], 1.0 / NEL, None, ALU.mult))
        SYNC(v, "V", v.tensor_mul(mu2_b[:, 0:NB], ostat[:], ostat[:]))
        ins = v.scalar_tensor_tensor(ostat2[:], psF[:, NB:2 * NB], 1.0 / NEL,
                                     mu2_b[:, 0:NB], ALU.mult, ALU.subtract)
        ctx.inc(ins, "V")
    V_(_)
    m_gvar = ctx.cnt["V"]
    def _(s):
        s.wait_ge(ctx.s("V"), m_gvar)
        s.activation(ostat2[:], ostat2[:], AF.Ln, bias=1e-5)
        ins = s.activation(mu2_b[:, 0:NB], ostat2[:], AF.Exp, scale=-0.5)
        ctx.inc(ins, "S")
    S_(_)
    m_gsd = ctx.cnt["S"]
    def _(v):
        v.wait_ge(ctx.s("S"), m_gsd)
        SYNC(v, "V", v.tensor_mul(gamhat[:], mu2_b[:, 0:NB], gnw[:]))
        SYNC(v, "V", v.tensor_mul(var_b[:, 0:NB], ostat[:], gamhat[:]))
        ins = v.tensor_sub(bethat[:], gnb[:], var_b[:, 0:NB])
        ctx.inc(ins, "V")
    V_(_)
    m_gn = ctx.cnt["V"]

    ctx.free("ost2_b", "ost_b", "ostat2", "ostat", "osq", "gmb", "MODb", "MEVb",
             "MOD", "MEV", "a2b", "ktb1", "ktb0", "q2b1", "q2b0", "qcb1", "qcb0",
             "k2b1", "k2b0", "kcb1", "kcb0", "vcb1", "vcb0", "Dst", "SRb", "stok",
             "kdF", "qdF", "S_sb")
    ctx.free("Wv", "opart2", "opart")
    if stop_after == "a4":
        return finish_early()

    Wg_sb = Wq
    Wo_sb = Wk
    def _(g):
        g.wait_ge(ctx.s("T"), m_chunkT)
        ctx.inc(g.dma_start(out=Wg_sb[:], in_=A_Wg), "DG", 16)
        ctx.inc(g.dma_start(out=Wo_sb[:], in_=A_Wo), "DG", 16)
    G_(_)
    m_wgo = ctx.cnt["DG"]

    x2_sb = ctx.alloc("x2_sb", [128, NB, T + 1], F32, side="right")
    def _(sy):
        sy.wait_ge(ctx.s("V"), m_chunkV)
        sy.wait_ge(ctx.s("S"), m_chunkS)
        ins = sy.dma_start(out=x2_sb[:, :, 1:T + 1], in_=x_spill)
        ctx.inc(ins, "DS", 16)
    Y_(_)
    m_xrel = ctx.cnt["DS"]
    def _(v):
        v.wait_ge(ctx.s("DS"), m_xrel)
        ins = v.memset(x2_sb[:, :, 0:1], 0.0)
        ctx.inc(ins, "V")
    V_(_)
    m_xrl2 = ctx.cnt["V"]

    gateb = ctx.alloc("gateb", [128, NB, 512], BF16, side="right")
    otld = ctx.alloc("otld", [128, NB, 512], BF16, side="right")
    m_gate_ev = 0
    m_otld = 0
    m_wo_half = [0, 0]
    m_xadd = 0
    for tch in range(4):
        t0 = 512 * tch
        for i2 in range(4):
            def _(t, tch=tch, t0=t0, i2=i2):
                if tch == 0 and i2 == 0:
                    t.wait_ge(ctx.s("DG"), m_wgo)
                    t.wait_ge(ctx.s("V"), m_gn)
                if m_gate_ev:
                    t.wait_ge(ctx.s("S"), m_gate_ev)
                if m_otld:
                    t.wait_ge(ctx.s("V"), m_otld)
                for j in range(2):
                    i = 2 * i2 + j
                    for kb in range(NB):
                        MM(t, psA[:, 512 * j:512 * j + 512],
                                 Wg_sb[:, kb, 128 * i:128 * i + 128],
                                 xn[:, kb, t0:t0 + 512],
                                 start=(kb == 0), stop=(kb == NB - 1))
                ctx.inc(LASTMM[0], "T")
            T_(_)
            m_gmm = ctx.cnt["T"]
            def _(s, i2=i2, m_gmm=m_gmm):
                s.wait_ge(ctx.s("T"), m_gmm)
                s.activation(gateb[:, 2 * i2, :], psA[:, 0:512], AF.Sigmoid)
                ins = s.activation(gateb[:, 2 * i2 + 1, :], psA[:, 512:1024],
                                   AF.Sigmoid)
                ctx.inc(ins, "S")
            S_(_)
            m_gate_ev = ctx.cnt["S"]

        def _(s, tch=tch, t0=t0):
            if tch == 0:
                s.wait_ge(ctx.s("V"), m_gn)
            if m_wo_half[1]:
                s.wait_ge(ctx.s("T"), m_wo_half[1])
            for pair in range(NB):
                ins = s.activation(otld[:, pair, :], o_gla[:, pair, t0:t0 + 512],
                                   AF.Identity, bias=bethat[:, pair:pair + 1],
                                   scale=gamhat[:, pair:pair + 1])
            ctx.inc(ins, "S")
        S_(_)
        m_oaff = ctx.cnt["S"]
        def _(v, tch=tch):
            v.wait_ge(ctx.s("S"), max(m_oaff, m_gate_ev))
            ins = v.tensor_mul(otld[:], otld[:], gateb[:])
            ctx.inc(ins, "V")
        V_(_)
        m_otld = ctx.cnt["V"]

        for dmb in range(NB):
            half = dmb % 2
            def _(t, dmb=dmb, half=half):
                t.wait_ge(ctx.s("V"), m_otld)
                if m_wo_half[half]:
                    t.wait_ge(ctx.s("V"), m_xadd_half[half])
                ps = psB[:, 512 * half:512 * half + 512]
                for kb in range(NB):
                    MM(t, ps, Wo_sb[:, kb, 128 * dmb:128 * dmb + 128],
                             otld[:, kb, :], start=(kb == 0), stop=(kb == NB - 1))
                ctx.inc(LASTMM[0], "T")
            T_(_)
            m_wo_half[half] = ctx.cnt["T"]
            def _(v, dmb=dmb, half=half, tch=tch, t0=t0, mwo=ctx.cnt["T"]):
                v.wait_ge(ctx.s("T"), mwo)
                ps = psB[:, 512 * half:512 * half + 512]
                if dmb == 0 and tch == 0:
                    v.wait_ge(ctx.s("V"), m_xrl2)
                ins = v.tensor_add(x2_sb[:, dmb, 1 + t0:1 + t0 + 512], ps,
                                   x2_sb[:, dmb, 1 + t0:1 + t0 + 512])
                ctx.inc(ins, "V")
            V_(_)
            if dmb == 0 and tch == 0:
                m_xadd_half = [0, 0]
                m_x2_tch = []
            m_xadd_half[half] = ctx.cnt["V"]
        m_x2_tch.append(ctx.cnt["V"])
    m_xnew = ctx.cnt["V"]

    ctx.free("otld", "gateb")
    ctx.free("o_gla", "Wk", "Wq", "ba", "Wa", "xn")
    dbg("xA", [128, NB, T], F32, x2_sb[:, :, 1:T + 1], [("V", m_xnew)])

    if stop_after == "a5":
        return finish_early()
    def _(sy):
        sy.wait_ge(ctx.s("V"), m_xnew)
        ins = sy.dma_start(out=x_spill, in_=x2_sb[:, :, 1:T + 1])
        ctx.inc(ins, "DS", 16)
    Y_(_)
    m_xnspill = ctx.cnt["DS"]

    xi_sb = ctx.alloc("xi_sb", [128, NB, T + 1], F32)
    def _(v):
        ins = v.memset(xi_sb[:, :, 0:1], 0.0)
        ctx.inc(ins, "V")
    V_(_)
    m_ln2, ln2_marks = emit_ln(x2_sb[:], xi_sb[:, :, 1:T + 1], waits=[],
                               waits_tch=[[("V", m_x2_tch[t])] for t in range(4)],
                               scratch=ln_scr)
    dbg("xi0", [128, NB, T], F32, xi_sb[:, :, 1:T + 1], [("V", m_ln2)])
    ctx.free("x2_sb")

    CIB = {}
    for k in cstB:
        v = cstB[k]
        CIB[k] = ctx.alloc("k_" + k, v.shape, F32 if v.dtype == np.float32 else BF16)

    def _(g):
        g.wait_ge(ctx.s("V"), m_ln2)
        for k in CIB:
            ctx.inc(g.dma_start(out=CIB[k][:], in_=CONST[k]), "DG", 16)
    G_(_)
    m_cib = ctx.cnt["DG"]

    def barrier():
        j = {k: ctx.cnt[k] for k in ("T", "V", "S", "DS", "DG", "G")}
        for emit, eng in ((T_, "T"), (V_, "V"), (S_, "S"), (Y_, "DS"), (G_, "DG")):
            def _(e, j=j):
                for k2, v2 in j.items():
                    if v2:
                        e.wait_ge(ctx.s(k2), v2)
                e.nop()
            emit(_)
        return j

    m_xi = m_ln2
    barrier()
    ctx.free("psF", "psE", "psD", "psC")
    sc2 = ctx.alloc("sc2", [128, 1024], F32, psum=True)
    sc3 = ctx.alloc("sc3", [128, 1024], F32, psum=True)
    SCB = [psA, psB, sc2, sc3]
    for l in range(2):
        barrier()
        xbfa = ctx.alloc(f"xbfa{l}", [128, NB, T + 1], BF16, side="right")
        wqkv_sb = ctx.alloc(f"wqkv{l}", [128, NB, 1536], BF16, side="right")
        wo_sb2 = ctx.alloc(f"wob{l}", [128, 4, D], BF16, side="right")
        qs_sb = ctx.alloc(f"qs{l}", [128, DH], BF16, side="right")
        ks_sb = ctx.alloc(f"ks{l}", [128, DH], BF16, side="right")
        v_sb = ln_scr
        ktr = ctx.alloc(f"ktr{l}", [65, HB, 1024], BF16, side="right")
        qtr = ctx.alloc(f"qtr{l}", [65, HB, 512], BF16, side="right")
        oT = ctx.alloc(f"oT{l}", [128, 4, 512], BF16, side="right")
        qraw = ctx.alloc(f"qraw{l}", [128, 512], BF16, side="right")
        kraw = ctx.alloc(f"kraw{l}", [128, 512], BF16, side="right")
        sqb = ctx.alloc(f"sqb{l}", [128, 1024], BF16, side="right")
        rnx = ctx.alloc(f"rnx{l}", [128, 1024], BF16, side="right")
        qp = ctx.alloc(f"qp{l}", [128, 512], BF16, side="right")
        kp = ctx.alloc(f"kp{l}", [128, 512], BF16, side="right")
        ss8 = ctx.alloc(f"ss8{l}", [128, 16], F32, side="right")
        sd8 = ctx.alloc(f"sd8{l}", [128, 16], F32, side="right")
        rn8 = ctx.alloc(f"rn8{l}", [128, 16], F32, side="right")
        nmall = ctx.alloc(f"nmall{l}", [128, HB, 4], F32, side="right")
        ssall = ctx.alloc(f"ssall{l}", [128, HB, 4], F32, side="right")
        lnsb = ctx.alloc(f"lnsb{l}", [128, HB, 4], F32, side="right")
        zf = ctx.alloc(f"zf{l}", [128, 32], F32, side="right")
        ztb = ctx.alloc(f"ztb{l}", [32, 128], BF16, side="right")
        ptil = ctx.alloc(f"ptil{l}", [128, 4, 640], BF16, side="right")

        def _(g, l=l):
            ctx.inc(g.dma_start(out=wqkv_sb[:], in_=B_wqkv[l]), "DG", 16)
            ctx.inc(g.dma_start(out=wo_sb2[:], in_=B_wo[l]), "DG", 16)
            ctx.inc(g.dma_start(out=qs_sb[:], in_=B_qs[l]), "DG", 16)
            ctx.inc(g.dma_start(out=ks_sb[:], in_=B_ks[l]), "DG", 16)
        G_(_)
        m_wl = ctx.cnt["DG"]

        def _(v, l=l):
            v.wait_ge(ctx.s("V"), m_xi)
            v.memset(xbfa[:, :, 0:1], 0.0)
            ins = v.memset(ktr[64:65, :, :], 1.0)
            ctx.inc(ins, "V")
        V_(_)
        m_xbset = ctx.cnt["V"]
        def _(s, l=l):
            s.wait_ge(ctx.s("V"), m_xbset)
            for tch in range(4):
                ins = s.activation(xbfa[:, :, 1 + 512 * tch:1 + 512 * tch + 512],
                                   xi_sb[:, :, 1 + 512 * tch:1 + 512 * tch + 512],
                                   AF.Copy)
            ctx.inc(ins, "S")
        S_(_)
        m_xbf = ctx.cnt["S"]

        m_oT_done = 0
        m_resid = 0
        m_xbfw = m_xbf
        m_tr_ev = 0
        m_qkev = 0
        m_tf = 0
        m_tp = 0
        m_p1sfree = [0, 0, 0, 0]
        m_p2sfree = [0, 0, 0, 0]
        m_ptilfree = [0, 0, 0, 0]
        m_oslotfree = [0] * 8
        m_resid_half = [0, 0]
        for w in range(NW):
            m_tf_l = {}
            for ci in range(5):
                if ci < 4:
                    c = 4 * w + ci
                    def _(t, l=l, w=w, c=c, ci=ci, mqk=m_qkev):
                        if c == 0:
                            t.wait_ge(ctx.s("DG"), m_wl)
                            t.wait_ge(ctx.s("S"), m_xbf)
                        if w > 0 and ci == 0:
                            t.wait_ge(ctx.s("S"), m_xbfw)
                        if mqk:
                            t.wait_ge(ctx.s("S"), mqk)
                        for n in range(3):
                            ps = psA[:, 0:512] if n == 0 else (
                                psA[:, 512:1024] if n == 1 else psB[:, 0:512])
                            for kb in range(NB):
                                off = 0 if kb >= 4 else 1
                                MM(t, ps,
                                   xbfa[:, kb, off + 128 * c: off + 128 * c + 128],
                                   wqkv_sb[:, kb, 512 * n:512 * n + 512],
                                   start=(kb == 0), stop=(kb == NB - 1))
                            ctx.inc(LASTMM[0], "T")
                    T_(_)
                    m_qkv = ctx.cnt["T"]
                if ci >= 1:
                    cm = ci - 1
                    def _(t, cm=cm, mtf=m_tf_l[ci - 1], mtrev=m_tr_ev):
                        t.wait_ge(ctx.s("V"), mtf)
                        if mtrev:
                            t.wait_ge(ctx.s("V"), mtrev)
                        for h in range(HB):
                            MM(t, sc2[0:64, 128 * h:128 * h + 128],
                               qp[:, 64 * h:64 * h + 64], CI["I128b"][:],
                               start=True, stop=True)
                        ctx.inc(LASTMM[0], "T")
                        for h in range(HB):
                            MM(t, sc3[0:64, 128 * h:128 * h + 128],
                               kp[:, 64 * h:64 * h + 64], CI["I128b"][:],
                               start=True, stop=True)
                        ctx.inc(LASTMM[0], "T")
                    T_(_)
                    m_tp = ctx.cnt["T"]
                    def _(v, cm=cm, w=w, m_tp=m_tp):
                        v.wait_ge(ctx.s("T"), m_tp)
                        v.tensor_copy(qtr[0:64, :, 128 * cm:128 * cm + 128],
                                      sc2[0:64, :].rearrange("p (h q) -> p h q", h=HB))
                        ins = v.tensor_copy(
                            ktr[0:64, :, 512 * (w % 2) + 128 * cm:
                                512 * (w % 2) + 128 * cm + 128],
                            sc3[0:64, :].rearrange("p (h q) -> p h q", h=HB))
                        ctx.inc(ins, "V")
                    V_(_)
                    m_tr_ev = ctx.cnt["V"]
                if ci < 4:
                    def _(s, c=c, m_qkv=m_qkv, mtf=m_tf):
                        s.wait_ge(ctx.s("T"), m_qkv)
                        if mtf:
                            s.wait_ge(ctx.s("V"), mtf)
                        s.activation(qraw[:], psA[:, 0:512], AF.Copy)
                        s.activation(kraw[:], psA[:, 512:1024], AF.Copy)
                        s.activation(v_sb[:, c % 8, :], psB[:, 0:512], AF.Copy)
                        s.activation(sqb[:, 0:512], psA[:, 0:512], AF.Square)
                        ins = s.activation(sqb[:, 512:1024], psA[:, 512:1024],
                                           AF.Square)
                        ctx.inc(ins, "S")
                    S_(_)
                    m_qkev = ctx.cnt["S"]
                    def _(v, m_qkev=m_qkev):
                        v.wait_ge(ctx.s("S"), m_qkev)
                        v.reduce_sum(ss8[:, 0:8].unsqueeze(2),
                                     sqb[:, 0:512].rearrange("p (h e) -> p h e", h=HB),
                                     axis=AX.X)
                        ins = v.reduce_sum(ss8[:, 8:16].unsqueeze(2),
                                     sqb[:, 512:1024].rearrange("p (h e) -> p h e", h=HB),
                                     axis=AX.X)
                        ctx.inc(ins, "V")
                    V_(_)
                    m_ss = ctx.cnt["V"]
                    def _(s, m_ss=m_ss):
                        s.wait_ge(ctx.s("V"), m_ss)
                        s.activation(sd8[:], ss8[:], AF.Ln, bias=1e-24)
                        ins = s.activation(rn8[:], sd8[:], AF.Exp, scale=-0.5)
                        ctx.inc(ins, "S")
                    S_(_)
                    m_rn = ctx.cnt["S"]
                    def _(v, c=c, m_rn=m_rn, mtp=m_tp):
                        v.wait_ge(ctx.s("S"), m_rn)
                        if mtp:
                            v.wait_ge(ctx.s("T"), mtp)
                        for which in range(2):
                            raw = qraw if which == 0 else kraw
                            out_p = qp if which == 0 else kp
                            scl = qs_sb if which == 0 else ks_sb
                            CS = CIB["CSQ8"] if which == 0 else CIB["CSK1"]
                            SN = CIB["SNQ8"] if which == 0 else CIB["SNK1"]
                            off = 512 * which
                            rnxs = rnx[:, off:off + 512]
                            sqbs = sqb[:, off:off + 512]
                            rn_ap = rn8[:, 8 * which:8 * which + 8].unsqueeze(2) \
                                .broadcast_to([128, HB, DH])
                            qs_ap = scl[:].unsqueeze(1).broadcast_to([128, HB, DH])
                            v.tensor_mul(
                                rnxs.rearrange("p (h e) -> p h e", h=HB), rn_ap, qs_ap)
                            v.tensor_mul(sqbs, raw[:], rnxs)
                            qn3 = sqbs.rearrange("p (h e) -> p h e", h=HB)
                            op3 = out_p[:].rearrange("p (h e) -> p h e", h=HB)
                            t23 = rnxs.rearrange("p (h e) -> p h e", h=HB)
                            cs_ap = CS[:, c, :].unsqueeze(1).broadcast_to([128, HB, DH])
                            sn0 = SN[:, c, 0:32].unsqueeze(1).broadcast_to([128, HB, 32])
                            sn1 = SN[:, c, 32:64].unsqueeze(1).broadcast_to([128, HB, 32])
                            v.tensor_mul(op3, qn3, cs_ap)
                            v.tensor_mul(t23[:, :, 0:32], qn3[:, :, 32:64], sn0)
                            v.tensor_mul(t23[:, :, 32:64], qn3[:, :, 0:32], sn1)
                            ins = v.tensor_add(out_p[:], out_p[:], rnxs)
                            ctx.inc(ins, "V")
                    V_(_)
                    m_tf = ctx.cnt["V"]
                    m_tf_l[ci] = m_tf
                    if l == 0 and c == 0:
                        dbg("qp00", [128, 512], BF16, qp[:], [("V", m_tf)])
                        dbg("kp00", [128, 512], BF16, kp[:], [("V", m_tf)])

            if l == 0 and w == 0:
                dbg("v00", [128, 4, 512], BF16, v_sb[:, 0:4, :], [("V", m_tr_ev)])
                dbg("qtr00", [64, HB, 512], BF16, qtr[0:64], [("V", m_tr_ev)])
                dbg("ktr00", [64, HB, 512], BF16, ktr[0:64, :, 0:512], [("V", m_tr_ev)])
            tiles = [(h, ci) for h in range(HB) for ci in range(4)]
            m_p1v = 0
            for ti in range(0, 32, 2):
                for k2 in range(2):
                    h, ci = tiles[ti + k2]
                    b = (ti + k2) % 4
                    psX = SCB[b]
                    c = 4 * w + ci
                    jmin = max(0, 4 - c)
                    def _(t, h=h, c=c, ci=ci, jmin=jmin, psX=psX,
                          mtr=m_tr_ev, msf=m_p1sfree[b]):
                        t.wait_ge(ctx.s("V"), mtr)
                        if msf:
                            t.wait_ge(ctx.s("S"), msf)
                        for j in range(jmin, 5):
                            kt = c - 4 + j
                            slot = (kt // 4) % 2
                            MM(t, psX[:, 128 * j:128 * j + 128],
                               qtr[0:64, h, 128 * ci:128 * ci + 128],
                               ktr[0:64, h, 512 * slot + 128 * (kt % 4):
                                   512 * slot + 128 * (kt % 4) + 128],
                               start=True, stop=True)
                        ctx.inc(LASTMM[0], "T")
                    T_(_)
                    m_s1 = ctx.cnt["T"]
                    def _(v, h=h, ci=ci, jmin=jmin, psX=psX, m_s1=m_s1):
                        v.wait_ge(ctx.s("T"), m_s1)
                        if jmin == 0:
                            mview = psX[:, 0:640].rearrange(
                                "p (a b) -> p a b", b=128)[:, 0:5:4, :]
                            ins = v.tensor_add(mview, mview, CIB["P1MASK"][:])
                        else:
                            ins = v.tensor_add(psX[:, 512:640], psX[:, 512:640],
                                               CIB["P1MASK"][:, 1, :])
                        mend = ctx.inc(ins, "V")
                        v.wait_ge(ctx.s("V"), mend)
                        ins = v.reduce_max(nmall[:, h, ci:ci + 1],
                                           psX[:, 128 * jmin:640], axis=AX.X,
                                           negate=True)
                        ctx.inc(ins, "V")
                    V_(_)
                    m_p1v = ctx.cnt["V"]
                def _(s, ti=ti, w=w, m_p1v=m_p1v):
                    s.wait_ge(ctx.s("V"), m_p1v)
                    for k2 in range(2):
                        h, ci = tiles[ti + k2]
                        psX = SCB[(ti + k2) % 4]
                        jmin = max(0, 4 - (4 * w + ci))
                        ins = s.activation(ptil[:, 0, 128 * jmin:640],
                                           psX[:, 128 * jmin:640], AF.Exp,
                                           bias=nmall[:, h, ci:ci + 1],
                                           accum_out=ssall[:, h, ci:ci + 1])
                    ctx.inc(ins, "S")
                S_(_)
                m_p1sfree[ti % 4] = m_p1sfree[(ti + 1) % 4] = ctx.cnt["S"]
            m_p1 = ctx.cnt["S"]

            if l == 0 and w == 0:
                dbg("nm00", [128, HB, 4], F32, nmall[:], [("S", m_p1)])
                dbg("ss00", [128, HB, 4], F32, ssall[:], [("S", m_p1)])
            def _(s, m_p1=m_p1):
                s.wait_ge(ctx.s("S"), m_p1)
                ins = s.activation(lnsb[:], ssall[:], AF.Ln)
                ctx.inc(ins, "S")
            S_(_)
            m_lns = ctx.cnt["S"]
            def _(v, m_lns=m_lns):
                v.wait_ge(ctx.s("S"), m_lns)
                ins = v.tensor_sub(zf[:].rearrange("p (h q) -> p h q", h=HB),
                                   nmall[:], lnsb[:])
                ctx.inc(ins, "V")
            V_(_)
            m_zf = ctx.cnt["V"]
            def _(t, m_zf=m_zf):
                t.wait_ge(ctx.s("V"), m_zf)
                MM(t, psB[0:32, 0:128], zf[:], CI["I128f"][:], start=True, stop=True)
                ctx.inc(LASTMM[0], "T")
            T_(_)
            m_zt = ctx.cnt["T"]
            def _(s, m_zt=m_zt):
                s.wait_ge(ctx.s("T"), m_zt)
                ins = s.activation(ztb[:], psB[0:32, 0:128], AF.Copy)
                ctx.inc(ins, "S")
            S_(_)
            m_ztb = ctx.cnt["S"]
            def _(sy, m_ztb=m_ztb):
                sy.wait_ge(ctx.s("S"), m_ztb)
                ins = sy.dma_start(out=qtr[64:65, :, :], in_=ztb[:])
                ctx.inc(ins, "DS", 16)
            Y_(_)
            m_zdma = ctx.cnt["DS"]

            m_p2v = 0
            pend = None
            for ti in range(0, 32, 2):
                for k2 in range(2):
                    h, ci = tiles[ti + k2]
                    b = (ti + k2) % 4
                    psX = SCB[b]
                    c = 4 * w + ci
                    jmin = max(0, 4 - c)
                    def _(t, h=h, c=c, ci=ci, jmin=jmin, psX=psX,
                          m_zdma=m_zdma, msf=m_p2sfree[b]):
                        t.wait_ge(ctx.s("DS"), m_zdma)
                        if msf:
                            t.wait_ge(ctx.s("S"), msf)
                        for j in range(jmin, 5):
                            kt = c - 4 + j
                            slot = (kt // 4) % 2
                            MM(t, psX[:, 128 * j:128 * j + 128],
                               ktr[0:65, h, 512 * slot + 128 * (kt % 4):
                                   512 * slot + 128 * (kt % 4) + 128],
                               qtr[0:65, h, 128 * ci:128 * ci + 128],
                               start=True, stop=True)
                        ctx.inc(LASTMM[0], "T")
                    T_(_)
                    m_s2 = ctx.cnt["T"]
                    def _(v, jmin=jmin, psX=psX, m_s2=m_s2):
                        v.wait_ge(ctx.s("T"), m_s2)
                        if jmin == 0:
                            mview = psX[:, 0:640].rearrange(
                                "p (a b) -> p a b", b=128)[:, 0:5:4, :]
                            ins = v.tensor_add(mview, mview, CIB["P2ADD"][:])
                        else:
                            ins = v.tensor_add(psX[:, 512:640], psX[:, 512:640],
                                               CIB["P2ADD"][:, 1, :])
                        ctx.inc(ins, "V")
                    V_(_)
                    m_p2v = ctx.cnt["V"]
                def _(s, ti=ti, w=w, m_p2v=m_p2v,
                      mpt=(m_ptilfree[ti % 4], m_ptilfree[(ti + 1) % 4])):
                    s.wait_ge(ctx.s("V"), m_p2v)
                    for k2 in range(2):
                        if mpt[k2]:
                            s.wait_ge(ctx.s("T"), mpt[k2])
                        h, ci = tiles[ti + k2]
                        b = (ti + k2) % 4
                        jmin = max(0, 4 - (4 * w + ci))
                        ins = s.activation(ptil[:, b, 128 * jmin:640],
                                           SCB[b][:, 128 * jmin:640], AF.Exp)
                    ctx.inc(ins, "S")
                S_(_)
                msec = ctx.cnt["S"]
                m_p2sfree[ti % 4] = m_p2sfree[(ti + 1) % 4] = msec
                mgmask = msec

                def emit_oacc(oti, omg, w=w):
                    def _(t, ti=oti, w=w, mg=omg,
                          mev=(m_oslotfree[oti % 8], m_oslotfree[(oti + 1) % 8])):
                        t.wait_ge(ctx.s("S"), mg)
                        for k2 in range(2):
                            k = ti + k2
                            h, ci = tiles[k]
                            b = k % 4
                            sl = k % 8
                            oten = SCB[sl % 4]
                            ocol = 640 + 128 * (sl // 4)
                            if mev[k2]:
                                t.wait_ge(ctx.s("V"), mev[k2])
                            par = h % 2
                            orgn = oten[64 * par:64 * par + 64, ocol:ocol + 128]
                            c = 4 * w + ci
                            jmin = max(0, 4 - c)
                            for j in range(jmin, 5):
                                kt = c - 4 + j
                                MM(t, orgn, v_sb[:, kt % 8, 64 * h:64 * h + 64],
                                   ptil[:, b, 128 * j:128 * j + 128],
                                   start=(j == jmin), stop=(j == 4))
                            ctx.inc(LASTMM[0], "T")
                    T_(_)
                    mT2 = ctx.cnt["T"]
                    m_ptilfree[oti % 4] = mT2 - 1
                    m_ptilfree[(oti + 1) % 4] = mT2
                    def _(v, ti=oti, mT2=mT2):
                        v.wait_ge(ctx.s("T"), mT2)
                        for k2 in range(2):
                            k = ti + k2
                            h, ci = tiles[k]
                            sl = k % 8
                            oten = SCB[sl % 4]
                            ocol = 640 + 128 * (sl // 4)
                            par = h % 2
                            p2i = h // 2
                            ins = v.tensor_copy(
                                oT[64 * par:64 * par + 64, p2i,
                                   128 * ci:128 * ci + 128],
                                oten[64 * par:64 * par + 64, ocol:ocol + 128])
                        ctx.inc(ins, "V")
                    V_(_)
                    m_oslotfree[oti % 8] = m_oslotfree[(oti + 1) % 8] = ctx.cnt["V"]

                emit_oacc(ti, mgmask)
            m_oev = ctx.cnt["V"]

            if l == 0 and w == 0:
                dbg("oT00", [128, 4, 512], BF16, oT[:], [("V", m_oev)])
            m_wo = 0
            for dmb in range(NB):
                half = dmb % 2
                def _(t, dmb=dmb, half=half, m_oev=m_oev, w=w,
                      mrh=m_resid_half[half]):
                    t.wait_ge(ctx.s("V"), m_oev)
                    if mrh:
                        t.wait_ge(ctx.s("V"), mrh)
                    ps = psA[:, 512 * half:512 * half + 512]
                    for kb in range(4):
                        MM(t, ps, wo_sb2[:, kb, 128 * dmb:128 * dmb + 128],
                           oT[:, kb, :], start=(kb == 0), stop=(kb == 3))
                    ctx.inc(LASTMM[0], "T")
                T_(_)
                m_wo = ctx.cnt["T"]
                def _(v, dmb=dmb, half=half, w=w, m_wo=m_wo):
                    v.wait_ge(ctx.s("T"), m_wo)
                    ps = psA[:, 512 * half:512 * half + 512]
                    dst = xi_sb[:, dmb, 1 + 512 * w:1 + 512 * w + 512]
                    ins = v.tensor_add(dst, ps, dst)
                    ctx.inc(ins, "V")
                V_(_)
                m_resid_half[half] = ctx.cnt["V"]
            m_resid = ctx.cnt["V"]
            def _(s, w=w, m_resid=m_resid):
                s.wait_ge(ctx.s("V"), m_resid)
                ins = s.activation(xbfa[:, :, 1 + 512 * w:1 + 512 * w + 512],
                                   xi_sb[:, :, 1 + 512 * w:1 + 512 * w + 512],
                                   AF.Copy)
                ctx.inc(ins, "S")
            S_(_)
            m_xbfw = ctx.cnt["S"]

        m_attn_done_V = ctx.cnt["V"]
        m_attn_done_S = ctx.cnt["S"]
        dbg(f"xiA{l}", [128, NB, T], F32, xi_sb[:, :, 1:T + 1], [("V", m_attn_done_V)])

        barrier()
        ctx.free(f"ptil{l}", f"ztb{l}", f"zf{l}", f"lnsb{l}",
                 f"ssall{l}", f"nmall{l}", f"rn8{l}", f"sd8{l}", f"ss8{l}",
                 f"kp{l}", f"qp{l}", f"rnx{l}", f"sqb{l}", f"kraw{l}", f"qraw{l}",
                 f"oT{l}", f"qtr{l}", f"ktr{l}", f"ks{l}", f"qs{l}",
                 f"wob{l}", f"wqkv{l}", f"xbfa{l}")

        h2 = ctx.alloc(f"h2_{l}", [128, NB, T], BF16, side="right")
        lngc = ctx.alloc(f"lng{l}", [128, NB], F32, side="right")
        lnbc = ctx.alloc(f"lnb{l}", [128, NB], F32, side="right")
        b1ac = ctx.alloc(f"b1a{l}", [128, 22], F32, side="right")
        b1gc = ctx.alloc(f"b1g{l}", [128, 22], F32, side="right")
        b2c = ctx.alloc(f"b2{l}", [128, NB], F32, side="right")
        w1q = ctx.alloc(f"w1q{l}", [128, NB, 1536], BF16, side="right")
        w2q = ctx.alloc(f"w2q{l}", [128, 6, D], BF16, side="right")
        glu = ctx.alloc(f"glu{l}", [128, 6, T], BF16, side="right")
        glscr = ctx.alloc(f"glscr{l}", [128, 1024], BF16, side="right")

        def _(g, l=l):
            ctx.inc(g.dma_start(out=lngc[:], in_=B_lng[l]), "DG", 16)
            ctx.inc(g.dma_start(out=lnbc[:], in_=B_lnb[l]), "DG", 16)
            ctx.inc(g.dma_start(out=b1ac[:], in_=B_b1a[l]), "DG", 16)
            ctx.inc(g.dma_start(out=b1gc[:], in_=B_b1g[l]), "DG", 16)
            ctx.inc(g.dma_start(out=b2c[:], in_=B_b2[l]), "DG", 16)
        G_(_)
        m_fl = ctx.cnt["DG"]

        m_h2a_t = [0, 0, 0, 0]
        def _affine(tch, mk):
            def _(v, tch=tch):
                if tch == 0:
                    v.wait_ge(ctx.s("DG"), m_fl)
                for kb in range(NB):
                    ins = v.tensor_scalar(h2[:, kb, 512 * tch:512 * tch + 512],
                                          h2[:, kb, 512 * tch:512 * tch + 512],
                                          lngc[:, kb:kb + 1], lnbc[:, kb:kb + 1],
                                          ALU.mult, ALU.add)
                ctx.inc(ins, "V")
            V_(_)
            m_h2a_t[tch] = ctx.cnt["V"]
        m_h2, _lnm = emit_ln(xi_sb[:], h2[:], waits=[("V", m_attn_done_V)],
                             shift=True, pstats=(sc2[:, 0:512], sc3[:, 0:512]),
                             post_tch=_affine)
        m_h2a = ctx.cnt["V"]
        dbg(f"h2_{l}", [128, NB, T], BF16, h2[:], [("V", m_h2a)])

        QBLKS = [list(range(6 * qd, min(6 * qd + 6, 22))) for qd in range(4)]
        m_w2add = 0
        m_gluev = 0
        m_w1read = {}
        m_w2mm_last = 0
        m_glu_ev = [0, 0]
        m_gelu_ev = [0, 0]
        m_w2slot = [0, 0, 0, 0]
        PSB2 = [psA, psB]
        for qd in range(4):
            blks = QBLKS[qd]
            nb_q = len(blks)
            m_w1blk = {}
            for ji, gb in enumerate(blks):
                def _(g, qd=qd, ji=ji, gb=gb, l=l,
                      mrd=m_w1read.get(ji, 0)):
                    if mrd:
                        g.wait_ge(ctx.s("T"), mrd)
                    a0 = 128 * gb
                    ctx.inc(g.dma_start(out=w1q[:, :, 128 * ji:128 * ji + 128],
                                        in_=B_w1[l][:, :, a0:a0 + 128]), "DG", 16)
                    ctx.inc(g.dma_start(
                        out=w1q[:, :, 768 + 128 * ji:768 + 128 * ji + 128],
                        in_=B_w1[l][:, :, 2816 + a0:2816 + a0 + 128]), "DG", 16)
                G_(_)
                m_w1blk[ji] = ctx.cnt["DG"]
            def _(g, blks=blks, nb_q=nb_q, l=l, mw=m_w2mm_last):
                if mw:
                    g.wait_ge(ctx.s("T"), mw)
                ctx.inc(g.dma_start(out=w2q[:, 0:nb_q, :],
                                    in_=B_w2[l][:, blks[0]:blks[0] + nb_q, :]),
                        "DG", 16)
            G_(_)
            m_w2dma = ctx.cnt["DG"]

            it = 0
            for ji in range(nb_q):
                gb = blks[ji]
                for tch in range(4):
                    buf = PSB2[it % 2]
                    def _(t, ji=ji, tch=tch, qd=qd, it=it, buf=buf,
                          mblk=m_w1blk[ji],
                          mge=m_glu_ev[it % 2], mse=m_gelu_ev[it % 2]):
                        t.wait_ge(ctx.s("DG"), mblk)
                        if qd == 0:
                            t.wait_ge(ctx.s("V"), m_h2a_t[tch])
                        if it < 2 and m_w2add:
                            t.wait_ge(ctx.s("V"), m_w2add)
                        if mge:
                            t.wait_ge(ctx.s("V"), mge)
                            t.wait_ge(ctx.s("S"), mse)
                        for kb in range(NB):
                            MM(t, buf[:, 0:512], w1q[:, kb, 128 * ji:128 * ji + 128],
                               h2[:, kb, 512 * tch:512 * tch + 512],
                               start=(kb == 0), stop=(kb == NB - 1))
                        ctx.inc(LASTMM[0], "T")
                        for kb in range(NB):
                            MM(t, buf[:, 512:1024],
                               w1q[:, kb, 768 + 128 * ji:768 + 128 * ji + 128],
                               h2[:, kb, 512 * tch:512 * tch + 512],
                               start=(kb == 0), stop=(kb == NB - 1))
                        ctx.inc(LASTMM[0], "T")
                    T_(_)
                    m_ag = ctx.cnt["T"]
                    if tch == 3:
                        m_w1read[ji] = m_ag
                    def _(s, gb=gb, buf=buf, it=it, m_ag=m_ag):
                        s.wait_ge(ctx.s("T"), m_ag)
                        ins = s.activation(
                            glscr[:, 512 * (it % 2):512 * (it % 2) + 512],
                            buf[:, 512:1024], AF.Gelu, bias=b1gc[:, gb:gb + 1])
                        ctx.inc(ins, "S")
                    S_(_)
                    m_gelu_ev[it % 2] = ctx.cnt["S"]
                    def _(v, ji=ji, tch=tch, gb=gb, buf=buf, it=it,
                          mg=ctx.cnt["S"]):
                        v.wait_ge(ctx.s("S"), mg)
                        ins = v.scalar_tensor_tensor(
                            glu[:, ji, 512 * tch:512 * tch + 512], buf[:, 0:512],
                            b1ac[:, gb:gb + 1],
                            glscr[:, 512 * (it % 2):512 * (it % 2) + 512],
                            ALU.add, ALU.mult)
                        ctx.inc(ins, "V")
                    V_(_)
                    m_glu_ev[it % 2] = ctx.cnt["V"]
                    it += 1
            m_gluev = ctx.cnt["V"]

            SLOT = [(psA, 0), (psA, 1), (psB, 0), (psB, 1)]
            it2 = 0
            for tch in range(4):
                for dmb in range(NB):
                    ps, half = SLOT[it2 % 4]
                    def _(t, tch=tch, dmb=dmb, ps=ps, half=half, it2=it2,
                          nb_q=nb_q, mglu=m_gluev, mw2=m_w2dma,
                          msl=m_w2slot[it2 % 4]):
                        t.wait_ge(ctx.s("V"), mglu)
                        t.wait_ge(ctx.s("DG"), mw2)
                        if msl:
                            t.wait_ge(ctx.s("V"), msl)
                        pss = ps[:, 512 * half:512 * half + 512]
                        for jji in range(nb_q):
                            MM(t, pss, w2q[:, jji, 128 * dmb:128 * dmb + 128],
                               glu[:, jji, 512 * tch:512 * tch + 512],
                               start=(jji == 0), stop=(jji == nb_q - 1))
                        ctx.inc(LASTMM[0], "T")
                    T_(_)
                    m_w2m = ctx.cnt["T"]
                    def _(v, tch=tch, dmb=dmb, ps=ps, half=half, qd=qd,
                          m_w2m=m_w2m):
                        v.wait_ge(ctx.s("T"), m_w2m)
                        pss = ps[:, 512 * half:512 * half + 512]
                        dst = xi_sb[:, dmb, 1 + 512 * tch:1 + 512 * tch + 512]
                        if qd == 0:
                            ins = v.scalar_tensor_tensor(dst, pss, b2c[:, dmb:dmb + 1],
                                                         dst, ALU.add, ALU.add)
                        else:
                            ins = v.tensor_add(dst, pss, dst)
                        ctx.inc(ins, "V")
                    V_(_)
                    m_w2slot[it2 % 4] = ctx.cnt["V"]
                    it2 += 1
            m_w2mm_last = ctx.cnt["T"]
            m_w2add = ctx.cnt["V"]

        m_xi = ctx.cnt["V"]
        dbg(f"xiF{l}", [128, NB, T], F32, xi_sb[:, :, 1:T + 1], [("V", m_xi)])
        barrier()
        ctx.free(f"glscr{l}", f"glu{l}", f"w2q{l}", f"w1q{l}", f"b2{l}",
                 f"b1g{l}", f"b1a{l}", f"lnb{l}", f"lng{l}", f"h2_{l}")

    m_inner = m_xi

    x3_sb = ctx.alloc("x3_sb", [128, NB, T], F32, side="right")
    yst = ctx.alloc("yst", [128, NB, 512], F32, side="right")
    def _(sy):
        sy.wait_ge(ctx.s("DS"), m_xnspill)
        sy.wait_ge(ctx.s("V"), ctx.cnt["V"])
        sy.wait_ge(ctx.s("S"), ctx.cnt["S"])
        sy.wait_ge(ctx.s("T"), ctx.cnt["T"])
        ins = sy.dma_start(out=x3_sb[:], in_=x_spill)
        ctx.inc(ins, "DS", 16)
    Y_(_)
    m_x3 = ctx.cnt["DS"]
    m_ydma = 0
    for tch in range(4):
        t0 = 512 * tch
        def _(v, tch=tch, t0=t0):
            v.wait_ge(ctx.s("V"), m_inner)
            v.wait_ge(ctx.s("DS"), m_x3)
            if m_ydma:
                v.wait_ge(ctx.s("DS"), m_ydma)
            ins = v.tensor_add(yst[:], x3_sb[:, :, t0:t0 + 512],
                               xi_sb[:, :, 1 + t0:1 + t0 + 512])
            ctx.inc(ins, "V")
        V_(_)
        m_y = ctx.cnt["V"]
        def _(sy, tch=tch, t0=t0, m_y=m_y):
            sy.wait_ge(ctx.s("V"), m_y)
            ins = sy.dma_start(out=yT[:, :, t0:t0 + 512], in_=yst[:])
            ctx.inc(ins, "DS", 16)
        Y_(_)
        m_ydma = ctx.cnt["DS"]

    def _(sy):
        sy.wait_ge(ctx.s("DS"), ctx.cnt["DS"])
        sy.wait_ge(ctx.s("DG"), ctx.cnt["DG"])
        if ctx.cnt["G"]:
            sy.wait_ge(ctx.s("G"), ctx.cnt["G"])
        sy.nop()
    Y_(_)

    ctx._open.pop("block").__exit__(None, None, None)
    for k in reversed(list(ctx._open)):
        ctx._open.pop(k).__exit__(None, None, None)
    return nc, dbg_out


def make_in_maps(inputs):
    f = np.float32
    cstA = host_constants()
    cstB = host_constants_b()
    x = np.asarray(inputs["L0_attn_x"], dtype=f)
    base = {}
    base["A_Wq"] = np.ascontiguousarray(np.asarray(inputs["L0_attn_Wq"], f).reshape(NB, 128, D).transpose(1, 0, 2))
    base["A_Wk"] = np.ascontiguousarray(np.asarray(inputs["L0_attn_Wk"], f).reshape(NB, 128, D).transpose(1, 0, 2))
    base["A_Wv"] = np.ascontiguousarray(np.asarray(inputs["L0_attn_Wv"], f).reshape(NB, 128, D).transpose(1, 0, 2))
    base["A_Wg"] = np.ascontiguousarray(np.asarray(inputs["L0_attn_Wg"], f).reshape(NB, 128, D).transpose(1, 0, 2))
    HSLOT = [2 * (s % 8) + (s // 8) for s in range(16)]
    wa = np.asarray(inputs["L0_attn_Wa"], f)[:, HSLOT]
    base["A_Wa"] = np.ascontiguousarray(wa.reshape(NB, 128, HA).transpose(1, 0, 2))
    base["A_ba"] = np.ascontiguousarray(np.asarray(inputs["L0_attn_ba"], f)[HSLOT].reshape(HA, 1))
    base["A_Wo"] = np.ascontiguousarray(np.asarray(inputs["L0_attn_Wo"], f).reshape(NB, 128, D).transpose(1, 0, 2))
    base["A_gnw"] = np.ascontiguousarray(np.asarray(inputs["L0_attn_gn_w"], f).reshape(NB, 128).T)
    base["A_gnb"] = np.ascontiguousarray(np.asarray(inputs["L0_attn_gn_b"], f).reshape(NB, 128).T)
    wqkv = np.asarray(inputs["L0_ffn_wqkv"], f)
    qs = np.asarray(inputs["L0_ffn_q_scale"], f)
    ks = np.asarray(inputs["L0_ffn_k_scale"], f)
    wo = np.asarray(inputs["L0_ffn_wo"], f)
    lng = np.asarray(inputs["L0_ffn_ln_g"], f)
    lnb = np.asarray(inputs["L0_ffn_ln_b"], f)
    w1 = np.asarray(inputs["L0_ffn_w1"], f)
    b1 = np.asarray(inputs["L0_ffn_b1"], f)
    w2 = np.asarray(inputs["L0_ffn_w2"], f)
    b2 = np.asarray(inputs["L0_ffn_b2"], f)
    for l in range(2):
        base[f"B_wqkv{l}"] = np.ascontiguousarray(wqkv[l].reshape(NB, 128, 1536).transpose(1, 0, 2))
        base[f"B_qs{l}"] = np.ascontiguousarray(np.broadcast_to(qs[l], (128, DH)))
        base[f"B_ks{l}"] = np.ascontiguousarray(np.broadcast_to(ks[l], (128, DH)))
        base[f"B_wo{l}"] = np.ascontiguousarray(wo[l].reshape(4, 128, D).transpose(1, 0, 2))
        base[f"B_lng{l}"] = np.ascontiguousarray(lng[l].reshape(NB, 128).T)
        base[f"B_lnb{l}"] = np.ascontiguousarray(lnb[l].reshape(NB, 128).T)
        w1p = np.zeros((D, 2 * HIDP), f)
        w1p[:, 0:HID] = w1[l][:, 0:HID]
        w1p[:, HIDP:HIDP + HID] = w1[l][:, HID:2 * HID]
        base[f"B_w1{l}"] = np.ascontiguousarray(w1p.reshape(NB, 128, 2 * HIDP).transpose(1, 0, 2))
        b1a = np.zeros((HIDP,), f); b1a[0:HID] = b1[l][0:HID]
        b1g = np.zeros((HIDP,), f); b1g[0:HID] = b1[l][HID:2 * HID]
        base[f"B_b1a{l}"] = np.ascontiguousarray(b1a.reshape(22, 128).T)
        base[f"B_b1g{l}"] = np.ascontiguousarray(b1g.reshape(22, 128).T)
        w2p = np.zeros((HIDP, D), f)
        w2p[0:HID] = w2[l]
        base[f"B_w2{l}"] = np.ascontiguousarray(w2p.reshape(22, 128, D).transpose(1, 0, 2))
        base[f"B_b2{l}"] = np.ascontiguousarray(b2[l].reshape(NB, 128).T)
    for k, v in {**cstA, **cstB}.items():
        base["c_" + k] = v
    maps = []
    for b in range(8):
        m = dict(base)
        m["xT"] = np.ascontiguousarray(x[b].T.reshape(NB, 128, T).transpose(1, 0, 2))
        maps.append(m)
    return maps


def kernel(**inputs):
    nc, _ = build_program()
    in_maps = make_in_maps(inputs)
    res = run_bass_kernel_spmd(nc, in_maps, core_ids=list(range(8)))
    out = np.stack([r["yT"].transpose(1, 0, 2).reshape(D, T).T for r in res.results], axis=0)
    return np.ascontiguousarray(out.astype(np.float32))


# version 16
# speedup vs baseline: 1.6425x; 1.0644x over previous
import numpy as np
import ml_dtypes
import concourse.bass as bass
import concourse.mybir as mybir
from concourse.bass_utils import run_bass_kernel_spmd

F32 = mybir.dt.float32
BF16 = mybir.dt.bfloat16
AF = mybir.ActivationFunctionType
ALU = mybir.AluOpType
AX = mybir.AxisListType

T = 2048
D = 1024
NB = 8
HA = 16
DH = 64
C = 128
NC = T // C
HB = 8
WIN = 512
NW = T // WIN
HID = 2730
HIDP = 2816


def host_constants():
    c = {}
    c["I128f"] = np.eye(128, dtype=np.float32)
    c["I128b"] = np.eye(128, dtype=ml_dtypes.bfloat16)
    c["ones128"] = np.ones((128, 128), dtype=ml_dtypes.bfloat16)
    ed = (np.arange(128)[:, None] // 64 == np.arange(128)[None, :] // 64)
    c["EDIAG"] = ed.astype(ml_dtypes.bfloat16)
    HSLOT = [2 * (s % 8) + (s // 8) for s in range(16)]
    sel = np.zeros((16, 1024), dtype=np.float32)
    for s in range(16):
        h = HSLOT[s]
        sel[s, 64 * h:64 * h + 64] = 1.0
    c["SEL16"] = sel.astype(ml_dtypes.bfloat16)
    selq = np.zeros((16, 1024), dtype=np.float32)
    for s in range(16):
        h = HSLOT[s]
        pair, par = h // 2, h % 2
        selq[s, 128 * pair + 64 * par:128 * pair + 64 * par + 64] = 1.0
    c["SELQ"] = selq.astype(ml_dtypes.bfloat16)
    return c


def host_constants_b():
    c = {}
    half = np.arange(0, DH, 2, dtype=np.float64)
    inv_freq = 1.0 / 10000.0 ** (half / DH)
    t = np.arange(T, dtype=np.float64)
    freqs = t[:, None] * inv_freq[None, :]
    freqs = np.concatenate([freqs, freqs], axis=-1)
    base = (half + 0.4 * DH) / (1.4 * DH)
    power = (t - T // 2) / float(WIN // 2)
    scale = base[None, :] ** power[:, None]
    scale = np.concatenate([scale, scale], axis=-1)
    cos, sin = np.cos(freqs), np.sin(freqs)
    sgn = np.ones(DH); sgn[:DH // 2] = -1.0
    QK = 8.0
    def tm(a):
        return np.ascontiguousarray(
            a.reshape(NC, 128, DH).transpose(1, 0, 2)).astype(ml_dtypes.bfloat16)
    c["CSQ8"] = tm(cos * scale * QK)
    c["SNQ8"] = tm(sin * scale * QK * sgn[None, :])
    c["CSK1"] = tm(cos / scale)
    c["SNK1"] = tm(sin / scale * sgn[None, :])
    i = np.arange(128)[:, None]; j = np.arange(128)[None, :]
    m = np.zeros((128, 2, 128), dtype=np.float32)
    m[:, 0, :] = np.where(j < i, -1e30, 0.0)
    m[:, 1, :] = np.where(j > i, -1e30, 0.0)
    c["P1MASK"] = m
    m2 = np.zeros((128, 2, 128), dtype=np.float32)
    m2[:, 0, :] = np.where(i < j, -1e30, 0.0)
    m2[:, 1, :] = np.where(i > j, -1e30, 0.0)
    c["P2ADD"] = m2
    return c


class Ctx:
    def __init__(self, nc):
        self.nc = nc
        self._sems = {}
        self.cnt = {}
        self._open = {}

    def sem(self, name):
        g = self.nc.semaphore(name)
        h = g.__enter__()
        self._open["sem_" + name] = g
        self._sems[name] = h
        self.cnt[name] = 0
        return h

    def s(self, name):
        return self._sems[name]

    def inc(self, ins, name, n=1):
        self.cnt[name] += n
        ins.then_inc(self._sems[name], n)
        return self.cnt[name]

    def alloc(self, name, shape, dtype, psum=False, side="left"):
        if psum:
            g = self.nc.psum_tensor(name, list(shape), dtype)
        else:
            g = self.nc.sbuf_tensor(name, list(shape), dtype, side=side)
        h = g.__enter__()
        self._open[name] = g
        return h

    def free(self, *names):
        for n in names:
            self._open.pop(n).__exit__(None, None, None)


def build_program(dbg_names=(), race_check=False, stop_after=None, n_chunks=NC):
    nc = bass.Bass("TRN2", target_bir_lowering=False,
                   detect_race_conditions=race_check)
    cstA = host_constants()
    cstB = host_constants_b()

    def din(name, shape, dtype=F32):
        return nc.dram_tensor(name, list(shape), dtype, kind="ExternalInput").ap()

    xT = din("xT", [128, NB, T])
    A_Wq = din("A_Wq", [128, NB, D]); A_Wk = din("A_Wk", [128, NB, D])
    A_Wv = din("A_Wv", [128, NB, D]); A_Wg = din("A_Wg", [128, NB, D])
    A_Wa = din("A_Wa", [128, NB, HA]); A_ba = din("A_ba", [HA, 1])
    A_Wo = din("A_Wo", [128, NB, D])
    A_gnw = din("A_gnw", [128, NB]); A_gnb = din("A_gnb", [128, NB])
    B_wqkv = [din(f"B_wqkv{l}", [128, NB, 1536]) for l in range(2)]
    B_qs = [din(f"B_qs{l}", [128, DH]) for l in range(2)]
    B_ks = [din(f"B_ks{l}", [128, DH]) for l in range(2)]
    B_wo = [din(f"B_wo{l}", [128, 4, D]) for l in range(2)]
    B_lng = [din(f"B_lng{l}", [128, NB]) for l in range(2)]
    B_lnb = [din(f"B_lnb{l}", [128, NB]) for l in range(2)]
    B_w1 = [din(f"B_w1{l}", [128, NB, 2 * HIDP]) for l in range(2)]
    B_b1a = [din(f"B_b1a{l}", [128, 22]) for l in range(2)]
    B_b1g = [din(f"B_b1g{l}", [128, 22]) for l in range(2)]
    B_w2 = [din(f"B_w2{l}", [128, 22, D]) for l in range(2)]
    B_b2 = [din(f"B_b2{l}", [128, NB]) for l in range(2)]
    CONST = {}
    for k, v in {**cstA, **cstB}.items():
        CONST[k] = din("c_" + k, v.shape, F32 if v.dtype == np.float32 else BF16)

    yT = nc.dram_tensor("yT", [128, NB, T], F32, kind="ExternalOutput").ap()
    dbg_out = {}

    x_spill = nc.dram_tensor("x_spill", [128, NB, T], F32).ap()
    s_spill = nc.dram_tensor("s_spill", [HA, T + 1], F32).ap()

    ctx = Ctx(nc)
    for s in ("T", "V", "S", "DS", "DG", "G"):
        ctx.sem(s)

    CI = {}
    for k in cstA:
        v = cstA[k]
        CI[k] = ctx.alloc("k_" + k, v.shape, F32 if v.dtype == np.float32 else BF16)

    cpool = ctx.alloc("cpool", [128, 4], F32)
    CPOOL_VALS = [1e-5, 1.0, -1.0, 1e-24]
    for i, val in enumerate(CPOOL_VALS):
        nc.const_aps.register(F32, val, cpool[:, i:i + 1])

    x_sb = ctx.alloc("x_sb", [128, NB, T + 1], F32, side="right")
    gnw = ctx.alloc("gnw", [128, NB], F32); gnb = ctx.alloc("gnb", [128, NB], F32)
    mu_b = ctx.alloc("mu_b", [128, 512], F32)
    mu2_b = ctx.alloc("mu2_b", [128, 512], F32)
    var_b = ctx.alloc("var_b", [128, 512], F32)
    rb_b = ctx.alloc("rb_b", [128, 512], F32)
    sd_b = mu2_b
    tsub = ctx.alloc("tsub", [128, 512], F32)

    gamhat = ctx.alloc("gamhat", [128, NB], F32)
    bethat = ctx.alloc("bethat", [128, NB], F32)
    ln_scr = ctx.alloc("ln_scr", [128, NB, 512], BF16)
    xn = ctx.alloc("xn", [128, NB, T], BF16)
    Wa = ctx.alloc("Wa", [128, NB, HA], BF16)
    ba = ctx.alloc("ba", [HA, 1], F32)
    Wq = ctx.alloc("Wq", [128, NB, D], BF16)
    Wk = ctx.alloc("Wk", [128, NB, D], BF16)
    o_gla = ctx.alloc("o_gla", [128, NB, T], BF16)
    opart = ctx.alloc("opart", [128, NB, NC], F32)
    opart2 = ctx.alloc("opart2", [128, NB, NC], F32)
    Wv = ctx.alloc("Wv", [128, NB, D], BF16)

    psA = ctx.alloc("psA", [128, 1024], F32, psum=True)
    psB = ctx.alloc("psB", [128, 1024], F32, psum=True)
    psC = ctx.alloc("psC", [128, 512], F32, psum=True)
    psD = ctx.alloc("psD", [128, 512], F32, psum=True)
    psE = ctx.alloc("psE", [128, 512], F32, psum=True)
    psF = ctx.alloc("psF", [128, 512], F32, psum=True)

    blkg = nc.Block()
    block = blkg.__enter__()
    ctx._open["block"] = blkg

    LASTMM = [None]
    def MM(t, *a, **kw):
        ins = t.matmul(*a, **kw)
        LASTMM[0] = ins
        return ins

    def SYNC(eng, sem, ins):
        m = ctx.inc(ins, sem)
        eng.wait_ge(ctx.s(sem), m)

    T_, V_, S_, G_, Y_ = block.tensor, block.vector, block.scalar, block.gpsimd, block.sync

    def dbg(name, shape, dtype, src_ap, waits):
        if name not in dbg_names:
            return
        dbg_out[name] = nc.dram_tensor("dbg_" + name, list(shape), dtype,
                                       kind="ExternalOutput").ap()
        def _(sy):
            for se, vv in waits:
                sy.wait_ge(ctx.s(se), vv)
            ins = sy.dma_start(out=dbg_out[name], in_=src_ap)
            ctx.inc(ins, "DS", 16)
        Y_(_)


    def finish_early():
        def _(v):
            v.wait_ge(ctx.s("T"), ctx.cnt["T"])
            v.wait_ge(ctx.s("S"), ctx.cnt["S"])
            ins = v.memset(mu_b[:], 0.0)
            ctx.inc(ins, "V")
        V_(_)
        mz = ctx.cnt["V"]
        for tch in range(4):
            def _(sy, tch=tch):
                sy.wait_ge(ctx.s("V"), mz)
                for kb in range(NB):
                    ins = sy.dma_start(out=yT[:, kb, 512 * tch:512 * tch + 512],
                                       in_=mu_b[:])
                    ctx.inc(ins, "DS", 16)
            Y_(_)
        def _(sy):
            sy.wait_ge(ctx.s("DS"), ctx.cnt["DS"])
            sy.wait_ge(ctx.s("DG"), ctx.cnt["DG"])
            if ctx.cnt["G"]:
                sy.wait_ge(ctx.s("G"), ctx.cnt["G"])
            sy.nop()
        Y_(_)
        ctx._open.pop("block").__exit__(None, None, None)
        for k in reversed(list(ctx._open)):
            ctx._open.pop(k).__exit__(None, None, None)
        return nc, dbg_out

    def _(g):
        for k in cstA:
            ctx.inc(g.dma_start(out=CI[k][:], in_=CONST[k]), "DG", 16)
        ctx.inc(g.dma_start(out=Wq[:], in_=A_Wq), "DG", 16)
        ctx.inc(g.dma_start(out=Wk[:], in_=A_Wk), "DG", 16)
        ctx.inc(g.dma_start(out=Wv[:], in_=A_Wv), "DG", 16)
        ctx.inc(g.dma_start(out=Wa[:], in_=A_Wa), "DG", 16)
        ctx.inc(g.dma_start(out=ba[:], in_=A_ba), "DG", 16)
        ctx.inc(g.dma_start(out=gnw[:], in_=A_gnw), "DG", 16)
        ctx.inc(g.dma_start(out=gnb[:], in_=A_gnb), "DG", 16)
    G_(_)
    m_loads = ctx.cnt["DG"]

    def _(sy):
        ins = sy.dma_start(out=x_sb[:, :, 1:T + 1], in_=xT)
        ctx.inc(ins, "DS", 16)
    Y_(_)
    m_xin = ctx.cnt["DS"]

    def _(v):
        v.memset(x_sb[:, :, 0:1], 0.0)
        for i, val in enumerate(CPOOL_VALS):
            ins = v.memset(cpool[:, i:i + 1], val)
        ctx.inc(ins, "V")
    V_(_)
    m_cpool = ctx.cnt["V"]
    def _(s):
        s.wait_ge(ctx.s("V"), m_cpool)
        s.nop()
    S_(_)

    ln_state = {"pslast": 0}

    def emit_ln(src3, out3, waits, shift=False, out_f32=False, pstats=None,
                waits_tch=None, post_tch=None, scratch_side="right",
                scratch=None):
        ln_state["n"] = ln_state.get("n", 0) + 1
        nm = str(ln_state["n"])
        psS1, psS2 = pstats if pstats is not None else (psC[:], psD[:])
        xbfs = scratch if scratch is not None else \
            ctx.alloc("ln_xbfs" + nm, [128, NB, 512], BF16, side=scratch_side)
        m_apply = 0
        marks = []
        for tch in range(4):
            t0 = 512 * tch
            wt = list(waits) + (list(waits_tch[tch]) if waits_tch else [])
            def _(s, tch=tch, t0=t0, wt=wt):
                for se, vv in wt:
                    s.wait_ge(ctx.s(se), vv)
                if ln_state["pslast"]:
                    s.wait_ge(ctx.s("V"), ln_state["pslast"])
                for kb in range(NB):
                    off = 0 if (shift and kb >= 4) else 1
                    src = src3[:, kb, off + t0: off + t0 + 512]
                    ins = s.activation(xbfs[:, kb, :], src, AF.Copy)
                ctx.inc(ins, "S")
            S_(_)
            mcast = ctx.cnt["S"]
            def _(t, tch=tch, wt=wt):
                t.wait_ge(ctx.s("S"), mcast)
                for se, vv in wt:
                    t.wait_ge(ctx.s(se), vv)
                if ln_state["pslast"]:
                    t.wait_ge(ctx.s("V"), ln_state["pslast"])
                for kb in range(NB):
                    MM(t, psS1, CI["ones128"][:], xbfs[:, kb, :],
                             start=(kb == 0), stop=(kb == NB - 1))
                ctx.inc(LASTMM[0], "T")
            T_(_)
            msum = ctx.cnt["T"]
            def _(s, tch=tch, t0=t0):
                s.wait_ge(ctx.s("T"), msum)
                for kb in range(NB):
                    off = 0 if (shift and kb >= 4) else 1
                    src = src3[:, kb, off + t0: off + t0 + 512]
                    ins = s.activation(xbfs[:, kb, :], src, AF.Square)
                ctx.inc(ins, "S")
            S_(_)
            msq = ctx.cnt["S"]
            def _(t, tch=tch):
                t.wait_ge(ctx.s("S"), msq)
                for kb in range(NB):
                    MM(t, psS2, CI["ones128"][:], xbfs[:, kb, :],
                             start=(kb == 0), stop=(kb == NB - 1))
                ctx.inc(LASTMM[0], "T")
            T_(_)
            mstat = ctx.cnt["T"]
            def _(v, tch=tch):
                v.wait_ge(ctx.s("T"), mstat)
                SYNC(v, "V", v.tensor_scalar(mu_b[:], psS1, 1.0 / D, None, ALU.mult))
                SYNC(v, "V", v.tensor_mul(mu2_b[:], mu_b[:], mu_b[:]))
                ins = v.scalar_tensor_tensor(var_b[:], psS2, 1.0 / D, mu2_b[:],
                                             ALU.mult, ALU.subtract)
                ctx.inc(ins, "V")
            V_(_)
            mvar = ctx.cnt["V"]
            def _(s, tch=tch):
                s.wait_ge(ctx.s("V"), mvar)
                s.activation(sd_b[:], var_b[:], AF.Ln, bias=1e-5)
                ins = s.activation(rb_b[:], sd_b[:], AF.Exp, scale=-0.5)
                ctx.inc(ins, "S")
            S_(_)
            msd = ctx.cnt["S"]
            def _(v, tch=tch, t0=t0):
                v.wait_ge(ctx.s("S"), msd)
                tb = [tsub, var_b]
                for kb2 in range(0, NB, 2):
                    for j in range(2):
                        kb = kb2 + j
                        off = 0 if (shift and kb >= 4) else 1
                        v.tensor_sub(tb[j][:], src3[:, kb, off + t0: off + t0 + 512],
                                     mu_b[:])
                    for j in range(2):
                        kb = kb2 + j
                        ins = v.tensor_mul(out3[:, kb, t0:t0 + 512], tb[j][:], rb_b[:])
                ctx.inc(ins, "V")
            V_(_)
            ln_state["pslast"] = ctx.cnt["V"]
            m_apply = ctx.cnt["V"]
            marks.append(m_apply)
            if post_tch is not None:
                post_tch(tch, m_apply)
        if scratch is None:
            ctx.free("ln_xbfs" + nm)
        return m_apply, marks

    m_ln1, ln1_marks = emit_ln(x_sb[:], xn[:],
                               waits=[("DS", m_xin), ("DG", m_loads)],
                               scratch=ln_scr)

    def _(sy):
        sy.wait_ge(ctx.s("V"), m_ln1)
        ins = sy.dma_start(out=x_spill, in_=x_sb[:, :, 1:T + 1])
        ctx.inc(ins, "DS", 16)
    Y_(_)
    m_xspill = ctx.cnt["DS"]
    ctx.free("x_sb")
    dbg("xn", [128, NB, T], BF16, xn[:], [("V", m_ln1)])
    if stop_after == "ln1":
        return finish_early()

    S_sb = ctx.alloc("S_sb", [HA, T + 1], F32, side="right")
    qdF = ctx.alloc("qdF", [HA, T], BF16, side="right")
    kdF = ctx.alloc("kdF", [HA, T], BF16, side="right")
    sp_sb = ctx.alloc("sp_sb", [HA, T], F32, side="right")
    m_sp = 0
    def _(v):
        v.wait_ge(ctx.s("DS"), m_xspill)
        ins = v.memset(S_sb[:, 0:1], 0.0)
        ctx.inc(ins, "V")
    V_(_)
    for tch in range(4):
        def _(t, tch=tch):
            t.wait_ge(ctx.s("V"), ln1_marks[tch])
            if tch == 0:
                t.wait_ge(ctx.s("DG"), m_loads)
            if m_sp:
                t.wait_ge(ctx.s("S"), m_sp)
            for kb in range(NB):
                MM(t, psE[0:HA, :], Wa[:, kb, :],
                         xn[:, kb, 512 * tch:512 * tch + 512],
                         start=(kb == 0), stop=(kb == NB - 1))
            ctx.inc(LASTMM[0], "T")
        T_(_)
        mla = ctx.cnt["T"]
        def _(s, tch=tch, mla=mla):
            s.wait_ge(ctx.s("T"), mla)
            s.wait_ge(ctx.s("DS"), m_xspill)
            SYNC(s, "S", s.activation(sd_b[0:HA, :], psE[0:HA, :], AF.Exp, bias=ba[:]))
            ins = s.activation(sp_sb[:, 512 * tch:512 * tch + 512], sd_b[0:HA, :],
                               AF.Ln, bias=1.0)
            ctx.inc(ins, "S")
        S_(_)
        m_sp = ctx.cnt["S"]

    qkE = ctx.alloc("qkE", [HA, T], F32, side="right")
    def _(v):
        v.wait_ge(ctx.s("V"), ctx.cnt["V"])
        v.wait_ge(ctx.s("S"), m_sp)
        ins = v.tensor_tensor_scan(S_sb[:, 1:T + 1], sp_sb[:], sp_sb[:], 0.0,
                                   ALU.add, ALU.bypass)
        mscan = ctx.inc(ins, "V")
        v.wait_ge(ctx.s("V"), mscan)
        for c in range(NC):
            ins = v.tensor_scalar(qkE[:, C * c:C * c + C],
                                  S_sb[:, 1 + C * c: 1 + C * c + C],
                                  S_sb[:, C * c:C * c + 1], None, ALU.subtract)
        ctx.inc(ins, "V")
    V_(_)
    m_qde = ctx.cnt["V"]
    def _(s):
        s.wait_ge(ctx.s("V"), m_qde)
        ins = s.activation(qdF[:], qkE[:], AF.Exp, scale=-1.0)
        ctx.inc(ins, "S")
    S_(_)
    m_qdf = ctx.cnt["S"]
    def _(v):
        v.wait_ge(ctx.s("S"), m_qdf)
        for c in range(NC):
            ins = v.tensor_scalar(qkE[:, C * c:C * c + C],
                                  S_sb[:, 1 + C * c: 1 + C * c + C],
                                  S_sb[:, C * (c + 1):C * (c + 1) + 1], None,
                                  ALU.subtract)
        ctx.inc(ins, "V")
    V_(_)
    m_kde = ctx.cnt["V"]
    def _(s):
        s.wait_ge(ctx.s("V"), m_kde)
        ins = s.activation(kdF[:], qkE[:], AF.Exp)
        ctx.inc(ins, "S")
    S_(_)
    m_kdf = ctx.cnt["S"]
    def _(sy):
        sy.wait_ge(ctx.s("V"), m_qde)
        ins = sy.dma_start(out=s_spill, in_=S_sb[:])
        ctx.inc(ins, "DS", 16)
    Y_(_)
    m_sdram = ctx.cnt["DS"]
    dbg("S", [HA, T + 1], F32, S_sb[:], [("V", m_qde)])

    ctx.free("qkE", "sp_sb")
    if stop_after == "a2":
        return finish_early()

    stok = ctx.alloc("stok", [128, HA], F32, side="right")
    SRb = ctx.alloc("SRb", [128, HA, C], F32, side="right")
    Dst = ctx.alloc("Dst", [128, HA, C], BF16, side="right")
    vcb = [ctx.alloc(f"vcb{i}", [128, D], BF16, side="right") for i in range(2)]
    kcb = [ctx.alloc(f"kcb{i}", [128, D], BF16, side="right") for i in range(2)]
    k2b = [ctx.alloc(f"k2b{i}", [128, D], BF16, side="right") for i in range(2)]
    qcb = [ctx.alloc(f"qcb{i}", [128, NB, C], BF16, side="right") for i in range(2)]
    q2b = [ctx.alloc(f"q2b{i}", [128, NB, C], BF16, side="right") for i in range(2)]
    ktb = [ctx.alloc(f"ktb{i}", [128, NB, C], BF16, side="right") for i in range(2)]
    a2b = ctx.alloc("a2b", [128, HA, C], BF16, side="right")
    MEV = ctx.alloc("MEV", [128, NB, DH], F32, side="right")
    MOD = ctx.alloc("MOD", [128, NB, DH], F32, side="right")
    MEVb = ctx.alloc("MEVb", [128, NB, DH], BF16, side="right")
    MODb = ctx.alloc("MODb", [128, NB, DH], BF16, side="right")
    gmb = ctx.alloc("gmb", [128, NB], F32, side="right")
    osq = ctx.alloc("osq", [128, D], BF16, side="right")
    if stop_after == "calloc":
        return finish_early()

    def _(v):
        v.memset(MEV[:], 0.0)
        ins = v.memset(MOD[:], 0.0)
        ctx.inc(ins, "V")
    V_(_)
    m_msinit = ctx.cnt["V"]
    def _(v):
        v.tensor_copy(MEVb[:], MEV[:])
        ins = v.tensor_copy(MODb[:], MOD[:])
        ctx.inc(ins, "V")
    V_(_)
    if stop_after == "cm":
        return finish_early()

    m_G = 0
    m_a2rdy = 0
    m_psB_free = 0
    m_psE_sfree = 0
    m_Mup = 0
    m_Mb = ctx.cnt["V"]
    m_ost = 0
    m_opred = 0
    m_v = 0

    for c in range(n_chunks):
        ib = c % 2
        cc = slice(C * c, C * c + C)

        def _(sy, c=c):
            sy.wait_ge(ctx.s("DS"), m_sdram)
            if c == 0:
                sy.wait_ge(ctx.s("V"), m_kde)
                sy.wait_ge(ctx.s("S"), m_kdf)
            if m_G:
                sy.wait_ge(ctx.s("V"), m_G)
                sy.wait_ge(ctx.s("S"), m_Dexp)
            src = s_spill[:, 1 + C * c: 1 + C * c + C].unsqueeze(0).partition_broadcast(128)
            ins = sy.dma_start(out=SRb[:], in_=src)
            ctx.inc(ins, "DS", 16)
        Y_(_)
        m_sr = ctx.cnt["DS"]
        if stop_after == "csr" and c == n_chunks - 1:
            return finish_early()

        def _(t, c=c, cc=cc):
            if c < 4:
                t.wait_ge(ctx.s("V"), ln1_marks[c // 4])
            if m_a2rdy:
                t.wait_ge(ctx.s("V"), m_a2rdy)
            for n in range(2):
                for kb in range(NB):
                    MM(t, psA[:, 512 * n:512 * n + 512], xn[:, kb, cc],
                             Wv[:, kb, 512 * n:512 * n + 512],
                             start=(kb == 0), stop=(kb == NB - 1))
            ctx.inc(LASTMM[0], "T")
            if stop_after == "cw1a":
                return
            if c == 0:
                t.wait_ge(ctx.s("S"), m_kdf)
            if m_psE_sfree:
                t.wait_ge(ctx.s("V"), m_psE_sfree)
            MM(t, psE[:, 0:HA], S_sb[:, 1 + C * c:1 + C * c + C],
                     CI["I128f"][0:HA, 0:HA], start=True, stop=True)
            ctx.inc(LASTMM[0], "T")
        T_(_)
        m_w1 = ctx.cnt["T"]
        if stop_after == "cw1a" and c == n_chunks - 1:
            return finish_early()

        def _(v, c=c):
            v.wait_ge(ctx.s("T"), m_w1)
            v.tensor_copy(vcb[ib][:], psA[:])
            ins = v.tensor_copy(stok[:], psE[:, 0:HA])
            ctx.inc(ins, "V")
        V_(_)
        m_v = ctx.cnt["V"]
        if stop_after == "cw1" and c == n_chunks - 1:
            return finish_early()

        def _(t, c=c, cc=cc):
            t.wait_ge(ctx.s("V"), m_v)
            for n in range(2):
                for kb in range(NB):
                    MM(t, psA[:, 512 * n:512 * n + 512], xn[:, kb, cc],
                             Wk[:, kb, 512 * n:512 * n + 512],
                             start=(kb == 0), stop=(kb == NB - 1))
            ctx.inc(LASTMM[0], "T")
            if m_ost:
                t.wait_ge(ctx.s("V"), m_opred)
            for n in range(2):
                MM(t, psB[:, 512 * n:512 * n + 512], kdF[:, cc],
                         CI["SEL16"][:, 512 * n:512 * n + 512], start=True, stop=True)
            ctx.inc(LASTMM[0], "T")
        T_(_)
        m_w2 = ctx.cnt["T"]

        def _(s, c=c):
            s.wait_ge(ctx.s("T"), m_w2)
            ins = s.activation(kcb[ib][:], psA[:], AF.Silu)
            ctx.inc(ins, "S")
        S_(_)
        m_ksilu = ctx.cnt["S"]

        def _(v, c=c):
            v.wait_ge(ctx.s("S"), m_ksilu)
            ins = v.tensor_mul(k2b[ib][:], kcb[ib][:], psB[:])
            ctx.inc(ins, "V")
        V_(_)
        m_k2 = ctx.cnt["V"]
        if stop_after == "cw2" and c == n_chunks - 1:
            return finish_early()

        def _(t, c=c, cc=cc):
            t.wait_ge(ctx.s("S"), m_ksilu)
            for i in range(NB):
                for kb in range(NB):
                    MM(t, psA[:, 128 * i:128 * i + 128],
                             Wq[:, kb, 128 * i:128 * i + 128], xn[:, kb, cc],
                             start=(kb == 0), stop=(kb == NB - 1))
            ctx.inc(LASTMM[0], "T")
            t.wait_ge(ctx.s("V"), m_k2)
            for pair in range(8):
                MM(t, psB[:, 128 * pair:128 * pair + 128],
                         CI["SELQ"][:, 128 * pair:128 * pair + 128],
                         qdF[:, cc], start=True, stop=True)
            ctx.inc(LASTMM[0], "T")
        T_(_)
        m_w3 = ctx.cnt["T"]

        def _(s, c=c):
            s.wait_ge(ctx.s("T"), m_w3)
            ins = s.activation(qcb[ib][:], psA[:], AF.Silu)
            ctx.inc(ins, "S")
        S_(_)
        m_qsilu = ctx.cnt["S"]

        def _(v, c=c):
            v.wait_ge(ctx.s("S"), m_qsilu)
            ins = v.tensor_mul(q2b[ib][:], qcb[ib][:], psB[:])
            ctx.inc(ins, "V")
        V_(_)
        m_q2 = ctx.cnt["V"]
        if stop_after == "cw3" and c == n_chunks - 1:
            return finish_early()

        def _(t, c=c):
            t.wait_ge(ctx.s("S"), m_qsilu)
            for pair in range(8):
                MM(t, psA[:, 128 * pair:128 * pair + 128],
                         kcb[ib][:, 128 * pair:128 * pair + 128], CI["I128b"][:],
                         start=True, stop=True)
            ctx.inc(LASTMM[0], "T")
        T_(_)
        m_w4 = ctx.cnt["T"]
        def _(v, c=c):
            v.wait_ge(ctx.s("T"), m_w4)
            ins = v.tensor_copy(ktb[ib][:], psA[:])
            ctx.inc(ins, "V")
        V_(_)
        m_kt = ctx.cnt["V"]
        if stop_after == "cw4" and c == n_chunks - 1:
            return finish_early()

        def _(v, c=c):
            v.wait_ge(ctx.s("DS"), m_sr)
            if m_a2rdy:
                v.wait_ge(ctx.s("S"), m_Dexp)
            stok_bc = stok[:].unsqueeze(2).broadcast_to([128, HA, C])
            ins = v.tensor_sub(SRb[:], stok_bc, SRb[:])
            mend = ctx.inc(ins, "V")
            v.wait_ge(ctx.s("V"), mend)
            ins = v.scalar_tensor_tensor(SRb[:], SRb[:], -1e30, SRb[:],
                                         ALU.mult, ALU.min)
            ctx.inc(ins, "V")
        V_(_)
        m_G = ctx.cnt["V"]
        if c == 0:
            dbg("G0", [128, HA, C], F32, SRb[:], [("V", m_G)])
            dbg("stok0", [128, HA], F32, stok[:], [("V", m_G)])

        def _(s, c=c):
            s.wait_ge(ctx.s("V"), m_G)
            if c == 0 and "E0" in dbg_names:
                s.wait_ge(ctx.s("DS"), ctx.cnt["DS"])
            if m_a2rdy:
                s.wait_ge(ctx.s("V"), m_a2rdy)
            ins = s.activation(Dst[:], SRb[:], AF.Exp)
            ctx.inc(ins, "S")
        S_(_)
        m_Dexp = ctx.cnt["S"]
        if stop_after == "cdecay" and c == n_chunks - 1:
            return finish_early()

        def _(t, c=c):
            t.wait_ge(ctx.s("V"), m_kt)
            t.wait_ge(ctx.s("V"), m_q2)
            npar = 1 if stop_after in ("cw5a", "cw5k") else 2
            if stop_after == "cw5m":
                npar = 2
            for par in range(npar):
                for h2 in range(8):
                    h = 2 * h2 + par
                    pair = h // 2
                    dst = psA if par == 0 else psB
                    if stop_after == "cw5k":
                        MM(t, dst[:, 128 * h2:128 * h2 + 128],
                                 ktb[ib][:, pair, :], qcb[ib][:, pair, :],
                                 start=True, stop=True)
                    else:
                        MM(t, dst[:, 128 * h2:128 * h2 + 128],
                                 ktb[ib][64 * par:64 * par + 64, pair, :],
                                 qcb[ib][64 * par:64 * par + 64, pair, :],
                                 start=True, stop=True)
                ctx.inc(LASTMM[0], "T")
            if stop_after in ("cw5a", "cw5k"):
                ctx.inc(LASTMM[0], "T")
        T_(_)
        m_A2m = ctx.cnt["T"]
        m_A1m = m_A2m - 1
        if stop_after in ("cw5a", "cw5k", "cw5m") and c == n_chunks - 1:
            return finish_early()

        def _(v, c=c):
            v.wait_ge(ctx.s("S"), m_Dexp)
            v.wait_ge(ctx.s("T"), m_A1m)
            v.tensor_mul(a2b[:, 0:8, :],
                         psA[:].rearrange("p (h q) -> p h q", h=8), Dst[:, 0:8, :])
            v.wait_ge(ctx.s("T"), m_A2m)
            ins = v.tensor_mul(a2b[:, 8:16, :],
                               psB[:].rearrange("p (h q) -> p h q", h=8),
                               Dst[:, 8:16, :])
            ctx.inc(ins, "V")
        V_(_)
        m_a2rdy = ctx.cnt["V"]
        if stop_after == "cw5" and c == n_chunks - 1:
            return finish_early()
        if c == 0:
            dbg("Dst0", [128, HA, C], BF16, Dst[:], [("S", m_Dexp)])
            dbg("a2b0", [128, HA, C], BF16, a2b[:], [("V", m_a2rdy)])
            dbg("q0", [128, NB, C], BF16, qcb[ib][:], [("S", m_qsilu)])
            dbg("kt0", [128, NB, C], BF16, ktb[ib][:], [("S", m_kt)])

        def _(t, c=c):
            t.wait_ge(ctx.s("V"), m_a2rdy)
            t.wait_ge(ctx.s("V"), m_Mb)
            skip_inter = bool(dbg_names) and "nointer0" in dbg_names and c == 0
            for h in range(HA):
                pair, par = h // 2, h % 2
                orgn = psB[64 * par:64 * par + 64, 128 * pair:128 * pair + 128]
                MM(t, orgn, vcb[ib][:, 64 * h:64 * h + 64],
                         a2b[:, 8 * (h % 2) + h // 2, :],
                         start=True, stop=skip_inter)
                if not skip_inter:
                    Mb_ = MEVb if par == 0 else MODb
                    MM(t, orgn, Mb_[:, pair, :], q2b[ib][:, pair, :],
                             start=False, stop=True)
            ctx.inc(LASTMM[0], "T")
            if m_Mup:
                t.wait_ge(ctx.s("V"), m_Mup)
            for h in range(HA):
                pair, par = h // 2, h % 2
                MM(t, psC[64 * par:64 * par + 64, 64 * pair:64 * pair + 64],
                         k2b[ib][:, 64 * h:64 * h + 64],
                         vcb[ib][:, 64 * h:64 * h + 64], start=True, stop=True)
            ctx.inc(LASTMM[0], "T")
            if m_Mup:
                t.wait_ge(ctx.s("V"), m_gmbev)
            for pair in range(8):
                MM(t, psF[:, pair:pair + 1],
                         CI["SELQ"][:, 128 * pair:128 * pair + 128],
                         qdF[:, C * c + C - 1:C * c + C], start=True, stop=True)
            ctx.inc(LASTMM[0], "T")
        T_(_)
        m_gam = ctx.cnt["T"]
        m_dM = m_gam - 1
        m_o = m_gam - 2

        def _(v, c=c):
            v.wait_ge(ctx.s("T"), m_gam)
            ins = v.tensor_copy(gmb[:], psF[:, 0:NB])
            ctx.inc(ins, "V")
        V_(_)
        m_gmbev = ctx.cnt["V"]
        if stop_after == "cw6" and c == n_chunks - 1:
            return finish_early()
        m_psE_sfree = m_v

        def _(v, c=c, cc=cc):
            v.wait_ge(ctx.s("T"), m_o)
            og3 = o_gla[:, :, cc]
            ins = v.tensor_copy(og3, psB[:].rearrange("p (h q) -> p h q", h=8))
            mend = ctx.inc(ins, "V")
            v.wait_ge(ctx.s("V"), mend)
            ins = v.tensor_mul(osq[:].rearrange("p (h q) -> p h q", h=8), og3, og3)
            mend = ctx.inc(ins, "V")
            v.wait_ge(ctx.s("V"), mend)
            v.reduce_sum(opart[:, :, c:c + 1],
                         psB[:].rearrange("p (h q) -> p h q", h=8), axis=AX.X)
            ins = v.reduce_sum(opart2[:, :, c:c + 1],
                               osq[:].rearrange("p (h q) -> p h q", h=8), axis=AX.X)
            ctx.inc(ins, "V")
        V_(_)
        m_ost = ctx.cnt["V"]
        m_opred = ctx.cnt["V"]

        def _(v, c=c):
            v.wait_ge(ctx.s("T"), m_dM)
            for h in range(HA):
                pair, par = h // 2, h % 2
                M_ = MEV if par == 0 else MOD
                rows = slice(64 * par, 64 * par + 64)
                ins = v.scalar_tensor_tensor(
                    M_[rows, pair, :], M_[rows, pair, :], gmb[rows, pair:pair + 1],
                    psC[rows, 64 * pair:64 * pair + 64], ALU.mult, ALU.add)
            ctx.inc(ins, "V")
        V_(_)
        m_Mup = ctx.cnt["V"]

        def _(v, c=c):
            v.tensor_copy(MEVb[:], MEV[:])
            ins = v.tensor_copy(MODb[:], MOD[:])
            ctx.inc(ins, "V")
        V_(_)
        m_Mb = ctx.cnt["V"]
        if stop_after == "cw7" and c == n_chunks - 1:
            return finish_early()

    m_chunkV = ctx.cnt["V"]
    m_chunkS = ctx.cnt["S"]
    m_chunkT = ctx.cnt["T"]
    if stop_after == "chunks":
        return finish_early()
    dbg("o_gla", [128, NB, T], BF16, o_gla[:], [("S", m_chunkS)])

    ostat = ctx.alloc("ostat", [128, NB], F32, side="right")
    ostat2 = ctx.alloc("ostat2", [128, NB], F32, side="right")
    ost_b = ctx.alloc("ost_b", [128, NB], BF16, side="right")
    ost2_b = ctx.alloc("ost2_b", [128, NB], BF16, side="right")

    def _(v):
        v.wait_ge(ctx.s("V"), m_chunkV)
        v.reduce_sum(ostat[:], opart[:], axis=AX.X)
        ins = v.reduce_sum(ostat2[:], opart2[:], axis=AX.X)
        ctx.inc(ins, "V")
    V_(_)
    m_op = ctx.cnt["V"]
    def _(s):
        s.wait_ge(ctx.s("V"), m_op)
        s.activation(ost_b[:], ostat[:], AF.Copy)
        ins = s.activation(ost2_b[:], ostat2[:], AF.Copy)
        ctx.inc(ins, "S")
    S_(_)
    m_opb = ctx.cnt["S"]
    def _(t):
        t.wait_ge(ctx.s("S"), m_opb)
        t.wait_ge(ctx.s("V"), m_gmbev)
        MM(t, psF[:, 0:NB], CI["EDIAG"][:], ost_b[:], start=True, stop=True)
        MM(t, psF[:, NB:2 * NB], CI["EDIAG"][:], ost2_b[:], start=True, stop=True)
        ctx.inc(LASTMM[0], "T")
    T_(_)
    m_gsum = ctx.cnt["T"]
    NEL = float(T * DH)
    def _(v):
        v.wait_ge(ctx.s("T"), m_gsum)
        SYNC(v, "V", v.tensor_scalar(ostat[:], psF[:, 0:NB], 1.0 / NEL, None, ALU.mult))
        SYNC(v, "V", v.tensor_mul(mu2_b[:, 0:NB], ostat[:], ostat[:]))
        ins = v.scalar_tensor_tensor(ostat2[:], psF[:, NB:2 * NB], 1.0 / NEL,
                                     mu2_b[:, 0:NB], ALU.mult, ALU.subtract)
        ctx.inc(ins, "V")
    V_(_)
    m_gvar = ctx.cnt["V"]
    def _(s):
        s.wait_ge(ctx.s("V"), m_gvar)
        s.activation(ostat2[:], ostat2[:], AF.Ln, bias=1e-5)
        ins = s.activation(mu2_b[:, 0:NB], ostat2[:], AF.Exp, scale=-0.5)
        ctx.inc(ins, "S")
    S_(_)
    m_gsd = ctx.cnt["S"]
    def _(v):
        v.wait_ge(ctx.s("S"), m_gsd)
        SYNC(v, "V", v.tensor_mul(gamhat[:], mu2_b[:, 0:NB], gnw[:]))
        SYNC(v, "V", v.tensor_mul(var_b[:, 0:NB], ostat[:], gamhat[:]))
        ins = v.tensor_sub(bethat[:], gnb[:], var_b[:, 0:NB])
        ctx.inc(ins, "V")
    V_(_)
    m_gn = ctx.cnt["V"]

    ctx.free("ost2_b", "ost_b", "ostat2", "ostat", "osq", "gmb", "MODb", "MEVb",
             "MOD", "MEV", "a2b", "ktb1", "ktb0", "q2b1", "q2b0", "qcb1", "qcb0",
             "k2b1", "k2b0", "kcb1", "kcb0", "vcb1", "vcb0", "Dst", "SRb", "stok",
             "kdF", "qdF", "S_sb")
    ctx.free("Wv", "opart2", "opart")
    if stop_after == "a4":
        return finish_early()

    Wg_sb = Wq
    Wo_sb = Wk
    def _(g):
        g.wait_ge(ctx.s("T"), m_chunkT)
        ctx.inc(g.dma_start(out=Wg_sb[:], in_=A_Wg), "DG", 16)
        ctx.inc(g.dma_start(out=Wo_sb[:], in_=A_Wo), "DG", 16)
    G_(_)
    m_wgo = ctx.cnt["DG"]

    x2_sb = ctx.alloc("x2_sb", [128, NB, T + 1], F32, side="right")
    def _(sy):
        sy.wait_ge(ctx.s("V"), m_chunkV)
        sy.wait_ge(ctx.s("S"), m_chunkS)
        ins = sy.dma_start(out=x2_sb[:, :, 1:T + 1], in_=x_spill)
        ctx.inc(ins, "DS", 16)
    Y_(_)
    m_xrel = ctx.cnt["DS"]
    def _(v):
        v.wait_ge(ctx.s("DS"), m_xrel)
        ins = v.memset(x2_sb[:, :, 0:1], 0.0)
        ctx.inc(ins, "V")
    V_(_)
    m_xrl2 = ctx.cnt["V"]

    gateb = ctx.alloc("gateb", [128, NB, 512], BF16, side="right")
    otld = ctx.alloc("otld", [128, NB, 512], BF16, side="right")
    m_gate_ev = 0
    m_otld = 0
    m_wo_half = [0, 0]
    m_xadd = 0
    for tch in range(4):
        t0 = 512 * tch
        for i2 in range(4):
            def _(t, tch=tch, t0=t0, i2=i2):
                if tch == 0 and i2 == 0:
                    t.wait_ge(ctx.s("DG"), m_wgo)
                    t.wait_ge(ctx.s("V"), m_gn)
                if m_gate_ev:
                    t.wait_ge(ctx.s("S"), m_gate_ev)
                if m_otld:
                    t.wait_ge(ctx.s("V"), m_otld)
                for j in range(2):
                    i = 2 * i2 + j
                    for kb in range(NB):
                        MM(t, psA[:, 512 * j:512 * j + 512],
                                 Wg_sb[:, kb, 128 * i:128 * i + 128],
                                 xn[:, kb, t0:t0 + 512],
                                 start=(kb == 0), stop=(kb == NB - 1))
                ctx.inc(LASTMM[0], "T")
            T_(_)
            m_gmm = ctx.cnt["T"]
            def _(s, i2=i2, m_gmm=m_gmm):
                s.wait_ge(ctx.s("T"), m_gmm)
                s.activation(gateb[:, 2 * i2, :], psA[:, 0:512], AF.Sigmoid)
                ins = s.activation(gateb[:, 2 * i2 + 1, :], psA[:, 512:1024],
                                   AF.Sigmoid)
                ctx.inc(ins, "S")
            S_(_)
            m_gate_ev = ctx.cnt["S"]

        def _(s, tch=tch, t0=t0):
            if tch == 0:
                s.wait_ge(ctx.s("V"), m_gn)
            if m_wo_half[1]:
                s.wait_ge(ctx.s("T"), m_wo_half[1])
            for pair in range(NB):
                ins = s.activation(otld[:, pair, :], o_gla[:, pair, t0:t0 + 512],
                                   AF.Identity, bias=bethat[:, pair:pair + 1],
                                   scale=gamhat[:, pair:pair + 1])
            ctx.inc(ins, "S")
        S_(_)
        m_oaff = ctx.cnt["S"]
        def _(v, tch=tch):
            v.wait_ge(ctx.s("S"), max(m_oaff, m_gate_ev))
            ins = v.tensor_mul(otld[:], otld[:], gateb[:])
            ctx.inc(ins, "V")
        V_(_)
        m_otld = ctx.cnt["V"]

        for dmb in range(NB):
            half = dmb % 2
            def _(t, dmb=dmb, half=half):
                t.wait_ge(ctx.s("V"), m_otld)
                if m_wo_half[half]:
                    t.wait_ge(ctx.s("V"), m_xadd_half[half])
                ps = psB[:, 512 * half:512 * half + 512]
                for kb in range(NB):
                    MM(t, ps, Wo_sb[:, kb, 128 * dmb:128 * dmb + 128],
                             otld[:, kb, :], start=(kb == 0), stop=(kb == NB - 1))
                ctx.inc(LASTMM[0], "T")
            T_(_)
            m_wo_half[half] = ctx.cnt["T"]
            def _(v, dmb=dmb, half=half, tch=tch, t0=t0, mwo=ctx.cnt["T"]):
                v.wait_ge(ctx.s("T"), mwo)
                ps = psB[:, 512 * half:512 * half + 512]
                if dmb == 0 and tch == 0:
                    v.wait_ge(ctx.s("V"), m_xrl2)
                ins = v.tensor_add(x2_sb[:, dmb, 1 + t0:1 + t0 + 512], ps,
                                   x2_sb[:, dmb, 1 + t0:1 + t0 + 512])
                ctx.inc(ins, "V")
            V_(_)
            if dmb == 0 and tch == 0:
                m_xadd_half = [0, 0]
                m_x2_tch = []
            m_xadd_half[half] = ctx.cnt["V"]
        m_x2_tch.append(ctx.cnt["V"])
    m_xnew = ctx.cnt["V"]

    ctx.free("otld", "gateb")
    ctx.free("o_gla", "Wk", "Wq", "ba", "Wa", "xn")
    dbg("xA", [128, NB, T], F32, x2_sb[:, :, 1:T + 1], [("V", m_xnew)])

    if stop_after == "a5":
        return finish_early()
    def _(sy):
        sy.wait_ge(ctx.s("V"), m_xnew)
        ins = sy.dma_start(out=x_spill, in_=x2_sb[:, :, 1:T + 1])
        ctx.inc(ins, "DS", 16)
    Y_(_)
    m_xnspill = ctx.cnt["DS"]

    xi_sb = ctx.alloc("xi_sb", [128, NB, T + 1], F32)
    def _(v):
        ins = v.memset(xi_sb[:, :, 0:1], 0.0)
        ctx.inc(ins, "V")
    V_(_)
    m_ln2, ln2_marks = emit_ln(x2_sb[:], xi_sb[:, :, 1:T + 1], waits=[],
                               waits_tch=[[("V", m_x2_tch[t])] for t in range(4)],
                               scratch=ln_scr)
    dbg("xi0", [128, NB, T], F32, xi_sb[:, :, 1:T + 1], [("V", m_ln2)])
    ctx.free("x2_sb")

    CIB = {}
    for k in cstB:
        v = cstB[k]
        CIB[k] = ctx.alloc("k_" + k, v.shape, F32 if v.dtype == np.float32 else BF16)

    def _(g):
        g.wait_ge(ctx.s("V"), m_ln2)
        for k in CIB:
            ctx.inc(g.dma_start(out=CIB[k][:], in_=CONST[k]), "DG", 16)
    G_(_)
    m_cib = ctx.cnt["DG"]

    def barrier():
        j = {k: ctx.cnt[k] for k in ("T", "V", "S", "DS", "DG", "G")}
        for emit, eng in ((T_, "T"), (V_, "V"), (S_, "S"), (Y_, "DS"), (G_, "DG")):
            def _(e, j=j):
                for k2, v2 in j.items():
                    if v2:
                        e.wait_ge(ctx.s(k2), v2)
                e.nop()
            emit(_)
        return j

    m_xi = m_ln2
    barrier()
    ctx.free("psF", "psE", "psD", "psC")
    sc2 = ctx.alloc("sc2", [128, 1024], F32, psum=True)
    sc3 = ctx.alloc("sc3", [128, 1024], F32, psum=True)
    SCB = [psA, psB, sc2, sc3]
    for l in range(2):
        barrier()
        xbfa = ctx.alloc(f"xbfa{l}", [128, NB, T + 1], BF16, side="right")
        wqkv_sb = ctx.alloc(f"wqkv{l}", [128, NB, 1536], BF16, side="right")
        wo_sb2 = ctx.alloc(f"wob{l}", [128, 4, D], BF16, side="right")
        qs_sb = ctx.alloc(f"qs{l}", [128, DH], BF16, side="right")
        ks_sb = ctx.alloc(f"ks{l}", [128, DH], BF16, side="right")
        v_sb = ln_scr
        ktr = ctx.alloc(f"ktr{l}", [65, HB, 1024], BF16, side="right")
        qtr = ctx.alloc(f"qtr{l}", [65, HB, 512], BF16, side="right")
        oT = ctx.alloc(f"oT{l}", [128, 4, 512], BF16, side="right")
        qraw = ctx.alloc(f"qraw{l}", [128, 512], BF16, side="right")
        kraw = ctx.alloc(f"kraw{l}", [128, 512], BF16, side="right")
        sqb = ctx.alloc(f"sqb{l}", [128, 1024], BF16, side="right")
        rnx = ctx.alloc(f"rnx{l}", [128, 1024], BF16, side="right")
        qp = ctx.alloc(f"qp{l}", [128, 512], BF16, side="right")
        kp = ctx.alloc(f"kp{l}", [128, 512], BF16, side="right")
        ss8 = ctx.alloc(f"ss8{l}", [128, 16], F32, side="right")
        sd8 = ctx.alloc(f"sd8{l}", [128, 16], F32, side="right")
        rn8 = ctx.alloc(f"rn8{l}", [128, 16], F32, side="right")
        nmall = ctx.alloc(f"nmall{l}", [128, HB, 4], F32, side="right")
        ssall = ctx.alloc(f"ssall{l}", [128, HB, 4], F32, side="right")
        lnsb = ctx.alloc(f"lnsb{l}", [128, HB, 4], F32, side="right")
        zf = ctx.alloc(f"zf{l}", [128, 32], F32, side="right")
        ztb = ctx.alloc(f"ztb{l}", [32, 128], BF16, side="right")
        ptil = ctx.alloc(f"ptil{l}", [128, 4, 640], BF16, side="right")

        def _(g, l=l):
            ctx.inc(g.dma_start(out=wqkv_sb[:], in_=B_wqkv[l]), "DG", 16)
            ctx.inc(g.dma_start(out=wo_sb2[:], in_=B_wo[l]), "DG", 16)
            ctx.inc(g.dma_start(out=qs_sb[:], in_=B_qs[l]), "DG", 16)
            ctx.inc(g.dma_start(out=ks_sb[:], in_=B_ks[l]), "DG", 16)
        G_(_)
        m_wl = ctx.cnt["DG"]

        def _(v, l=l):
            v.wait_ge(ctx.s("V"), m_xi)
            v.memset(xbfa[:, :, 0:1], 0.0)
            ins = v.memset(ktr[64:65, :, :], 1.0)
            ctx.inc(ins, "V")
        V_(_)
        m_xbset = ctx.cnt["V"]
        def _(s, l=l):
            s.wait_ge(ctx.s("V"), m_xbset)
            for tch in range(4):
                ins = s.activation(xbfa[:, :, 1 + 512 * tch:1 + 512 * tch + 512],
                                   xi_sb[:, :, 1 + 512 * tch:1 + 512 * tch + 512],
                                   AF.Copy)
            ctx.inc(ins, "S")
        S_(_)
        m_xbf = ctx.cnt["S"]

        m_oT_done = 0
        m_resid = 0
        m_xbfw = m_xbf
        m_tr_ev = 0
        m_qkev = 0
        m_tf = 0
        m_tp = 0
        m_p1sfree = [0, 0, 0, 0]
        m_p2sfree = [0, 0, 0, 0]
        m_ptilfree = [0, 0, 0, 0]
        m_oslotfree = [0] * 8
        m_resid_half = [0, 0]
        for w in range(NW):
            m_tf_l = {}
            for ci in range(5):
                if ci < 4:
                    c = 4 * w + ci
                    def _(t, l=l, w=w, c=c, ci=ci, mqk=m_qkev):
                        if c == 0:
                            t.wait_ge(ctx.s("DG"), m_wl)
                            t.wait_ge(ctx.s("S"), m_xbf)
                        if w > 0 and ci == 0:
                            t.wait_ge(ctx.s("S"), m_xbfw)
                        if mqk:
                            t.wait_ge(ctx.s("S"), mqk)
                        for n in range(3):
                            ps = psA[:, 0:512] if n == 0 else (
                                psA[:, 512:1024] if n == 1 else psB[:, 0:512])
                            for kb in range(NB):
                                off = 0 if kb >= 4 else 1
                                MM(t, ps,
                                   xbfa[:, kb, off + 128 * c: off + 128 * c + 128],
                                   wqkv_sb[:, kb, 512 * n:512 * n + 512],
                                   start=(kb == 0), stop=(kb == NB - 1))
                            ctx.inc(LASTMM[0], "T")
                    T_(_)
                    m_qkv = ctx.cnt["T"]
                if ci >= 1:
                    cm = ci - 1
                    def _(t, cm=cm, mtf=m_tf_l[ci - 1], mtrev=m_tr_ev):
                        t.wait_ge(ctx.s("V"), mtf)
                        if mtrev:
                            t.wait_ge(ctx.s("V"), mtrev)
                        for h in range(HB):
                            MM(t, sc2[0:64, 128 * h:128 * h + 128],
                               qp[:, 64 * h:64 * h + 64], CI["I128b"][:],
                               start=True, stop=True)
                        ctx.inc(LASTMM[0], "T")
                        for h in range(HB):
                            MM(t, sc3[0:64, 128 * h:128 * h + 128],
                               kp[:, 64 * h:64 * h + 64], CI["I128b"][:],
                               start=True, stop=True)
                        ctx.inc(LASTMM[0], "T")
                    T_(_)
                    m_tp = ctx.cnt["T"]
                    def _(v, cm=cm, w=w, m_tp=m_tp):
                        v.wait_ge(ctx.s("T"), m_tp)
                        v.tensor_copy(qtr[0:64, :, 128 * cm:128 * cm + 128],
                                      sc2[0:64, :].rearrange("p (h q) -> p h q", h=HB))
                        ins = v.tensor_copy(
                            ktr[0:64, :, 512 * (w % 2) + 128 * cm:
                                512 * (w % 2) + 128 * cm + 128],
                            sc3[0:64, :].rearrange("p (h q) -> p h q", h=HB))
                        ctx.inc(ins, "V")
                    V_(_)
                    m_tr_ev = ctx.cnt["V"]
                if ci < 4:
                    def _(s, c=c, m_qkv=m_qkv, mtf=m_tf):
                        s.wait_ge(ctx.s("T"), m_qkv)
                        if mtf:
                            s.wait_ge(ctx.s("V"), mtf)
                        s.activation(qraw[:], psA[:, 0:512], AF.Copy)
                        s.activation(kraw[:], psA[:, 512:1024], AF.Copy)
                        s.activation(v_sb[:, c % 8, :], psB[:, 0:512], AF.Copy)
                        s.activation(sqb[:, 0:512], psA[:, 0:512], AF.Square)
                        ins = s.activation(sqb[:, 512:1024], psA[:, 512:1024],
                                           AF.Square)
                        ctx.inc(ins, "S")
                    S_(_)
                    m_qkev = ctx.cnt["S"]
                    def _(v, m_qkev=m_qkev):
                        v.wait_ge(ctx.s("S"), m_qkev)
                        v.reduce_sum(ss8[:, 0:8].unsqueeze(2),
                                     sqb[:, 0:512].rearrange("p (h e) -> p h e", h=HB),
                                     axis=AX.X)
                        ins = v.reduce_sum(ss8[:, 8:16].unsqueeze(2),
                                     sqb[:, 512:1024].rearrange("p (h e) -> p h e", h=HB),
                                     axis=AX.X)
                        ctx.inc(ins, "V")
                    V_(_)
                    m_ss = ctx.cnt["V"]
                    def _(s, m_ss=m_ss):
                        s.wait_ge(ctx.s("V"), m_ss)
                        s.activation(sd8[:], ss8[:], AF.Ln, bias=1e-24)
                        ins = s.activation(rn8[:], sd8[:], AF.Exp, scale=-0.5)
                        ctx.inc(ins, "S")
                    S_(_)
                    m_rn = ctx.cnt["S"]
                    def _(v, c=c, m_rn=m_rn, mtp=m_tp):
                        v.wait_ge(ctx.s("S"), m_rn)
                        if mtp:
                            v.wait_ge(ctx.s("T"), mtp)
                        for which in range(2):
                            raw = qraw if which == 0 else kraw
                            out_p = qp if which == 0 else kp
                            scl = qs_sb if which == 0 else ks_sb
                            CS = CIB["CSQ8"] if which == 0 else CIB["CSK1"]
                            SN = CIB["SNQ8"] if which == 0 else CIB["SNK1"]
                            off = 512 * which
                            rnxs = rnx[:, off:off + 512]
                            sqbs = sqb[:, off:off + 512]
                            rn_ap = rn8[:, 8 * which:8 * which + 8].unsqueeze(2) \
                                .broadcast_to([128, HB, DH])
                            qs_ap = scl[:].unsqueeze(1).broadcast_to([128, HB, DH])
                            v.tensor_mul(
                                rnxs.rearrange("p (h e) -> p h e", h=HB), rn_ap, qs_ap)
                            v.tensor_mul(sqbs, raw[:], rnxs)
                            qn3 = sqbs.rearrange("p (h e) -> p h e", h=HB)
                            op3 = out_p[:].rearrange("p (h e) -> p h e", h=HB)
                            t23 = rnxs.rearrange("p (h e) -> p h e", h=HB)
                            cs_ap = CS[:, c, :].unsqueeze(1).broadcast_to([128, HB, DH])
                            sn0 = SN[:, c, 0:32].unsqueeze(1).broadcast_to([128, HB, 32])
                            sn1 = SN[:, c, 32:64].unsqueeze(1).broadcast_to([128, HB, 32])
                            v.tensor_mul(op3, qn3, cs_ap)
                            v.tensor_mul(t23[:, :, 0:32], qn3[:, :, 32:64], sn0)
                            v.tensor_mul(t23[:, :, 32:64], qn3[:, :, 0:32], sn1)
                            ins = v.tensor_add(out_p[:], out_p[:], rnxs)
                            ctx.inc(ins, "V")
                    V_(_)
                    m_tf = ctx.cnt["V"]
                    m_tf_l[ci] = m_tf
                    if l == 0 and c == 0:
                        dbg("qp00", [128, 512], BF16, qp[:], [("V", m_tf)])
                        dbg("kp00", [128, 512], BF16, kp[:], [("V", m_tf)])

            if l == 0 and w == 0:
                dbg("v00", [128, 4, 512], BF16, v_sb[:, 0:4, :], [("V", m_tr_ev)])
                dbg("qtr00", [64, HB, 512], BF16, qtr[0:64], [("V", m_tr_ev)])
                dbg("ktr00", [64, HB, 512], BF16, ktr[0:64, :, 0:512], [("V", m_tr_ev)])
            tiles = [(h, ci) for h in range(HB) for ci in range(4)]
            m_p1v = 0
            for ti in range(0, 32, 2):
                for k2 in range(2):
                    h, ci = tiles[ti + k2]
                    b = (ti + k2) % 4
                    psX = SCB[b]
                    c = 4 * w + ci
                    jmin = max(0, 4 - c)
                    def _(t, h=h, c=c, ci=ci, jmin=jmin, psX=psX,
                          mtr=m_tr_ev, msf=m_p1sfree[b]):
                        t.wait_ge(ctx.s("V"), mtr)
                        if msf:
                            t.wait_ge(ctx.s("S"), msf)
                        for j in range(jmin, 5):
                            kt = c - 4 + j
                            slot = (kt // 4) % 2
                            MM(t, psX[:, 128 * j:128 * j + 128],
                               qtr[0:64, h, 128 * ci:128 * ci + 128],
                               ktr[0:64, h, 512 * slot + 128 * (kt % 4):
                                   512 * slot + 128 * (kt % 4) + 128],
                               start=True, stop=True)
                        ctx.inc(LASTMM[0], "T")
                    T_(_)
                    m_s1 = ctx.cnt["T"]
                    def _(v, h=h, ci=ci, jmin=jmin, psX=psX, m_s1=m_s1):
                        v.wait_ge(ctx.s("T"), m_s1)
                        if jmin == 0:
                            mview = psX[:, 0:640].rearrange(
                                "p (a b) -> p a b", b=128)[:, 0:5:4, :]
                            ins = v.tensor_add(mview, mview, CIB["P1MASK"][:])
                        else:
                            ins = v.tensor_add(psX[:, 512:640], psX[:, 512:640],
                                               CIB["P1MASK"][:, 1, :])
                        mend = ctx.inc(ins, "V")
                        v.wait_ge(ctx.s("V"), mend)
                        ins = v.reduce_max(nmall[:, h, ci:ci + 1],
                                           psX[:, 128 * jmin:640], axis=AX.X,
                                           negate=True)
                        ctx.inc(ins, "V")
                    V_(_)
                    m_p1v = ctx.cnt["V"]
                def _(s, ti=ti, w=w, m_p1v=m_p1v):
                    s.wait_ge(ctx.s("V"), m_p1v)
                    for k2 in range(2):
                        h, ci = tiles[ti + k2]
                        psX = SCB[(ti + k2) % 4]
                        jmin = max(0, 4 - (4 * w + ci))
                        ins = s.activation(ptil[:, 0, 128 * jmin:640],
                                           psX[:, 128 * jmin:640], AF.Exp,
                                           bias=nmall[:, h, ci:ci + 1],
                                           accum_out=ssall[:, h, ci:ci + 1])
                    ctx.inc(ins, "S")
                S_(_)
                m_p1sfree[ti % 4] = m_p1sfree[(ti + 1) % 4] = ctx.cnt["S"]
            m_p1 = ctx.cnt["S"]

            if l == 0 and w == 0:
                dbg("nm00", [128, HB, 4], F32, nmall[:], [("S", m_p1)])
                dbg("ss00", [128, HB, 4], F32, ssall[:], [("S", m_p1)])
            def _(s, m_p1=m_p1):
                s.wait_ge(ctx.s("S"), m_p1)
                ins = s.activation(lnsb[:], ssall[:], AF.Ln)
                ctx.inc(ins, "S")
            S_(_)
            m_lns = ctx.cnt["S"]
            def _(v, m_lns=m_lns):
                v.wait_ge(ctx.s("S"), m_lns)
                ins = v.tensor_sub(zf[:].rearrange("p (h q) -> p h q", h=HB),
                                   nmall[:], lnsb[:])
                ctx.inc(ins, "V")
            V_(_)
            m_zf = ctx.cnt["V"]
            def _(t, m_zf=m_zf):
                t.wait_ge(ctx.s("V"), m_zf)
                MM(t, psB[0:32, 0:128], zf[:], CI["I128f"][:], start=True, stop=True)
                ctx.inc(LASTMM[0], "T")
            T_(_)
            m_zt = ctx.cnt["T"]
            def _(s, m_zt=m_zt):
                s.wait_ge(ctx.s("T"), m_zt)
                ins = s.activation(ztb[:], psB[0:32, 0:128], AF.Copy)
                ctx.inc(ins, "S")
            S_(_)
            m_ztb = ctx.cnt["S"]
            def _(sy, m_ztb=m_ztb):
                sy.wait_ge(ctx.s("S"), m_ztb)
                ins = sy.dma_start(out=qtr[64:65, :, :], in_=ztb[:])
                ctx.inc(ins, "DS", 16)
            Y_(_)
            m_zdma = ctx.cnt["DS"]

            m_p2v = 0
            pend = None
            m_tailfree = [0, 0, 0, 0]
            for ti in range(0, 32, 2):
                for k2 in range(2):
                    h, ci = tiles[ti + k2]
                    b = (ti + k2) % 4
                    psX = SCB[b]
                    c = 4 * w + ci
                    jmin = max(0, 4 - c)
                    opend = pend if k2 == 1 else None
                    def _(t, h=h, c=c, ci=ci, jmin=jmin, psX=psX, w=w,
                          m_zdma=m_zdma, msf=m_p2sfree[b], opend=opend,
                          mtl=m_tailfree[b],
                          omev=(None if opend is None else
                                (m_oslotfree[opend[0] % 8],
                                 m_oslotfree[(opend[0] + 1) % 8]))):
                        t.wait_ge(ctx.s("DS"), m_zdma)
                        if msf:
                            t.wait_ge(ctx.s("S"), msf)
                        if mtl:
                            t.wait_ge(ctx.s("V"), mtl)
                        for j in range(jmin, 5):
                            kt = c - 4 + j
                            slot = (kt // 4) % 2
                            MM(t, psX[:, 128 * j:128 * j + 128],
                               ktr[0:65, h, 512 * slot + 128 * (kt % 4):
                                   512 * slot + 128 * (kt % 4) + 128],
                               qtr[0:65, h, 128 * ci:128 * ci + 128],
                               start=True, stop=True)
                        ctx.inc(LASTMM[0], "T")
                        if opend is not None:
                            oti, omg = opend
                            t.wait_ge(ctx.s("S"), omg)
                            for j2 in range(2):
                                k = oti + j2
                                hh, cci = tiles[k]
                                bb = k % 4
                                sl = k % 8
                                oten = SCB[sl % 4]
                                ocol = 640 + 128 * (sl // 4)
                                if omev[j2]:
                                    t.wait_ge(ctx.s("V"), omev[j2])
                                par = hh % 2
                                orgn = oten[64 * par:64 * par + 64,
                                            ocol:ocol + 128]
                                cc2 = 4 * w + cci
                                jmn = max(0, 4 - cc2)
                                for j in range(jmn, 5):
                                    kt = cc2 - 4 + j
                                    MM(t, orgn,
                                       v_sb[:, kt % 8, 64 * hh:64 * hh + 64],
                                       ptil[:, bb, 128 * j:128 * j + 128],
                                       start=(j == jmn), stop=(j == 4))
                                ctx.inc(LASTMM[0], "T")
                    T_(_)
                    m_s2_all = ctx.cnt["T"]
                    m_s2 = m_s2_all - 2 if opend is not None else m_s2_all
                    if opend is not None:
                        m_ptilfree[opend[0] % 4] = m_s2_all - 1
                        m_ptilfree[(opend[0] + 1) % 4] = m_s2_all
                        pend_evac = (opend[0], m_s2_all)
                    def _(v, jmin=jmin, psX=psX, m_s2=m_s2):
                        v.wait_ge(ctx.s("T"), m_s2)
                        if jmin == 0:
                            mview = psX[:, 0:640].rearrange(
                                "p (a b) -> p a b", b=128)[:, 0:5:4, :]
                            ins = v.tensor_add(mview, mview, CIB["P2ADD"][:])
                        else:
                            ins = v.tensor_add(psX[:, 512:640], psX[:, 512:640],
                                               CIB["P2ADD"][:, 1, :])
                        ctx.inc(ins, "V")
                    V_(_)
                    m_p2v = ctx.cnt["V"]
                def _(s, ti=ti, w=w, m_p2v=m_p2v,
                      mpt=(m_ptilfree[ti % 4], m_ptilfree[(ti + 1) % 4])):
                    s.wait_ge(ctx.s("V"), m_p2v)
                    for k2 in range(2):
                        if mpt[k2]:
                            s.wait_ge(ctx.s("T"), mpt[k2])
                        h, ci = tiles[ti + k2]
                        b = (ti + k2) % 4
                        jmin = max(0, 4 - (4 * w + ci))
                        ins = s.activation(ptil[:, b, 128 * jmin:640],
                                           SCB[b][:, 128 * jmin:640], AF.Exp)
                    ctx.inc(ins, "S")
                S_(_)
                msec = ctx.cnt["S"]
                m_p2sfree[ti % 4] = m_p2sfree[(ti + 1) % 4] = msec
                mgmask = msec
                if pend is not None:
                    def _(v, oti=pend_evac[0], mT2=pend_evac[1], w=w):
                        v.wait_ge(ctx.s("T"), mT2)
                        for j2 in range(2):
                            k = oti + j2
                            hh, cci = tiles[k]
                            sl = k % 8
                            oten = SCB[sl % 4]
                            ocol = 640 + 128 * (sl // 4)
                            par = hh % 2
                            p2i = hh // 2
                            ins = v.tensor_copy(
                                oT[64 * par:64 * par + 64, p2i,
                                   128 * cci:128 * cci + 128],
                                oten[64 * par:64 * par + 64, ocol:ocol + 128])
                        ctx.inc(ins, "V")
                    V_(_)
                    m_oslotfree[pend_evac[0] % 8] = \
                        m_oslotfree[(pend_evac[0] + 1) % 8] = ctx.cnt["V"]
                    m_tailfree[pend_evac[0] % 4] = \
                        m_tailfree[(pend_evac[0] + 1) % 4] = ctx.cnt["V"]

                def emit_oacc(oti, omg, w=w):
                    def _(t, ti=oti, w=w, mg=omg,
                          mev=(m_oslotfree[oti % 8], m_oslotfree[(oti + 1) % 8])):
                        t.wait_ge(ctx.s("S"), mg)
                        for k2 in range(2):
                            k = ti + k2
                            h, ci = tiles[k]
                            b = k % 4
                            sl = k % 8
                            oten = SCB[sl % 4]
                            ocol = 640 + 128 * (sl // 4)
                            if mev[k2]:
                                t.wait_ge(ctx.s("V"), mev[k2])
                            par = h % 2
                            orgn = oten[64 * par:64 * par + 64, ocol:ocol + 128]
                            c = 4 * w + ci
                            jmin = max(0, 4 - c)
                            for j in range(jmin, 5):
                                kt = c - 4 + j
                                MM(t, orgn, v_sb[:, kt % 8, 64 * h:64 * h + 64],
                                   ptil[:, b, 128 * j:128 * j + 128],
                                   start=(j == jmin), stop=(j == 4))
                            ctx.inc(LASTMM[0], "T")
                    T_(_)
                    mT2 = ctx.cnt["T"]
                    m_ptilfree[oti % 4] = mT2 - 1
                    m_ptilfree[(oti + 1) % 4] = mT2
                    def _(v, ti=oti, mT2=mT2):
                        v.wait_ge(ctx.s("T"), mT2)
                        for k2 in range(2):
                            k = ti + k2
                            h, ci = tiles[k]
                            sl = k % 8
                            oten = SCB[sl % 4]
                            ocol = 640 + 128 * (sl // 4)
                            par = h % 2
                            p2i = h // 2
                            ins = v.tensor_copy(
                                oT[64 * par:64 * par + 64, p2i,
                                   128 * ci:128 * ci + 128],
                                oten[64 * par:64 * par + 64, ocol:ocol + 128])
                        ctx.inc(ins, "V")
                    V_(_)
                    m_oslotfree[oti % 8] = m_oslotfree[(oti + 1) % 8] = ctx.cnt["V"]

                pend = (ti, mgmask)
            emit_oacc(*pend)
            pend = None
            m_oev = ctx.cnt["V"]

            if l == 0 and w == 0:
                dbg("oT00", [128, 4, 512], BF16, oT[:], [("V", m_oev)])
            m_wo = 0
            for dmb in range(NB):
                half = dmb % 2
                def _(t, dmb=dmb, half=half, m_oev=m_oev, w=w,
                      mrh=m_resid_half[half]):
                    t.wait_ge(ctx.s("V"), m_oev)
                    if mrh:
                        t.wait_ge(ctx.s("V"), mrh)
                    ps = psA[:, 512 * half:512 * half + 512]
                    for kb in range(4):
                        MM(t, ps, wo_sb2[:, kb, 128 * dmb:128 * dmb + 128],
                           oT[:, kb, :], start=(kb == 0), stop=(kb == 3))
                    ctx.inc(LASTMM[0], "T")
                T_(_)
                m_wo = ctx.cnt["T"]
                def _(v, dmb=dmb, half=half, w=w, m_wo=m_wo):
                    v.wait_ge(ctx.s("T"), m_wo)
                    ps = psA[:, 512 * half:512 * half + 512]
                    dst = xi_sb[:, dmb, 1 + 512 * w:1 + 512 * w + 512]
                    ins = v.tensor_add(dst, ps, dst)
                    ctx.inc(ins, "V")
                V_(_)
                m_resid_half[half] = ctx.cnt["V"]
            m_resid = ctx.cnt["V"]
            def _(s, w=w, m_resid=m_resid):
                s.wait_ge(ctx.s("V"), m_resid)
                ins = s.activation(xbfa[:, :, 1 + 512 * w:1 + 512 * w + 512],
                                   xi_sb[:, :, 1 + 512 * w:1 + 512 * w + 512],
                                   AF.Copy)
                ctx.inc(ins, "S")
            S_(_)
            m_xbfw = ctx.cnt["S"]

        m_attn_done_V = ctx.cnt["V"]
        m_attn_done_S = ctx.cnt["S"]
        dbg(f"xiA{l}", [128, NB, T], F32, xi_sb[:, :, 1:T + 1], [("V", m_attn_done_V)])

        barrier()
        ctx.free(f"ptil{l}", f"ztb{l}", f"zf{l}", f"lnsb{l}",
                 f"ssall{l}", f"nmall{l}", f"rn8{l}", f"sd8{l}", f"ss8{l}",
                 f"kp{l}", f"qp{l}", f"rnx{l}", f"sqb{l}", f"kraw{l}", f"qraw{l}",
                 f"oT{l}", f"qtr{l}", f"ktr{l}", f"ks{l}", f"qs{l}",
                 f"wob{l}", f"wqkv{l}", f"xbfa{l}")

        h2 = ctx.alloc(f"h2_{l}", [128, NB, T], BF16, side="right")
        lngc = ctx.alloc(f"lng{l}", [128, NB], F32, side="right")
        lnbc = ctx.alloc(f"lnb{l}", [128, NB], F32, side="right")
        b1ac = ctx.alloc(f"b1a{l}", [128, 22], F32, side="right")
        b1gc = ctx.alloc(f"b1g{l}", [128, 22], F32, side="right")
        b2c = ctx.alloc(f"b2{l}", [128, NB], F32, side="right")
        w1q = ctx.alloc(f"w1q{l}", [128, NB, 1536], BF16, side="right")
        w2q = ctx.alloc(f"w2q{l}", [128, 6, D], BF16, side="right")
        glu = ctx.alloc(f"glu{l}", [128, 6, T], BF16, side="right")
        glscr = ctx.alloc(f"glscr{l}", [128, 1024], BF16, side="right")

        def _(g, l=l):
            ctx.inc(g.dma_start(out=lngc[:], in_=B_lng[l]), "DG", 16)
            ctx.inc(g.dma_start(out=lnbc[:], in_=B_lnb[l]), "DG", 16)
            ctx.inc(g.dma_start(out=b1ac[:], in_=B_b1a[l]), "DG", 16)
            ctx.inc(g.dma_start(out=b1gc[:], in_=B_b1g[l]), "DG", 16)
            ctx.inc(g.dma_start(out=b2c[:], in_=B_b2[l]), "DG", 16)
        G_(_)
        m_fl = ctx.cnt["DG"]

        m_h2a_t = [0, 0, 0, 0]
        def _affine(tch, mk):
            def _(v, tch=tch):
                if tch == 0:
                    v.wait_ge(ctx.s("DG"), m_fl)
                for kb in range(NB):
                    ins = v.tensor_scalar(h2[:, kb, 512 * tch:512 * tch + 512],
                                          h2[:, kb, 512 * tch:512 * tch + 512],
                                          lngc[:, kb:kb + 1], lnbc[:, kb:kb + 1],
                                          ALU.mult, ALU.add)
                ctx.inc(ins, "V")
            V_(_)
            m_h2a_t[tch] = ctx.cnt["V"]
        m_h2, _lnm = emit_ln(xi_sb[:], h2[:], waits=[("V", m_attn_done_V)],
                             shift=True, pstats=(sc2[:, 0:512], sc3[:, 0:512]),
                             post_tch=_affine)
        m_h2a = ctx.cnt["V"]
        dbg(f"h2_{l}", [128, NB, T], BF16, h2[:], [("V", m_h2a)])

        QBLKS = [list(range(6 * qd, min(6 * qd + 6, 22))) for qd in range(4)]
        m_w2add = 0
        m_gluev = 0
        m_w1read = {}
        m_w2mm_last = 0
        m_glu_ev = [0, 0]
        m_gelu_ev = [0, 0]
        m_w2slot = [0, 0, 0, 0]
        PSB2 = [psA, psB]
        for qd in range(4):
            blks = QBLKS[qd]
            nb_q = len(blks)
            m_w1blk = {}
            for ji, gb in enumerate(blks):
                def _(g, qd=qd, ji=ji, gb=gb, l=l,
                      mrd=m_w1read.get(ji, 0)):
                    if mrd:
                        g.wait_ge(ctx.s("T"), mrd)
                    a0 = 128 * gb
                    ctx.inc(g.dma_start(out=w1q[:, :, 128 * ji:128 * ji + 128],
                                        in_=B_w1[l][:, :, a0:a0 + 128]), "DG", 16)
                    ctx.inc(g.dma_start(
                        out=w1q[:, :, 768 + 128 * ji:768 + 128 * ji + 128],
                        in_=B_w1[l][:, :, 2816 + a0:2816 + a0 + 128]), "DG", 16)
                G_(_)
                m_w1blk[ji] = ctx.cnt["DG"]
            def _(g, blks=blks, nb_q=nb_q, l=l, mw=m_w2mm_last):
                if mw:
                    g.wait_ge(ctx.s("T"), mw)
                ctx.inc(g.dma_start(out=w2q[:, 0:nb_q, :],
                                    in_=B_w2[l][:, blks[0]:blks[0] + nb_q, :]),
                        "DG", 16)
            G_(_)
            m_w2dma = ctx.cnt["DG"]

            it = 0
            for ji in range(nb_q):
                gb = blks[ji]
                for tch in range(4):
                    buf = PSB2[it % 2]
                    def _(t, ji=ji, tch=tch, qd=qd, it=it, buf=buf,
                          mblk=m_w1blk[ji],
                          mge=m_glu_ev[it % 2], mse=m_gelu_ev[it % 2]):
                        t.wait_ge(ctx.s("DG"), mblk)
                        if qd == 0:
                            t.wait_ge(ctx.s("V"), m_h2a_t[tch])
                        if it < 2 and m_w2add:
                            t.wait_ge(ctx.s("V"), m_w2add)
                        if mge:
                            t.wait_ge(ctx.s("V"), mge)
                            t.wait_ge(ctx.s("S"), mse)
                        for kb in range(NB):
                            MM(t, buf[:, 0:512], w1q[:, kb, 128 * ji:128 * ji + 128],
                               h2[:, kb, 512 * tch:512 * tch + 512],
                               start=(kb == 0), stop=(kb == NB - 1))
                        ctx.inc(LASTMM[0], "T")
                        for kb in range(NB):
                            MM(t, buf[:, 512:1024],
                               w1q[:, kb, 768 + 128 * ji:768 + 128 * ji + 128],
                               h2[:, kb, 512 * tch:512 * tch + 512],
                               start=(kb == 0), stop=(kb == NB - 1))
                        ctx.inc(LASTMM[0], "T")
                    T_(_)
                    m_ag = ctx.cnt["T"]
                    if tch == 3:
                        m_w1read[ji] = m_ag
                    def _(s, gb=gb, buf=buf, it=it, m_ag=m_ag):
                        s.wait_ge(ctx.s("T"), m_ag)
                        ins = s.activation(
                            glscr[:, 512 * (it % 2):512 * (it % 2) + 512],
                            buf[:, 512:1024], AF.Gelu, bias=b1gc[:, gb:gb + 1])
                        ctx.inc(ins, "S")
                    S_(_)
                    m_gelu_ev[it % 2] = ctx.cnt["S"]
                    def _(v, ji=ji, tch=tch, gb=gb, buf=buf, it=it,
                          mg=ctx.cnt["S"]):
                        v.wait_ge(ctx.s("S"), mg)
                        ins = v.scalar_tensor_tensor(
                            glu[:, ji, 512 * tch:512 * tch + 512], buf[:, 0:512],
                            b1ac[:, gb:gb + 1],
                            glscr[:, 512 * (it % 2):512 * (it % 2) + 512],
                            ALU.add, ALU.mult)
                        ctx.inc(ins, "V")
                    V_(_)
                    m_glu_ev[it % 2] = ctx.cnt["V"]
                    it += 1
            m_gluev = ctx.cnt["V"]

            SLOT = [(psA, 0), (psA, 1), (psB, 0), (psB, 1)]
            it2 = 0
            for tch in range(4):
                for dmb in range(NB):
                    ps, half = SLOT[it2 % 4]
                    def _(t, tch=tch, dmb=dmb, ps=ps, half=half, it2=it2,
                          nb_q=nb_q, mglu=m_gluev, mw2=m_w2dma,
                          msl=m_w2slot[it2 % 4]):
                        t.wait_ge(ctx.s("V"), mglu)
                        t.wait_ge(ctx.s("DG"), mw2)
                        if msl:
                            t.wait_ge(ctx.s("V"), msl)
                        pss = ps[:, 512 * half:512 * half + 512]
                        for jji in range(nb_q):
                            MM(t, pss, w2q[:, jji, 128 * dmb:128 * dmb + 128],
                               glu[:, jji, 512 * tch:512 * tch + 512],
                               start=(jji == 0), stop=(jji == nb_q - 1))
                        ctx.inc(LASTMM[0], "T")
                    T_(_)
                    m_w2m = ctx.cnt["T"]
                    def _(v, tch=tch, dmb=dmb, ps=ps, half=half, qd=qd,
                          m_w2m=m_w2m):
                        v.wait_ge(ctx.s("T"), m_w2m)
                        pss = ps[:, 512 * half:512 * half + 512]
                        dst = xi_sb[:, dmb, 1 + 512 * tch:1 + 512 * tch + 512]
                        if qd == 0:
                            ins = v.scalar_tensor_tensor(dst, pss, b2c[:, dmb:dmb + 1],
                                                         dst, ALU.add, ALU.add)
                        else:
                            ins = v.tensor_add(dst, pss, dst)
                        ctx.inc(ins, "V")
                    V_(_)
                    m_w2slot[it2 % 4] = ctx.cnt["V"]
                    it2 += 1
            m_w2mm_last = ctx.cnt["T"]
            m_w2add = ctx.cnt["V"]

        m_xi = ctx.cnt["V"]
        dbg(f"xiF{l}", [128, NB, T], F32, xi_sb[:, :, 1:T + 1], [("V", m_xi)])
        barrier()
        ctx.free(f"glscr{l}", f"glu{l}", f"w2q{l}", f"w1q{l}", f"b2{l}",
                 f"b1g{l}", f"b1a{l}", f"lnb{l}", f"lng{l}", f"h2_{l}")

    m_inner = m_xi

    x3_sb = ctx.alloc("x3_sb", [128, NB, T], F32, side="right")
    yst = ctx.alloc("yst", [128, NB, 512], F32, side="right")
    def _(sy):
        sy.wait_ge(ctx.s("DS"), m_xnspill)
        sy.wait_ge(ctx.s("V"), ctx.cnt["V"])
        sy.wait_ge(ctx.s("S"), ctx.cnt["S"])
        sy.wait_ge(ctx.s("T"), ctx.cnt["T"])
        ins = sy.dma_start(out=x3_sb[:], in_=x_spill)
        ctx.inc(ins, "DS", 16)
    Y_(_)
    m_x3 = ctx.cnt["DS"]
    m_ydma = 0
    for tch in range(4):
        t0 = 512 * tch
        def _(v, tch=tch, t0=t0):
            v.wait_ge(ctx.s("V"), m_inner)
            v.wait_ge(ctx.s("DS"), m_x3)
            if m_ydma:
                v.wait_ge(ctx.s("DS"), m_ydma)
            ins = v.tensor_add(yst[:], x3_sb[:, :, t0:t0 + 512],
                               xi_sb[:, :, 1 + t0:1 + t0 + 512])
            ctx.inc(ins, "V")
        V_(_)
        m_y = ctx.cnt["V"]
        def _(sy, tch=tch, t0=t0, m_y=m_y):
            sy.wait_ge(ctx.s("V"), m_y)
            ins = sy.dma_start(out=yT[:, :, t0:t0 + 512], in_=yst[:])
            ctx.inc(ins, "DS", 16)
        Y_(_)
        m_ydma = ctx.cnt["DS"]

    def _(sy):
        sy.wait_ge(ctx.s("DS"), ctx.cnt["DS"])
        sy.wait_ge(ctx.s("DG"), ctx.cnt["DG"])
        if ctx.cnt["G"]:
            sy.wait_ge(ctx.s("G"), ctx.cnt["G"])
        sy.nop()
    Y_(_)

    ctx._open.pop("block").__exit__(None, None, None)
    for k in reversed(list(ctx._open)):
        ctx._open.pop(k).__exit__(None, None, None)
    return nc, dbg_out


def make_in_maps(inputs):
    f = np.float32
    cstA = host_constants()
    cstB = host_constants_b()
    x = np.asarray(inputs["L0_attn_x"], dtype=f)
    base = {}
    base["A_Wq"] = np.ascontiguousarray(np.asarray(inputs["L0_attn_Wq"], f).reshape(NB, 128, D).transpose(1, 0, 2))
    base["A_Wk"] = np.ascontiguousarray(np.asarray(inputs["L0_attn_Wk"], f).reshape(NB, 128, D).transpose(1, 0, 2))
    base["A_Wv"] = np.ascontiguousarray(np.asarray(inputs["L0_attn_Wv"], f).reshape(NB, 128, D).transpose(1, 0, 2))
    base["A_Wg"] = np.ascontiguousarray(np.asarray(inputs["L0_attn_Wg"], f).reshape(NB, 128, D).transpose(1, 0, 2))
    HSLOT = [2 * (s % 8) + (s // 8) for s in range(16)]
    wa = np.asarray(inputs["L0_attn_Wa"], f)[:, HSLOT]
    base["A_Wa"] = np.ascontiguousarray(wa.reshape(NB, 128, HA).transpose(1, 0, 2))
    base["A_ba"] = np.ascontiguousarray(np.asarray(inputs["L0_attn_ba"], f)[HSLOT].reshape(HA, 1))
    base["A_Wo"] = np.ascontiguousarray(np.asarray(inputs["L0_attn_Wo"], f).reshape(NB, 128, D).transpose(1, 0, 2))
    base["A_gnw"] = np.ascontiguousarray(np.asarray(inputs["L0_attn_gn_w"], f).reshape(NB, 128).T)
    base["A_gnb"] = np.ascontiguousarray(np.asarray(inputs["L0_attn_gn_b"], f).reshape(NB, 128).T)
    wqkv = np.asarray(inputs["L0_ffn_wqkv"], f)
    qs = np.asarray(inputs["L0_ffn_q_scale"], f)
    ks = np.asarray(inputs["L0_ffn_k_scale"], f)
    wo = np.asarray(inputs["L0_ffn_wo"], f)
    lng = np.asarray(inputs["L0_ffn_ln_g"], f)
    lnb = np.asarray(inputs["L0_ffn_ln_b"], f)
    w1 = np.asarray(inputs["L0_ffn_w1"], f)
    b1 = np.asarray(inputs["L0_ffn_b1"], f)
    w2 = np.asarray(inputs["L0_ffn_w2"], f)
    b2 = np.asarray(inputs["L0_ffn_b2"], f)
    for l in range(2):
        base[f"B_wqkv{l}"] = np.ascontiguousarray(wqkv[l].reshape(NB, 128, 1536).transpose(1, 0, 2))
        base[f"B_qs{l}"] = np.ascontiguousarray(np.broadcast_to(qs[l], (128, DH)))
        base[f"B_ks{l}"] = np.ascontiguousarray(np.broadcast_to(ks[l], (128, DH)))
        base[f"B_wo{l}"] = np.ascontiguousarray(wo[l].reshape(4, 128, D).transpose(1, 0, 2))
        base[f"B_lng{l}"] = np.ascontiguousarray(lng[l].reshape(NB, 128).T)
        base[f"B_lnb{l}"] = np.ascontiguousarray(lnb[l].reshape(NB, 128).T)
        w1p = np.zeros((D, 2 * HIDP), f)
        w1p[:, 0:HID] = w1[l][:, 0:HID]
        w1p[:, HIDP:HIDP + HID] = w1[l][:, HID:2 * HID]
        base[f"B_w1{l}"] = np.ascontiguousarray(w1p.reshape(NB, 128, 2 * HIDP).transpose(1, 0, 2))
        b1a = np.zeros((HIDP,), f); b1a[0:HID] = b1[l][0:HID]
        b1g = np.zeros((HIDP,), f); b1g[0:HID] = b1[l][HID:2 * HID]
        base[f"B_b1a{l}"] = np.ascontiguousarray(b1a.reshape(22, 128).T)
        base[f"B_b1g{l}"] = np.ascontiguousarray(b1g.reshape(22, 128).T)
        w2p = np.zeros((HIDP, D), f)
        w2p[0:HID] = w2[l]
        base[f"B_w2{l}"] = np.ascontiguousarray(w2p.reshape(22, 128, D).transpose(1, 0, 2))
        base[f"B_b2{l}"] = np.ascontiguousarray(b2[l].reshape(NB, 128).T)
    for k, v in {**cstA, **cstB}.items():
        base["c_" + k] = v
    maps = []
    for b in range(8):
        m = dict(base)
        m["xT"] = np.ascontiguousarray(x[b].T.reshape(NB, 128, T).transpose(1, 0, 2))
        maps.append(m)
    return maps


def kernel(**inputs):
    nc, _ = build_program()
    in_maps = make_in_maps(inputs)
    res = run_bass_kernel_spmd(nc, in_maps, core_ids=list(range(8)))
    out = np.stack([r["yT"].transpose(1, 0, 2).reshape(D, T).T for r in res.results], axis=0)
    return np.ascontiguousarray(out.astype(np.float32))
